# version 09
# baseline (speedup 1.0000x reference)
import numpy as np
import ml_dtypes
from contextlib import ExitStack

import concourse.bass as bass
import concourse.mybir as mybir
from concourse.bass_utils import run_bass_kernel_spmd

F32 = mybir.dt.float32
BF16 = mybir.dt.bfloat16
AF = mybir.ActivationFunctionType
ALU = mybir.AluOpType
AX = mybir.AxisListType

BF16NP = ml_dtypes.bfloat16


ENGINES = ("tensor", "vector", "scalar", "gpsimd", "sync")


class Buf:

    __slots__ = ("name", "t", "prog")

    def __init__(self, prog, name, handle):
        self.prog = prog
        self.name = name
        self.t = handle

    def __getitem__(self, idx):
        return self.t[idx]

    def r(self, region):
        return (self, region)


class SubBuf:

    __slots__ = ("buf", "col0", "ncol", "dtype")

    def __init__(self, buf, col0, ncol, dtype=None):
        self.buf = buf
        self.col0 = col0
        self.ncol = ncol
        self.dtype = dtype

    def r(self, tag=None):
        return self

    def __getitem__(self, idx):
        rows, cols = idx
        start = cols.start or 0
        stop = cols.stop if cols.stop is not None else self.ncol
        if self.dtype is None:
            return self.buf.t[rows, self.col0 + start:self.col0 + stop]
        if self.dtype == F32:
            ap = self.buf.t[rows, self.col0 + 2 * start:self.col0 + 2 * stop]
        else:
            ap = self.buf.t[rows, self.col0 + start // 2:self.col0 + stop // 2]
        return ap.bitcast(self.dtype)


class Prog:
    def __init__(self, nc: bass.Bass):
        self.nc = nc
        self.es = ExitStack()
        self.ops = {e: [] for e in ENGINES}
        self.sem_count = {}
        self.vclock = {e: {} for e in ENGINES}
        self.sems = {}
        self.res = {}
        self._sb_bytes = 0
        self._psum = None
        self._started = False
        self._noinc_pending = {e: False for e in ENGINES}

    def begin(self):
        for e in ENGINES:
            self.sems[e] = self.es.enter_context(self.nc.semaphore("sem_" + e))
            self.sem_count[e] = 0
        self._psum = self.es.enter_context(self.nc.psum_tensor("psum_f32", [128, 3072], F32))
        self.psum = Buf(self, "psum_f32", self._psum)
        self._psumb = self.es.enter_context(self.nc.psum_tensor("psum_bf16", [128, 2048], BF16))
        self.psumb = Buf(self, "psum_bf16", self._psumb)
        self._started = True

    def sbuf(self, name, shape, dtype=F32):
        tgt = self._scopes[-1] if getattr(self, "_scopes", None) else self.es
        h = tgt.enter_context(self.nc.sbuf_tensor(name, list(shape), dtype))
        return Buf(self, name, h)

    def push_scope(self):
        if not hasattr(self, "_scopes"):
            self._scopes = []
        self._scopes.append(ExitStack())

    def pop_scope(self):
        self._scopes.pop().close()

    def bank(self, i, rows=128, cols=512):
        assert 0 <= i < 6
        return self._psum[0:rows, 512 * i: 512 * i + cols]

    def bankb(self, i, rows=128, cols=128, off=0):
        assert 0 <= i < 2
        return self._psumb[0:rows, 1024 * i + off: 1024 * i + off + cols]

    def _key(self, ref):
        if isinstance(ref, Buf):
            return (ref.name, None)
        if isinstance(ref, SubBuf):
            return (ref.buf.name, ("sub", ref.col0))
        buf, region = ref
        return (buf.name, region)

    def _conflicting_keys(self, key):
        name, region = key
        out = []
        for k in self.res:
            if k[0] != name:
                continue
            if region is None or k[1] is None or k[1] == region:
                out.append(k)
        return out

    def _need(self, needs, tok):
        if tok is None:
            return
        sem, val = tok
        if needs.get(sem, 0) < val:
            needs[sem] = val

    def op(self, eng, fn, reads=(), writes=(), dma=False, slot=None, inc=True,
           fence=()):
        assert self._started
        reads = list(reads) + list(fence)
        writes = list(writes) + list(fence)
        needs = {}
        rkeys = [self._key(r) for r in reads]
        wkeys = [self._key(w) for w in writes]
        for key in rkeys:
            for k in self._conflicting_keys(key):
                self._need(needs, self.res[k]["w"])
        for key in wkeys:
            for k in self._conflicting_keys(key):
                st = self.res[k]
                self._need(needs, st["w"])
                for sem, val in st["r"].items():
                    self._need(needs, (sem, val))
        vc = self.vclock[eng]
        waits = []
        for sem, val in needs.items():
            if sem.startswith("d:"):
                val = self.sem_count[sem]
            if vc.get(sem, 0) < val:
                waits.append((sem, val))
                vc[sem] = val
        if dma:
            assert slot is not None, "dma ops need a slot (one per physical buffer)"
            q = "d:" + slot
            if q not in self.sems:
                self.sems[q] = self.es.enter_context(self.nc.semaphore("sem_" + q.replace(":", "_")))
                self.sem_count[q] = 0
            self.sem_count[q] += 16
            tok = (q, self.sem_count[q])
            incspec = (q, 16)
        else:
            self.sem_count[eng] += 1
            tok = (eng, self.sem_count[eng])
            incspec = (eng, 1)
        for key in rkeys:
            st = self.res.setdefault(key, {"w": None, "r": {}})
            sem, val = tok
            if st["r"].get(sem, 0) < val:
                st["r"][sem] = val
        for key in wkeys:
            st = self.res.setdefault(key, {"w": None, "r": {}})
            st["w"] = tok
            st["r"] = {}
        self.ops[eng].append((waits, fn, incspec))
        return tok

    def wait_manual(self, eng, tok):
        sem, val = tok
        vc = self.vclock[eng]
        if vc.get(sem, 0) >= val:
            return
        vc[sem] = val
        self.ops[eng].append(([(sem, val)], None, None))

    def barrier(self, engines=ENGINES, dma=True):
        toks = []
        for e in engines:
            toks.append((e, self.sem_count[e]))
        if dma:
            for q in self.sems:
                if q.startswith("d:"):
                    toks.append((q, self.sem_count[q]))
        for e in engines:
            vc = self.vclock[e]
            waits = []
            for sem, val in toks:
                if val > 0 and vc.get(sem, 0) < val:
                    waits.append((sem, val))
                    vc[sem] = val
            if waits:
                self.ops[e].append((waits, None, None))
        if set(engines) == set(ENGINES):
            self.res.clear()

    def fresh(self, buf):
        for k in list(self.res):
            if k[0] == buf.name:
                del self.res[k]

    def drain(self):
        vc = self.vclock["sync"]
        waits = []
        for q, cnt in self.sem_count.items():
            if q.startswith("d:") and cnt > 0 and vc.get(q, 0) < cnt:
                waits.append((q, cnt))
                vc[q] = cnt
        if waits:
            self.ops["sync"].append((waits, None, None))

    def finish(self):
        self.drain()
        nc = self.nc
        block = self.es.enter_context(nc.Block())
        streams = self.ops
        sems = self.sems

        def make_body(eng_name):
            def body(e):
                for waits, fn, incspec in streams[eng_name]:
                    for semname, val in waits:
                        e.wait_ge(sems[semname], val)
                    if fn is not None:
                        inst = fn(e)
                        if incspec is not None:
                            assert inst is not None, f"op on {eng_name} returned no instruction"
                            inst.then_inc(sems[incspec[0]], incspec[1])
            return body

        block.tensor(make_body("tensor"))
        block.vector(make_body("vector"))
        block.scalar(make_body("scalar"))
        block.gpsimd(make_body("gpsimd"))
        block.sync(make_body("sync"))

    def close(self):
        self.es.close()


_DBGON = False
_DBGCNT = [0]
SCRATCH = 8192
ROWSLABS = 4


def dbg_dump(P, tag, buf, cols, dt=None, rows=128):
    if not _DBGON:
        return
    if dt is None:
        dt = mybir.dt.bfloat16
    nc = P.nc
    _DBGCNT[0] += 1
    name = f"dbg_{_DBGCNT[0]:02d}_{tag}"
    P.barrier()
    d = nc.dram_tensor(name, [rows, cols], dt, kind="ExternalOutput").ap()
    nslab = (cols + 2047) // 2048
    for s in range(nslab):
        c0 = s * 2048
        c1 = min(cols, c0 + 2048)
        P.op("sync", lambda e, c0=c0, c1=c1: e.dma_start(
            out=d[0:rows, c0:c1], in_=buf[0:rows, c0:c1]),
             reads=[buf], dma=True, slot="dbgd")
    P.barrier()



PACK_ITEMS = ([("a0_wq_d", 8192), ("a0_wk_d", 8192), ("a0_wv_d", 8192),
               ("moe_gw_d", 256)]
              + [(f"moe_w1_{e}", 16384) for e in range(32)]
              + [(f"moe_w2_{e}", 8192) for e in range(32)]
              + [("ab_ip_d", 24576), ("ab_op_d", 8192)]
              + [(f"ab_wl_{h}", 1536) for h in range(4)]
              + [(f"ab_wln_{h}", 512) for h in range(4)]
              + [("a1_qw_d", 8192), ("a1_kw_d", 8192), ("a1_vw_d", 8192),
                 ("a1_ow_d", 8192)]
              + [("f1_w1_d", 32768), ("f1_w2_d", 32768), ("f1_h1_d", 131072),
                 ("f1_h2_d", 131072), ("f1_w3_d", 32768)]
              + [("b1_qw_d", 8192), ("b1_kw_d", 8192)])
PACK_OFF = {}
_o = 0
for _n, _c in PACK_ITEMS:
    PACK_OFF[_n] = _o
    _o += _c
PACK_TOT = _o


WC_SPLIT = PACK_OFF["moe_w2_0"]


class WCache:

    __slots__ = ("apA", "apB")

    def __init__(self, apA, apB):
        self.apA = apA
        self.apB = apB

    def win(self, off, a, b):
        if off >= WC_SPLIT:
            return self.apB[0:128, off - WC_SPLIT + a: off - WC_SPLIT + b]
        assert off + b <= WC_SPLIT
        return self.apA[0:128, off + a: off + b]


class WView:

    __slots__ = ("wc", "off")

    def __init__(self, wc, name):
        self.wc = wc
        self.off = PACK_OFF[name]

    def __getitem__(self, idx):
        rows, cols = idx
        return self.wc.win(self.off, cols.start or 0, cols.stop)


def build_wpack(w):
    parts = []
    for name, ncols in PACK_ITEMS:
        sub, key = name.split("_", 1)
        arr = np.asarray(w[sub].pop(key), np.float32)
        assert arr.shape == (128, ncols), (name, arr.shape)
        parts.append(arr)
    return np.ascontiguousarray(np.concatenate(parts, axis=1))


def emit_prepass(P, C, wsrc, wc):
    P.push_scope()
    CH = 8192
    stgf = P.sbuf("stgf", [128, 2 * CH], F32)
    stgb = P.sbuf("stgb", [128, 2 * CH], BF16)
    ci = 0
    for base, end, ap in ((0, WC_SPLIT, wc.apA), (WC_SPLIT, PACK_TOT, wc.apB)):
        for c0 in range(base, end, CH):
            c1 = min(end, c0 + CH)
            ww = c1 - c0
            par = ci % 2
            ci += 1
            fsl = stgf[0:128, par * CH: par * CH + ww]
            bsl = stgb[0:128, par * CH: par * CH + ww]
            P.op("sync" if par == 0 else "scalar",
                 lambda e, fsl=fsl, c0=c0, c1=c1:
                 e.dma_start(out=fsl, in_=wsrc[0:128, c0:c1]),
                 writes=[stgf.r(par)], dma=True, slot="wpA" if par == 0 else "wpB")
            P.op("vector" if par == 0 else "gpsimd",
                 lambda e, fsl=fsl, bsl=bsl: e.tensor_copy(bsl, fsl),
                 reads=[stgf.r(par)], writes=[stgb.r(par)])
            P.op("scalar" if par == 0 else "sync",
                 lambda e, bsl=bsl, ap=ap, a0=c0 - base, a1=c1 - base:
                 e.dma_start(out=ap[0:128, a0:a1], in_=bsl),
                 reads=[stgb.r(par)], dma=True, slot="wsA" if par == 0 else "wsB")
    P.barrier()
    P.pop_scope()


def clear_scratch(P, bufs):
    engs = ("vector", "gpsimd")
    for i, b in enumerate(bufs):
        P.op(engs[i % 2], lambda e, b=b: e.memset(b.t[:, :], 0.0), writes=[b])


D = 1024
NC = 8
SB = 512
EPS = 1e-5
TH = 1024.0
TH2EPS = TH * TH * EPS
TH2EPS2 = TH * TH * EPS * EPS


def ceil_div(a, b):
    return (a + b - 1) // b


class Ctx:

    def __init__(self, P: Prog, rowslabs=6):
        self.P = P
        nc = P.nc
        self.d_consts = {}
        for name, shape, dt in [
            ("c_id_bf", [128, 128], BF16),
            ("c_id_f32", [128, 128], F32),
            ("c_ones_col_bf", [128, 1], BF16),
            ("c_ones_row_f32", [1, 128], F32),
            ("c_ones_row_bf", [1, 128], BF16),
            ("c_ones_col_f32", [128, 1], F32),
            ("c1024_row_f32", [1, 128], F32),
            ("c_ones_sb_f32", [1, SB], F32),
            ("c_bias_tbl", [128, 8], F32),
        ]:
            self.d_consts[name] = nc.dram_tensor(name, shape, dt, kind="ExternalInput").ap()
        self.id_bf = P.sbuf("id_bf", [128, 128], BF16)
        self.id_f32 = P.sbuf("id_f32", [128, 128], F32)
        self.ones_col_bf = P.sbuf("ones_col_bf", [128, 1], BF16)
        self.ones_row_f32 = P.sbuf("ones_row_f32", [1, 128], F32)
        self.ones_row_bf = P.sbuf("ones_row_bf", [1, 128], BF16)
        self.ones_col_f32 = P.sbuf("ones_col_f32", [128, 1], F32)
        self.c1024_row = P.sbuf("c1024_row", [1, 128], F32)
        self.cbias = P.sbuf("cbias", [128, 8], F32)
        self.row2 = P.sbuf("row2", [2, SB], F32)
        self.rowscr = P.sbuf("rowscr", [1, rowslabs * SB], F32)
        self.nscr = P.sbuf("nscr", [128, 2 * SB], BF16)
        self.f32scr = P.sbuf("f32scr", [128, 2 * SB], F32)
        for sb_buf, dname in [(self.id_bf, "c_id_bf"), (self.id_f32, "c_id_f32"),
                              (self.ones_col_bf, "c_ones_col_bf"),
                              (self.ones_row_f32, "c_ones_row_f32"),
                              (self.ones_row_bf, "c_ones_row_bf"),
                              (self.ones_col_f32, "c_ones_col_f32"),
                              (self.c1024_row, "c1024_row_f32"),
                              (self.cbias, "c_bias_tbl")]:
            dap = self.d_consts[dname]
            P.op("sync", lambda e, o=sb_buf, i=dap: e.dma_start(out=o[:, :], in_=i),
                 writes=[sb_buf], dma=True, slot="const")
        P.op("vector", lambda e: e.memset(self.row2[:, :], 1.0), writes=[self.row2])

    def cb(self, k, p=1):
        return self.cbias[0:p, k:k + 1]

    @staticmethod
    def host_consts():
        return {
            "c_id_bf": np.eye(128, dtype=np.float32).astype(BF16NP),
            "c_id_f32": np.eye(128, dtype=np.float32),
            "c_ones_col_bf": np.ones((128, 1), np.float32).astype(BF16NP),
            "c_ones_row_f32": np.ones((1, 128), np.float32),
            "c_ones_row_bf": np.ones((1, 128), np.float32).astype(BF16NP),
            "c_ones_col_f32": np.ones((128, 1), np.float32),
            "c1024_row_f32": np.full((1, 128), 1024.0, np.float32),
            "c_ones_sb_f32": np.ones((1, SB), np.float32),
            "c_bias_tbl": np.tile(np.array([[TH2EPS, TH2EPS2, 1.0, 0.0,
                                             0.0, 0.0, 0.0, 0.0]], np.float32), (128, 1)),
        }


def emit_norm(C: Ctx, x, xn, T, mode, banks=(0, 1, 2, 3), gl=None, gb2=None,
              u=None, ubanks=(4, 5)):
    P = C.P
    bSS, bS, bA, bB = banks
    nslab = ceil_div(T, SB)
    need_r2 = mode in ("lnln", "lnrms")
    for j in range(nslab):
        t0 = j * SB
        tt = min(SB, T - t0)
        rS = P.bank(bS, 1, tt)
        rSS = P.bank(bSS, 1, tt)
        for c in range(NC):
            xs = x[0:128, c * T + t0: c * T + t0 + tt]
            xb = xn[0:128, c * T + t0: c * T + t0 + tt]
            sq = C.nscr[0:128, (c % 2) * SB:(c % 2) * SB + tt]
            P.op("scalar", lambda e, o=xb, i=xs: e.copy(o, i),
                 reads=[x.r((c, j))], writes=[xn.r((c, j))])
            P.op("vector", lambda e, o=sq, i=xs: e.tensor_mul(o, i, i),
                 reads=[x.r((c, j))], writes=[C.nscr.r(c % 2)])

            def statmm(e, c=c, xb=xb, sq=sq, rS=rS, rSS=rSS):
                e.matmul(rS, C.ones_col_bf[:, :], xb, start=(c == 0), stop=(c == NC - 1))
                return e.matmul(rSS, C.ones_col_bf[:, :], sq, start=(c == 0), stop=(c == NC - 1))
            P.op("tensor", statmm,
                 reads=[xn.r((c, j)), C.nscr.r(c % 2), C.ones_col_bf],
                 writes=[(P.psum, bS), (P.psum, bSS)])
        r_row = C.rowscr[0:1, 0:tt]
        t_row = C.rowscr[0:1, SB:SB + tt]
        r0_row = C.rowscr[0:1, 2 * SB:2 * SB + tt]
        s2_row = C.rowscr[0:1, 3 * SB:3 * SB + tt]
        P.op("scalar", lambda e, o=s2_row, i=rS: e.copy(o, i),
             reads=[(P.psum, bS)], writes=[C.rowscr.r("s2")])
        P.op("vector", lambda e, o=s2_row: e.tensor_mul(o, o, o),
             reads=[C.rowscr.r("s2")], writes=[C.rowscr.r("s2")])
        P.op("vector", lambda e, o=t_row, i=rSS, s2=s2_row:
             e.scalar_tensor_tensor(o, i, TH, s2, ALU.mult, ALU.subtract),
             reads=[(P.psum, bSS), C.rowscr.r("s2")], writes=[C.rowscr.r("t")])
        if need_r2:
            P.op("scalar", lambda e, o=r_row, i=t_row:
                 e.activation(o, i, AF.Sqrt, bias=C.cb(1), scale=(1.0 + EPS)),
                 reads=[C.rowscr.r("t"), C.cbias], writes=[C.rowscr.r("r")])
            P.op("vector", lambda e, o=r_row: e.reciprocal(o, o),
                 reads=[C.rowscr.r("r")], writes=[C.rowscr.r("r")])
        else:
            P.op("scalar", lambda e, o=r_row, i=t_row:
                 e.activation(o, i, AF.Sqrt, bias=C.cb(0), scale=1.0),
                 reads=[C.rowscr.r("t"), C.cbias], writes=[C.rowscr.r("r")])
            P.op("vector", lambda e, o=r_row: e.reciprocal(o, o),
                 reads=[C.rowscr.r("r")], writes=[C.rowscr.r("r")])
        if u is not None:
            P.op("scalar", lambda e, o=r0_row, i=t_row:
                 e.activation(o, i, AF.Sqrt, bias=C.cb(0), scale=1.0),
                 reads=[C.rowscr.r("t"), C.cbias], writes=[C.rowscr.r("r0")])
            P.op("vector", lambda e, o=r0_row: e.reciprocal(o, o),
                 reads=[C.rowscr.r("r0")], writes=[C.rowscr.r("r0")])
        P.op("vector", lambda e, o=C.row2[0:1, 0:tt], i=rS, r=r_row:
             e.scalar_tensor_tensor(o, i, -1.0, r, ALU.mult, ALU.mult),
             reads=[(P.psum, bS), C.rowscr.r("r")], writes=[C.row2.r("b")])
        if u is not None:
            brow0 = C.rowscr[0:1, SB:SB + tt]
            P.op("vector", lambda e, o=brow0, i=rS, r=r0_row:
                 e.scalar_tensor_tensor(o, i, -1.0, r, ALU.mult, ALU.mult),
                 reads=[(P.psum, bS), C.rowscr.r("r0")], writes=[C.rowscr.r("t")])
        for c in range(NC):
            pA = P.bank(bA, 128, tt)
            pB = P.bank(bB, 128, tt)
            if mode == "plain":
                lA = C.c1024_row[:, :]
                lA_reads = [C.c1024_row]
            elif mode == "lnrms":
                lA = gl[0:1, c * 128:(c + 1) * 128]
                lA_reads = [gl]
            else:
                lA = gb2[0:1, c * 128:(c + 1) * 128]
                lA_reads = [gb2]

            def bcmm(e, lA=lA, r=r_row, pA=pA, mode=mode, c=c, pB=pB, tt=tt):
                e.matmul(pA, lA, r, start=True, stop=True)
                if mode == "plain":
                    return e.matmul(pB, C.ones_row_f32[:, :], C.row2[0:1, 0:tt],
                                    start=True, stop=True)
                elif mode == "lnrms":
                    return e.matmul(pB, gl[0:1, c * 128:(c + 1) * 128], C.row2[0:1, 0:tt],
                                    start=True, stop=True)
                else:
                    return e.matmul(pB, gb2[0:2, c * 128:(c + 1) * 128], C.row2[0:2, 0:tt],
                                    start=True, stop=True)
            P.op("tensor", bcmm,
                 reads=lA_reads + [C.rowscr.r("r"), C.row2, C.ones_row_f32],
                 writes=[(P.psum, bA), (P.psum, bB)])
            xb = xn[0:128, c * T + t0: c * T + t0 + tt]
            tmp = C.nscr[0:128, (c % 2) * SB:(c % 2) * SB + tt]
            scaleA = 1.0 if mode == "plain" else TH

            def apply1(e, tmp=tmp, xb=xb, pA=pA, scaleA=scaleA):
                if scaleA == 1.0:
                    return e.tensor_mul(tmp, xb, pA)
                return e.scalar_tensor_tensor(tmp, xb, scaleA, pA, ALU.mult, ALU.mult)
            P.op("vector", apply1,
                 reads=[xn.r((c, j)), (P.psum, bA)], writes=[C.nscr.r(c % 2)])
            P.op("vector", lambda e, o=xb, t=tmp, pB=pB: e.tensor_add(o, t, pB),
                 reads=[C.nscr.r(c % 2), (P.psum, bB)], writes=[xn.r((c, j))])
            if u is not None:
                bA0, bB0 = ubanks
                pA0 = P.bank(bA0, 128, tt)
                pB0 = P.bank(bB0, 128, tt)
                if u == "fold":
                    if c == 0:
                        a0p = C.rowscr[0:1, 3 * SB:3 * SB + tt]
                        P.op("vector", lambda e, o=a0p, i=r0_row:
                             e.tensor_scalar(o, i, TH, 1.0, ALU.mult, ALU.add),
                             reads=[C.rowscr.r("r0")], writes=[C.rowscr.r("s2")])

                def bcmm0(e, r0=r0_row, pA0=pA0, pB0=pB0, tt=tt,
                          brow0=C.rowscr[0:1, SB:SB + tt],
                          a0p=C.rowscr[0:1, 3 * SB:3 * SB + tt], fold=(u == "fold")):
                    if fold:
                        e.matmul(pA0, C.ones_row_f32[:, :], a0p, start=True, stop=True)
                    else:
                        e.matmul(pA0, C.c1024_row[:, :], r0, start=True, stop=True)
                    return e.matmul(pB0, C.ones_row_f32[:, :], brow0, start=True, stop=True)
                P.op("tensor", bcmm0,
                     reads=[C.c1024_row, C.rowscr.r("r0"), C.rowscr.r("t"),
                            C.rowscr.r("s2"), C.ones_row_f32],
                     writes=[(P.psum, bA0), (P.psum, bB0)])
                xs = x[0:128, c * T + t0: c * T + t0 + tt]
                if u == "fold":
                    tmpf = C.f32scr[0:128, (c % 2) * SB:(c % 2) * SB + tt]
                    P.op("vector", lambda e, o=tmpf, i=xs, pA0=pA0: e.tensor_mul(o, i, pA0),
                         reads=[x.r((c, j)), (P.psum, bA0)], writes=[C.f32scr.r(c % 2)])
                    P.op("vector", lambda e, o=xs, t=tmpf, pB0=pB0: e.tensor_add(o, t, pB0),
                         reads=[C.f32scr.r(c % 2), (P.psum, bB0)], writes=[x.r((c, j))])
                else:
                    us = u[0:128, c * T + t0: c * T + t0 + tt]
                    P.op("vector", lambda e, o=us, i=xs, pA0=pA0: e.tensor_mul(o, i, pA0),
                         reads=[x.r((c, j)), (P.psum, bA0)], writes=[u.r((c, j))])
                    P.op("vector", lambda e, o=us, pB0=pB0: e.tensor_add(o, o, pB0),
                         reads=[u.r((c, j)), (P.psum, bB0)], writes=[u.r((c, j))])



def hosttile(M, nsplit=1):
    K, N = M.shape
    assert K % 128 == 0 and N % nsplit == 0
    kc = K // 128
    ng = N // nsplit
    groups = []
    for g in range(nsplit):
        sub = M[:, g * ng:(g + 1) * ng]
        t = sub.reshape(kc, 128, ng).transpose(1, 0, 2)
        groups.append(t.reshape(128, kc * ng))
    return np.ascontiguousarray(np.concatenate(groups, axis=1))


def hostbias(b):
    return np.ascontiguousarray(b.reshape(-1, 128).T)



def emit_transpose_inplace(C, buf, R, Ct):
    P = C.P
    n = R * Ct

    def perm(s):
        r, t = divmod(s, Ct)
        return t * R + r

    def slot_ap(s):
        return buf[0:128, s * 128:(s + 1) * 128]

    bankpick = [0]
    first = [True]

    def tp(s, extra_reads=(), extra_writes=()):
        b = bankpick[0]
        bankpick[0] ^= 1
        pb = P.bankb(b)
        rd = [buf.r(("t", s)), C.id_bf] + list(extra_reads)
        wr = [(P.psumb, b)] + list(extra_writes)
        if first[0]:
            rd.append(buf)
            wr.append(buf)
            first[0] = False
        P.op("tensor", lambda e, i=slot_ap(s), pb=pb: e.transpose(pb, i, C.id_bf[:, :]),
             reads=rd, writes=wr)
        return b

    def evac(bank, dst_slot):
        P.op("vector", lambda e, o=slot_ap(dst_slot), pb=P.bankb(bank): e.tensor_copy(o, pb),
             reads=[(P.psumb, bank)], writes=[buf.r(("t", dst_slot))])

    seen = [False] * n
    for s0 in range(n):
        if seen[s0]:
            continue
        if perm(s0) == s0:
            seen[s0] = True
            b = tp(s0)
            evac(b, s0)
            continue
        cyc = [s0]
        s = perm(s0)
        while s != s0:
            seen[s] = True
            cyc.append(s)
            s = perm(s)
        seen[s0] = True
        hold = tp(cyc[0])
        for i in range(1, len(cyc)):
            nb = tp(cyc[i])
            evac(hold, cyc[i])
            hold = nb
        evac(hold, cyc[0])



def emit_projB(C, src, wdram, wbuf, T, n_out, dst=None, bias=None, bias_col0=0,
               nhalf=2, banks=(0, 1, 2, 3), wslots=("wA", "wB"), evac=None,
               src_col0=0):
    P = C.P
    TBn = T // 512
    ncol = n_out // nhalf
    noc = ncol // 128
    half_cols = 8 * ncol
    bi = 0
    for h in range(nhalf):
        wslot = h % 2
        base = wslot * half_cols
        P.op("sync", lambda e, h=h, base=base:
             e.dma_start(out=wbuf[0:128, base:base + half_cols],
                         in_=wdram[0:128, h * half_cols:(h + 1) * half_cols]),
             writes=[wbuf.r(("w", wslot))], dma=True, slot=wslots[wslot])
        for oc in range(noc):
            goc = h * noc + oc
            for tb in range(TBn):
                bk = banks[bi % len(banks)]
                bi += 1
                bank = P.bank(bk)

                def mm(e, base=base, oc=oc, tb=tb, bank=bank, wslot=wslot):
                    last = None
                    for kc in range(8):
                        last = e.matmul(
                            bank,
                            wbuf[0:128, base + kc * ncol + oc * 128: base + kc * ncol + oc * 128 + 128],
                            src[0:128, src_col0 + kc * T + tb * 512: src_col0 + kc * T + (tb + 1) * 512],
                            start=(kc == 0), stop=(kc == 7))
                    return last
                P.op("tensor", mm, reads=[wbuf.r(("w", wslot)), src],
                     writes=[(P.psum, bk)])
                if evac is not None:
                    evac(goc, tb, bank, bk)
                else:
                    dslice = dst[0:128, goc * T + tb * 512: goc * T + (tb + 1) * 512]
                    if bias is not None:
                        P.op("scalar", lambda e, o=dslice, bank=bank, col=bias_col0 + goc:
                             e.activation(o, bank, AF.Identity, bias=bias[0:128, col:col + 1]),
                             reads=[(P.psum, bk), bias], writes=[dst])
                    else:
                        P.op("scalar", lambda e, o=dslice, bank=bank:
                             e.copy(o, bank),
                             reads=[(P.psum, bk)], writes=[dst])



def emit_softmax_tokens(C, buf, T, cscr, scale2=None):
    P = C.P
    for c in range(8):
        sl = buf[0:128, c * T:(c + 1) * T]
        nm = cscr[0:128, c:c + 1]
        sm = cscr[0:128, 8 + c:9 + c]
        rc = cscr[0:128, 16 + c:17 + c]
        fence = [buf] if c == 0 else []
        P.op("vector", lambda e, o=nm, i=sl: e.reduce_max(o, i, axis=AX.X, negate=True),
             reads=[buf.r(("sm", c))], writes=[cscr.r(("nm", c))], fence=fence)
        P.op("scalar", lambda e, o=sl, i=sl, b=nm, a=sm:
             e.activation(o, i, AF.Exp, bias=b, accum_out=a),
             reads=[buf.r(("sm", c)), cscr.r(("nm", c))],
             writes=[buf.r(("sm", c)), cscr.r(("s", c))])
        P.op("vector", lambda e, o=rc, i=sm: e.reciprocal(o, i),
             reads=[cscr.r(("s", c))], writes=[cscr.r(("rc", c))])
        if scale2 is None:
            P.op("vector", lambda e, o=sl, r=rc: e.tensor_scalar_mul(o, o, r),
                 reads=[buf.r(("sm", c)), cscr.r(("rc", c))], writes=[buf.r(("sm", c))])
        else:
            P.op("vector", lambda e, o=sl, r=rc, s2=scale2:
                 e.tensor_scalar(o, o, r, s2, ALU.mult, ALU.mult),
                 reads=[buf.r(("sm", c)), cscr.r(("rc", c))], writes=[buf.r(("sm", c))])



def emit_l0_attn_a(C, x, xn, s1, s2, m4, dd, wbuf, bias, cscr, T):
    rbc = C.f32scr
    """x f32 [128,8T]; xn bf16 [128,8T] = LN(x) (consumed); s1,s2 bf16 [128,8T]
    arenas; m4 bf16 [128,8192]; dd: dram APs dict (wq,wk,wv tiled, nhalf=2);
    bias: Buf [128,24] (bq|bk|bv); cscr f32 [128,32]; zrow f32 [1, T];
    rbc f32 [128, 512]."""
    P = C.P
    Ct = T // 128
    TBn = T // 512
    emit_projB(C, xn, dd["wk"], wbuf, T, 1024, dst=s1, bias=bias, bias_col0=8)
    emit_softmax_tokens(C, s1, T, cscr, scale2=1.0 / 32.0)
    emit_transpose_inplace(C, s1, 8, Ct)
    emit_projB(C, xn, dd["wv"], wbuf, T, 1024, dst=s2, bias=bias, bias_col0=16)
    emit_softmax_tokens(C, s2, T, cscr)
    emit_transpose_inplace(C, s2, 8, Ct)
    bi = 0
    for oc in range(8):
        for nb in range(2):
            bk = (bi % 4)
            bi += 1
            bank = P.bank(bk)

            def mmM(e, oc=oc, nb=nb, bank=bank):
                last = None
                for tt in range(Ct):
                    last = e.matmul(
                        bank,
                        s1[0:128, (tt * 8 + oc) * 128:(tt * 8 + oc) * 128 + 128],
                        s2[0:128, (tt * 8 + nb * 4) * 128:(tt * 8 + nb * 4) * 128 + 512],
                        start=(tt == 0), stop=(tt == Ct - 1))
                return last
            P.op("tensor", mmM, reads=[s1, s2], writes=[(P.psum, bk)])
            P.op("scalar", lambda e, oc=oc, nb=nb, bank=bank:
                 e.copy(m4[0:128, oc * 1024 + nb * 512: oc * 1024 + (nb + 1) * 512], bank),
                 reads=[(P.psum, bk)], writes=[m4])
    emit_projB(C, xn, dd["wq"], wbuf, T, 1024, dst=s2, bias=bias, bias_col0=0)
    for c in range(8):
        P.op("vector", lambda e, c=c: e.reduce_max(cscr[0:128, c:c + 1],
                                                   s2[0:128, c * T:(c + 1) * T], axis=AX.X),
             reads=[s2], writes=[cscr.r(("gm", c))],
             fence=[s2, cscr] if c == 0 else [])
    P.op("vector", lambda e: e.reduce_max(cscr[0:128, 8:9], cscr[0:128, 0:8], axis=AX.X),
         reads=[cscr.r(("gm", c)) for c in range(8)], writes=[cscr.r("m128")])
    P.op("tensor", lambda e: e.transpose(P.bank(4, 1, 128), cscr[0:128, 8:9], C.id_f32[:, :]),
         reads=[cscr.r("m128"), C.id_f32], writes=[(P.psum, 4)])
    P.op("vector", lambda e: e.reduce_max(cscr[0:1, 9:10], P.bank(4, 1, 128), axis=AX.X),
         reads=[(P.psum, 4)], writes=[cscr.r("gmax")])
    P.op("tensor", lambda e: e.matmul(P.bank(4, 128, 1), C.ones_row_f32[:, :],
                                      cscr[0:1, 9:10], start=True, stop=True),
         reads=[cscr.r("gmax"), C.ones_row_f32], writes=[(P.psum, 4)])
    P.op("scalar", lambda e: e.mul(cscr[0:128, 10:11], P.bank(4, 128, 1), -1.0),
         reads=[(P.psum, 4)], writes=[cscr.r("negmax")])
    for c in range(8):
        P.op("scalar", lambda e, c=c: e.activation(s2[0:128, c * T:(c + 1) * T],
                                                   s2[0:128, c * T:(c + 1) * T],
                                                   AF.Exp, bias=cscr[0:128, 10:11]),
             reads=[s2.r(("e", c)), cscr.r("negmax")], writes=[s2.r(("e", c))],
             fence=[s2] if c == 0 else [])
    bi = 0
    zf = C.rowscr[0:1, 2 * SB:3 * SB]
    for tb in range(TBn):
        def mmZ(e, tb=tb):
            last = None
            for kc in range(8):
                last = e.matmul(P.bank(4, 1, 512), C.ones_col_bf[:, :],
                                s2[0:128, kc * T + tb * 512: kc * T + (tb + 1) * 512],
                                start=(kc == 0), stop=(kc == 7))
            return last
        P.op("tensor", mmZ, reads=[s2, C.ones_col_bf], writes=[(P.psum, 4)])
        P.op("vector", lambda e: e.reciprocal(zf, P.bank(4, 1, 512)),
             reads=[(P.psum, 4)], writes=[C.rowscr.r("zf")])
        P.op("tensor", lambda e: e.matmul(P.bank(5), C.ones_row_f32[:, :],
                                          zf, start=True, stop=True),
             reads=[C.rowscr.r("zf"), C.ones_row_f32], writes=[(P.psum, 5)])
        P.op("scalar", lambda e: e.copy(rbc[0:128, 0:512], P.bank(5)),
             reads=[(P.psum, 5)], writes=[rbc])
        for oc in range(8):
            bk = (bi % 4)
            bi += 1
            bank = P.bank(bk)

            def mmO(e, oc=oc, tb=tb, bank=bank):
                last = None
                for kc in range(8):
                    last = e.matmul(bank,
                                    m4[0:128, kc * 1024 + oc * 128: kc * 1024 + oc * 128 + 128],
                                    s2[0:128, kc * T + tb * 512: kc * T + (tb + 1) * 512],
                                    start=(kc == 0), stop=(kc == 7))
                return last
            P.op("tensor", mmO, reads=[m4, s2], writes=[(P.psum, bk)])
            xsl = x[0:128, oc * T + tb * 512: oc * T + (tb + 1) * 512]
            P.op("vector", lambda e, bank=bank: e.tensor_mul(rbc[0:128, 512:1024], bank, rbc[0:128, 0:512]),
                 reads=[(P.psum, bk), rbc.r("z")], writes=[rbc.r("t")])
            P.op("vector", lambda e, o=xsl: e.tensor_add(o, o, rbc[0:128, 512:1024]),
                 reads=[rbc.r("t"), x.r((oc, tb))], writes=[x.r((oc, tb))])



def emit_l0_ffn(C, x, t, s1, s2, dd, wbuf, moescr, T):
    P = C.P
    TBn = T // 512
    ntile = T // 128
    gw = moescr["gw"]; gbbc = moescr["gbbc"]; gbuf = moescr["gbuf"]
    top8 = moescr["top8"]; mwork = moescr["mwork"]; ctm = moescr["ctm"]
    cfmb = moescr["cfmb"]; oneh = moescr["oneh"]
    cebc = moescr["cebc"]; b1b = moescr["b1b"]; b2b = moescr["b2b"]
    vs1 = moescr["vs1"]; vs2 = moescr["vs2"]

    P.op("sync", lambda e: e.dma_start(out=gw[:, :], in_=dd["gw"][0:128, 0:256]),
         writes=[gw], dma=True, slot="gw")
    P.op("sync", lambda e: e.dma_start(out=vs1[0:32, 0:1024], in_=dd["b2"]),
         writes=[vs1], dma=True, slot="b2")
    P.op("vector", lambda e: e.tensor_copy(b2b[0:32, 0:1024], vs1[0:32, 0:1024]),
         reads=[vs1], writes=[b2b])

    P.op("sync", lambda e: e.dma_start(out=moescr["gbrow"][:, :], in_=dd["gb"]),
         writes=[moescr["gbrow"]], dma=True, slot="gbrow")
    P.op("tensor", lambda e: e.matmul(P.bank(4, 128, 32), C.ones_row_f32[:, :],
                                      moescr["gbrow"][0:1, :], start=True, stop=True),
         reads=[moescr["gbrow"], C.ones_row_f32], writes=[(P.psum, 4)])
    P.op("scalar", lambda e: e.copy(gbbc[:, :], P.bank(4, 128, 32)),
         reads=[(P.psum, 4)], writes=[gbbc])

    for tt in range(ntile):
        par = tt % 2
        def mmg(e, tt=tt):
            last = None
            for kc in range(8):
                last = e.matmul(P.bank(4, 128, 32),
                                t[0:128, kc * T + tt * 128: kc * T + (tt + 1) * 128],
                                gw[0:128, kc * 32:(kc + 1) * 32],
                                start=(kc == 0), stop=(kc == 7))
            return last
        P.op("tensor", mmg, reads=[t, gw], writes=[(P.psum, 4)])
        P.op("vector", lambda e, par=par: e.tensor_add(gbuf[0:128, par * 32:(par + 1) * 32],
                                                       P.bank(4, 128, 32), gbbc[:, :]),
             reads=[(P.psum, 4), gbbc], writes=[gbuf.r(("g", par))])
        gsl = gbuf[0:128, par * 32:(par + 1) * 32]
        P.op("vector", lambda e, gsl=gsl: e.max(top8[:, 0:8], gsl),
             reads=[gbuf.r(("g", par))], writes=[top8])
        P.op("scalar", lambda e: e.mul(top8[:, 8:9], top8[:, 0:1], -1.0),
             reads=[top8], writes=[top8.r("nm")])
        P.op("vector", lambda e, gsl=gsl: e.tensor_single_scalar(mwork[:, 0:32], gsl,
                                                                 top8[:, 3:4], ALU.is_ge),
             reads=[gbuf.r(("g", par)), top8], writes=[mwork.r("m")])
        P.op("scalar", lambda e, gsl=gsl: e.activation(mwork[:, 32:64], gsl, AF.Exp,
                                                       bias=top8[:, 8:9]),
             reads=[gbuf.r(("g", par)), top8.r("nm")], writes=[mwork.r("e")])
        P.op("vector", lambda e: e.tensor_mul(mwork[:, 32:64], mwork[:, 32:64], mwork[:, 0:32]),
             reads=[mwork.r("m"), mwork.r("e")], writes=[mwork.r("e")])
        P.op("vector", lambda e: e.reduce_sum(top8[:, 9:10], mwork[:, 32:64], axis=AX.X),
             reads=[mwork.r("e")], writes=[top8.r("z")])
        P.op("vector", lambda e: e.reciprocal(top8[:, 10:11], top8[:, 9:10]),
             reads=[top8.r("z")], writes=[top8.r("rz")])
        P.op("vector", lambda e, par=par: e.tensor_scalar_mul(ctm[0:128, par * 32:(par + 1) * 32],
                                                              mwork[:, 32:64], top8[:, 10:11]),
             reads=[mwork.r("e"), top8.r("rz")], writes=[ctm.r(("c", par))])
        P.op("tensor", lambda e, par=par: e.transpose(P.bank(5, 32, 128),
                                                      ctm[0:128, par * 32:(par + 1) * 32],
                                                      C.id_f32[:, :]),
             reads=[ctm.r(("c", par)), C.id_f32], writes=[(P.psum, 5)])
        P.op("scalar", lambda e, tt=tt: e.copy(cfmb[0:32, tt * 128:(tt + 1) * 128],
                                               P.bank(5, 32, 128)),
             reads=[(P.psum, 5)], writes=[cfmb])

    bi = 0
    for ei in range(32):
        P.op("sync", lambda e, ei=ei: e.dma_start(out=b1b[:, :], in_=dd["b1"][ei]),
             writes=[b1b], dma=True, slot="b1b")
        P.op("sync", lambda e, ei=ei: e.dma_start(
            out=oneh[0:32, 0:128], in_=dd["onehot"][0:32, ei * 128:(ei + 1) * 128]),
             writes=[oneh], dma=True, slot="oneh")

        for q in range(4):
            wslot = q % 2
            base = wslot * 4096
            P.op("sync", lambda e, ei=ei, q=q, base=base:
                 e.dma_start(out=wbuf[0:128, base:base + 4096],
                             in_=dd["w1"][ei][0:128, q * 4096:(q + 1) * 4096]),
                 writes=[wbuf.r(("w", wslot))], dma=True,
                 slot="wA" if wslot == 0 else "wB")
            if q < 2:
                octb = [(oc, tb) for oc in range(4) for tb in range(TBn)]
            else:
                octb = [(oc, tb) for tb in range(TBn) for oc in range(4)]
            for oc, tb in octb:
                if True:
                    goc = q * 4 + oc
                    bk = bi % 4
                    bi += 1
                    bank = P.bank(bk)

                    def mm1(e, base=base, oc=oc, tb=tb, bank=bank):
                        last = None
                        for kc in range(8):
                            last = e.matmul(
                                bank,
                                wbuf[0:128, base + kc * 512 + oc * 128: base + kc * 512 + oc * 128 + 128],
                                t[0:128, kc * T + tb * 512: kc * T + (tb + 1) * 512],
                                start=(kc == 0), stop=(kc == 7))
                        return last
                    P.op("tensor", mm1, reads=[wbuf.r(("w", wslot)), t],
                         writes=[(P.psum, bk)])
                    if goc < 8:
                        P.op("vector", lambda e, goc=goc, tb=tb, bank=bank:
                             e.tensor_scalar(s1[0:128, goc * T + tb * 512: goc * T + (tb + 1) * 512],
                                             bank, b1b[0:128, goc:goc + 1], 7.0,
                                             ALU.add, ALU.min),
                             reads=[(P.psum, bk), b1b], writes=[s1.r(("g", goc, tb))])
                    else:
                        c = goc - 8
                        if oc == 0:
                            P.op("tensor", lambda e, tb=tb:
                                 e.matmul(P.bank(4), oneh[0:32, 0:128],
                                          cfmb[0:32, tb * 512:(tb + 1) * 512],
                                          start=True, stop=True),
                                 reads=[oneh, cfmb], writes=[(P.psum, 4)])
                            P.op("scalar", lambda e: e.mul(cebc[0:128, 0:512],
                                                           P.bank(4), 1.0 / 1.702),
                                 reads=[(P.psum, 4)], writes=[cebc.r(("ce",))])
                        gsl = s1[0:128, c * T + tb * 512: c * T + (tb + 1) * 512]
                        par = (c * TBn + tb) % 2
                        v1 = vs1[0:128, par * 512:(par + 1) * 512]
                        v2 = vs2[0:128, par * 512:(par + 1) * 512]
                        P.op("scalar", lambda e, gsl=gsl, v1=v1:
                             e.activation(v1, gsl, AF.Silu, scale=1.702),
                             reads=[s1.r(("g", c, tb))], writes=[vs1.r(("a", par))])
                        P.op("vector", lambda e, c=c, bank=bank, v2=v2:
                             e.tensor_scalar(v2, bank, b1b[0:128, 8 + c:9 + c], 7.0,
                                             ALU.add, ALU.min),
                             reads=[(P.psum, bk), b1b], writes=[vs2.r(("b", par))])
                        P.op("vector", lambda e, v2=v2:
                             e.tensor_scalar(v2, v2, -7.0, 1.0, ALU.max, ALU.add),
                             reads=[vs2.r(("b", par))], writes=[vs2.r(("b", par))])
                        P.op("vector", lambda e, v1=v1, v2=v2:
                             e.tensor_mul(v1, v1, v2),
                             reads=[vs1.r(("a", par)), vs2.r(("b", par))], writes=[vs1.r(("a", par))])
                        P.op("vector", lambda e, c=c, tb=tb, v1=v1:
                             e.tensor_mul(s2[0:128, c * T + tb * 512: c * T + (tb + 1) * 512],
                                          v1, cebc[0:128, 0:512]),
                             reads=[vs1.r(("a", par)), cebc.r(("ce",))],
                             writes=[s2.r(("h", c, tb))])
        for hh in range(2):
            wslot = hh % 2
            base = wslot * 4096
            P.op("sync", lambda e, ei=ei, hh=hh, base=base:
                 e.dma_start(out=wbuf[0:128, base:base + 4096],
                             in_=dd["w2"][ei][0:128, hh * 4096:(hh + 1) * 4096]),
                 writes=[wbuf.r(("w", wslot))], dma=True,
                 slot="wA" if wslot == 0 else "wB")
            for oc in range(4):
                goc = hh * 4 + oc
                for tb in range(TBn):
                    bk = bi % 4
                    bi += 1
                    bank = P.bank(bk)

                    def mm2(e, base=base, oc=oc, tb=tb, bank=bank, ei=ei, goc=goc):
                        first = True
                        last = None
                        if ei == 0:
                            last = e.matmul(bank, b2b[0:32, goc * 128:(goc + 1) * 128],
                                            cfmb[0:32, tb * 512:(tb + 1) * 512],
                                            start=True, stop=False)
                            first = False
                        for kc in range(8):
                            last = e.matmul(
                                bank,
                                wbuf[0:128, base + kc * 512 + oc * 128: base + kc * 512 + oc * 128 + 128],
                                s2[0:128, kc * T + tb * 512: kc * T + (tb + 1) * 512],
                                start=first, stop=(kc == 7))
                            first = False
                        return last
                    rds = [wbuf.r(("w", wslot))] + [s2.r(("h", kc, tb)) for kc in range(8)]
                    if ei == 0:
                        rds += [b2b, cfmb]
                    P.op("tensor", mm2, reads=rds, writes=[(P.psum, bk)])
                    xsl = x[0:128, goc * T + tb * 512: goc * T + (tb + 1) * 512]
                    P.op("vector", lambda e, xsl=xsl, bank=bank:
                         e.tensor_add(xsl, xsl, bank),
                         reads=[(P.psum, bk), x.r((goc, tb))],
                         writes=[x.r((goc, tb))])



def emit_l0_attn_b(C, x, xn, ak, av, pt, outfm, aqc, dd, wbuf, bias, cscr,
                   mrowb, zbsb, lws, T, parts=("mha", "lact")):
    P = C.P
    ntile = T // 128
    NQB = T // 512
    nch = T // 1024

    def ptref(ki):
        return pt.r(("p", ki)) if isinstance(pt, Buf) else pt

    P.op("scalar", lambda e: e.mul(bias[0:128, 0:8], bias[0:128, 0:8], 0.125),
         reads=[bias], writes=[bias])

    def ipgroup(g):
        return dd["ip"][0:128, g * 4096:(g + 1) * 4096]

    for g in (2, 3):
        wslot = g % 2
        base = wslot * 4096
        P.op("sync", lambda e, g=g, base=base:
             e.dma_start(out=wbuf[0:128, base:base + 4096], in_=ipgroup(g)),
             writes=[wbuf.r(("w", wslot))], dma=True, slot="wA" if wslot == 0 else "wB")
        for oc in range(4):
            goc = (g - 2) * 4 + oc
            for tb in range(T // 512):
                bk = (goc * (T // 512) + tb) % 4
                bank = P.bank(bk)

                def mmk(e, base=base, oc=oc, tb=tb, bank=bank):
                    last = None
                    for kc in range(8):
                        last = e.matmul(bank,
                                        wbuf[0:128, base + kc * 512 + oc * 128: base + kc * 512 + oc * 128 + 128],
                                        xn[0:128, kc * T + tb * 512: kc * T + (tb + 1) * 512],
                                        start=(kc == 0), stop=(kc == 7))
                    return last
                P.op("tensor", mmk, reads=[wbuf.r(("w", wslot)), xn], writes=[(P.psum, bk)])
                P.op("scalar", lambda e, goc=goc, tb=tb, bank=bank:
                     e.activation(ak[0:128, goc * T + tb * 512: goc * T + (tb + 1) * 512],
                                  bank, AF.Identity, bias=bias[0:128, 8 + goc:9 + goc]),
                     reads=[(P.psum, bk), bias], writes=[ak])
    for g in (4, 5):
        wslot = g % 2
        base = wslot * 4096
        P.op("sync", lambda e, g=g, base=base:
             e.dma_start(out=wbuf[0:128, base:base + 4096], in_=ipgroup(g)),
             writes=[wbuf.r(("w", wslot))], dma=True, slot="wA" if wslot == 0 else "wB")
        vh = g - 4
        for tt in range(ntile):
            bk = (tt + vh) % 4
            bank = P.bank(bk)

            def mmv(e, base=base, tt=tt, bank=bank, vh=vh):
                last = e.matmul(bank, C.ones_row_bf[:, :],
                                lws["ibrow"][0:1, vh * 512:(vh + 1) * 512],
                                start=True, stop=False)
                for kc in range(8):
                    last = e.matmul(bank,
                                    xn[0:128, kc * T + tt * 128: kc * T + (tt + 1) * 128],
                                    wbuf[0:128, base + kc * 512: base + (kc + 1) * 512],
                                    start=False, stop=(kc == 7))
                return last
            P.op("tensor", mmv, reads=[wbuf.r(("w", wslot)), xn, lws["ibrow"],
                                       C.ones_row_bf], writes=[(P.psum, bk)])
            P.op("scalar", lambda e, tt=tt, vh=vh, bank=bank:
                 e.copy(av[0:128, tt * 1024 + vh * 512: tt * 1024 + (vh + 1) * 512], bank),
                 reads=[(P.psum, bk)], writes=[av])
    for qb in range(NQB if "mha" in parts else 0):
        for g in (0, 1):
            base = g * 4096
            P.op("sync", lambda e, g=g, base=base:
                 e.dma_start(out=wbuf[0:128, base:base + 4096], in_=ipgroup(g)),
                 writes=[wbuf.r(("w", g))], dma=True, slot="wA" if g == 0 else "wB")
        for c in range(8):
            bank = P.bank(4)
            g = c // 4
            oc = c % 4

            def mmq(e, g=g, oc=oc, qb=qb, bank=bank):
                last = None
                for kc in range(8):
                    last = e.matmul(bank,
                                    wbuf[0:128, g * 4096 + kc * 512 + oc * 128: g * 4096 + kc * 512 + oc * 128 + 128],
                                    xn[0:128, kc * T + qb * 512: kc * T + (qb + 1) * 512],
                                    start=(kc == 0), stop=(kc == 7))
                return last
            P.op("tensor", mmq, reads=[wbuf.r(("w", g)), xn], writes=[(P.psum, 4)])
            P.op("scalar", lambda e, c=c, bank=bank:
                 e.activation(aqc[0:128, 0:512], bank, AF.Identity,
                              bias=bias[0:128, c:c + 1], scale=0.125),
                 reads=[(P.psum, 4), bias], writes=[aqc])
            for hp in range(2):
                h = 2 * c + hp
                hb = hp * 64
                for qt in range(4):
                    def mmS(e, c=c, hb=hb, qb=qb, qt=qt):
                        last = None
                        for kb in range(T // 512):
                            last = e.matmul(
                                P.bank(kb % 4, 128, 512),
                                aqc[hb:hb + 64, qt * 128:(qt + 1) * 128],
                                ak[hb:hb + 64, c * T + kb * 512: c * T + (kb + 1) * 512],
                                start=True, stop=True)
                        return last
                    P.op("tensor", mmS, reads=[aqc, ak],
                         writes=[(P.psum, kb) for kb in range(min(4, T // 512))])
                    P.op("vector", lambda e, qt=qt: e.reduce_max(
                        cscr[0:128, qt:qt + 1], P.bank(0, 128, min(2048, T)), axis=AX.X,
                        negate=True),
                         reads=[(P.psum, kb) for kb in range(min(4, T // 512))],
                         writes=[cscr.r(("nm", qt))])
                    P.op("tensor", lambda e, qt=qt: e.transpose(
                        P.bank(4, 1, 128), cscr[0:128, qt:qt + 1], C.id_f32[:, :]),
                         reads=[cscr.r(("nm", qt)), C.id_f32], writes=[(P.psum, 4)])
                    P.op("scalar", lambda e, qt=qt: e.copy(
                        mrowb[0:1, qt * 128:(qt + 1) * 128], P.bank(4, 1, 128)),
                         reads=[(P.psum, 4)], writes=[mrowb])
                for half in range(2):
                    for ki in range(ntile // 2):
                        kt = half * (ntile // 2) + ki
                        bst = 4 + (ki % 2)
                        bank = P.bank(bst)

                        def mmST(e, c=c, hb=hb, kt=kt, qb=qb, bank=bank):
                            e.matmul(bank, C.ones_row_bf[:, :], mrowb[0:1, 0:512],
                                     start=True, stop=False)
                            return e.matmul(bank,
                                            ak[hb:hb + 64, c * T + kt * 128: c * T + (kt + 1) * 128],
                                            aqc[hb:hb + 64, 0:512],
                                            start=False, stop=True)
                        P.op("tensor", mmST, reads=[ak, aqc, mrowb, C.ones_row_bf],
                             writes=[(P.psum, bst)])
                        P.op("scalar", lambda e, ki=ki, bank=bank:
                             e.activation(pt[0:128, ki * 512:(ki + 1) * 512], bank, AF.Exp),
                             reads=[(P.psum, bst)], writes=[ptref(ki)])
                    def mmZ(e, half=half):
                        last = None
                        for ki in range(ntile // 2):
                            last = e.matmul(P.bank(3, 1, 512), C.ones_col_bf[:, :],
                                            pt[0:128, ki * 512:(ki + 1) * 512],
                                            start=(half == 0 and ki == 0),
                                            stop=(half == 1 and ki == ntile // 2 - 1))
                        return last
                    P.op("tensor", mmZ,
                         reads=[ptref(ki) for ki in range(ntile // 2)] + [C.ones_col_bf],
                         writes=[(P.psum, 3)])

                    def mmO(e, half=half, h=h):
                        last = None
                        for ki in range(ntile // 2):
                            kt = half * (ntile // 2) + ki
                            last = e.matmul(P.bank(2, 64, 512),
                                            av[0:128, kt * 1024 + h * 64: kt * 1024 + h * 64 + 64],
                                            pt[0:128, ki * 512:(ki + 1) * 512],
                                            start=(half == 0 and ki == 0),
                                            stop=(half == 1 and ki == ntile // 2 - 1))
                        return last
                    P.op("tensor", mmO,
                         reads=[ptref(ki) for ki in range(ntile // 2)] + [av],
                         writes=[(P.psum, 2)])
                zf = C.rowscr[0:1, 2 * SB:3 * SB]
                P.op("vector", lambda e, zf=zf: e.reciprocal(zf, P.bank(3, 1, 512)),
                     reads=[(P.psum, 3)], writes=[C.rowscr.r("zf")])
                P.op("tensor", lambda e, zf=zf: e.matmul(P.bank(5), C.ones_row_f32[:, :],
                                                         zf, start=True, stop=True),
                     reads=[C.rowscr.r("zf"), C.ones_row_f32], writes=[(P.psum, 5)])
                P.op("scalar", lambda e: e.copy(zbsb[0:128, 0:512], P.bank(5)),
                     reads=[(P.psum, 5)], writes=[zbsb])
                P.op("vector", lambda e, hb=hb, c=c:
                     e.tensor_mul(outfm[hb:hb + 64, c * 512:(c + 1) * 512],
                                  P.bank(2, 64, 512), zbsb[0:64, 0:512]),
                     reads=[(P.psum, 2), zbsb], writes=[outfm.r(("o", c, hb))])
        for hh2 in range(2):
            base2 = hh2 * 4096
            P.op("sync", lambda e, hh2=hh2, base2=base2:
                 e.dma_start(out=wbuf[0:128, base2:base2 + 4096],
                             in_=dd["op"][0:128, hh2 * 4096:(hh2 + 1) * 4096]),
                 writes=[wbuf.r(("w", hh2))], dma=True, slot="wA" if hh2 == 0 else "wB")
            for oc in range(4):
                goc = hh2 * 4 + oc
                bk = goc % 2
                bank = P.bank(bk)

                def mmop(e, base2=base2, oc=oc, bank=bank):
                    last = None
                    for kc in range(8):
                        last = e.matmul(bank,
                                        wbuf[0:128, base2 + kc * 512 + oc * 128: base2 + kc * 512 + oc * 128 + 128],
                                        outfm[0:128, kc * 512:(kc + 1) * 512],
                                        start=(kc == 0), stop=(kc == 7))
                    return last
                P.op("tensor", mmop,
                     reads=[wbuf.r(("w", hh2))] + [outfm.r(("o", kc, b2)) for kc in range(8) for b2 in (0, 64)],
                     writes=[(P.psum, bk)])
                xsl = x[0:128, goc * T + qb * 512: goc * T + (qb + 1) * 512]
                P.op("vector", lambda e, xsl=xsl, bank=bank, goc=goc:
                     e.scalar_tensor_tensor(xsl, bank, lws["opb"][0:128, goc:goc + 1],
                                            xsl, ALU.add, ALU.add),
                     reads=[(P.psum, bk), x.r((goc, qb)), lws["opb"]],
                     writes=[x.r((goc, qb))])

    wl = lws["wl"]; wn = lws["wn"]
    kvtm = lws["kvtm"]; ptm = lws["ptm"]; htm = lws["htm"]; h2tm = lws["h2tm"]
    for lh in (range(4) if "lact" in parts else range(0)):
        P.op("sync", lambda e, lh=lh: e.dma_start(out=wl[0:128, 0:1536],
                                                    in_=dd["wl"][lh][0:128, 0:1536]),
             writes=[wl], dma=True, slot="wl",
             fence=[ak, av, pt] if lh == 0 else [])
        P.op("sync", lambda e, lh=lh: e.dma_start(out=wn[0:128, 1024:1536],
                                                    in_=dd["wln"][lh][0:128, 0:512]),
             writes=[wn], dma=True, slot="wln")
        P.op("vector", lambda e: e.tensor_copy(wn[0:128, 0:1024], wl[0:128, 0:1024]),
             reads=[wl], writes=[wn])
        for n in range(nch):
            col0 = n * 1024
            ch0, ch1 = 2 * lh, 2 * lh + 1

            def kvap(kc, cb=None, lh=lh, col0=col0):
                ch = 2 * lh + kc
                if cb is None:
                    return xn[0:128, ch * T + col0: ch * T + col0 + 1024]
                return xn[0:128, ch * T + col0 + cb * 512: ch * T + col0 + (cb + 1) * 512]

            for dst, wbase in ((lws["e1"], 0), (lws["e2"], 512)):
                for ec in range(2):
                    for cb in range(2):
                        bank = P.bank((ec * 2 + cb) % 4)

                        def mmg(e, wbase=wbase, ec=ec, cb=cb, bank=bank, kvap=kvap):
                            last = None
                            for kc in range(2):
                                last = e.matmul(bank,
                                                wn[0:128, wbase + kc * 256 + ec * 128: wbase + kc * 256 + ec * 128 + 128],
                                                kvap(kc, cb), start=(kc == 0), stop=(kc == 1))
                            return last
                        P.op("tensor", mmg, reads=[wn, xn],
                             writes=[(P.psum, (ec * 2 + cb) % 4)])
                        P.op("scalar", lambda e, dst=dst, ec=ec, cb=cb, bank=bank:
                             e.copy(dst[0:128, ec * 1024 + cb * 512: ec * 1024 + (cb + 1) * 512], bank),
                             reads=[(P.psum, (ec * 2 + cb) % 4)], writes=[dst])
            for ec in range(2):
                for cb in range(2):
                    bank = P.bank((ec * 2 + cb) % 4)

                    def mmd(e, ec=ec, cb=cb, bank=bank, kvap=kvap):
                        last = None
                        for kc in range(2):
                            last = e.matmul(bank,
                                            wn[0:128, 1024 + kc * 256 + ec * 128: 1024 + kc * 256 + ec * 128 + 128],
                                            kvap(kc, cb), start=(kc == 0), stop=(kc == 1))
                        return last
                    P.op("tensor", mmd, reads=[wn, xn],
                         writes=[(P.psum, (ec * 2 + cb) % 4)])
                    P.op("scalar", lambda e, ec=ec, cb=cb, bank=bank:
                         e.copy(lws["e3"][0:128, ec * 1024 + cb * 512: ec * 1024 + (cb + 1) * 512], bank),
                         reads=[(P.psum, (ec * 2 + cb) % 4)], writes=[lws["e3"]])
            E = lws
            P.op("scalar", lambda e: e.activation(E["e4"][:, :], E["e1"][:, :], AF.Sigmoid),
                 reads=[E["e1"]], writes=[E["e4"]])
            P.op("vector", lambda e: e.tensor_mul(E["e5"][:, :], E["e1"][:, :], E["e4"][:, :]),
                 reads=[E["e1"], E["e4"]], writes=[E["e5"]])
            P.op("vector", lambda e: e.tensor_mul(E["e6"][:, :], E["e5"][:, :], E["e2"][:, :]),
                 reads=[E["e5"], E["e2"]], writes=[E["e6"]])
            P.op("vector", lambda e: e.scalar_tensor_tensor(E["e7"][:, :], E["e4"][:, :],
                                                            -1.0, E["e1"][:, :], ALU.mult,
                                                            ALU.bypass) if False else
                 e.tensor_scalar(E["e7"][:, :], E["e4"][:, :], -1.0, 1.0, ALU.mult, ALU.add),
                 reads=[E["e4"]], writes=[E["e7"]])
            P.op("vector", lambda e: e.tensor_mul(E["e7"][:, :], E["e7"][:, :], E["e1"][:, :]),
                 reads=[E["e7"], E["e1"]], writes=[E["e7"]])
            P.op("vector", lambda e: e.tensor_scalar_add(E["e7"][:, :], E["e7"][:, :], 1.0),
                 reads=[E["e7"]], writes=[E["e7"]])
            P.op("vector", lambda e: e.tensor_mul(E["e7"][:, :], E["e7"][:, :], E["e4"][:, :]),
                 reads=[E["e7"], E["e4"]], writes=[E["e7"]])
            P.op("vector", lambda e: e.tensor_mul(E["e1"][:, :], E["e3"][:, :], E["e2"][:, :]),
                 reads=[E["e3"], E["e2"]], writes=[E["e1"]])
            P.op("vector", lambda e: e.tensor_mul(E["e1"][:, :], E["e1"][:, :], E["e7"][:, :]),
                 reads=[E["e1"], E["e7"]], writes=[E["e1"]])
            P.op("vector", lambda e: e.tensor_mul(E["e3"][:, :], E["e3"][:, :], E["e5"][:, :]),
                 reads=[E["e3"], E["e5"]], writes=[E["e3"]])
            for (srcf, dstb) in ((None, kvtm), (E["e1"], ptm), (E["e6"], htm), (E["e3"], h2tm)):
                for ec in range(2):
                    for ttl in range(8):
                        b = (ec * 8 + ttl) % 2
                        pb = P.bankb(b)
                        if srcf is None:
                            chx = 2 * lh + ec
                            sap = xn[0:128, chx * T + col0 + ttl * 128: chx * T + col0 + (ttl + 1) * 128]
                            rd = [xn]
                        else:
                            sap = srcf[0:128, ec * 1024 + ttl * 128: ec * 1024 + (ttl + 1) * 128]
                            rd = [srcf]
                        P.op("tensor", lambda e, sap=sap, pb=pb: e.transpose(pb, sap, C.id_bf[:, :]),
                             reads=rd + [C.id_bf], writes=[(P.psumb, b)])
                        P.op("vector", lambda e, dstb=dstb, ttl=ttl, ec=ec, pb=pb:
                             e.tensor_copy(dstb[0:128, ttl * 256 + ec * 128: ttl * 256 + (ec + 1) * 128], pb),
                             reads=[(P.psumb, b)], writes=[dstb])
            for (wncol, lbuf, rbuf, wlcol) in ((1536, kvtm, ptm, 0),
                                               (2048, kvtm, h2tm, 512),
                                               (2560, htm, kvtm, 1024)):
                for dc in range(2):
                    bank = P.bank(4 + dc % 2, 128, 256)

                    def mmw(e, lbuf=lbuf, rbuf=rbuf, dc=dc, bank=bank):
                        last = None
                        for ttl in range(8):
                            last = e.matmul(bank,
                                            lbuf[0:128, ttl * 256 + dc * 128: ttl * 256 + dc * 128 + 128],
                                            rbuf[0:128, ttl * 256: ttl * 256 + 256],
                                            start=(ttl == 0), stop=(ttl == 7))
                        return last
                    P.op("tensor", mmw, reads=[lbuf, rbuf], writes=[(P.psum, 4 + dc % 2)])
                    P.op("vector", lambda e, wncol=wncol, dc=dc, wlcol=wlcol, bank=bank:
                         e.tensor_add(wn[0:128, wncol + dc * 256: wncol + dc * 256 + 256],
                                      P.bank(4 + dc % 2, 128, 256),
                                      wl[0:128, wlcol + dc * 256: wlcol + dc * 256 + 256]),
                         reads=[(P.psum, 4 + dc % 2), wl], writes=[wn])
            for (dst, wncol) in ((E["e1"], 1536), (E["e2"], 2048)):
                for ec in range(2):
                    for cb in range(2):
                        bank = P.bank((ec * 2 + cb) % 4)

                        def mmq2(e, wncol=wncol, ec=ec, cb=cb, bank=bank, kvap=kvap):
                            last = None
                            for kc in range(2):
                                last = e.matmul(bank,
                                                wn[0:128, wncol + kc * 256 + ec * 128: wncol + kc * 256 + ec * 128 + 128],
                                                kvap(kc, cb), start=(kc == 0), stop=(kc == 1))
                            return last
                        P.op("tensor", mmq2, reads=[wn, xn],
                             writes=[(P.psum, (ec * 2 + cb) % 4)])
                        if dst is E["e1"]:
                            P.op("scalar", lambda e, ec=ec, cb=cb, bank=bank:
                                 e.activation(E["e1"][0:128, ec * 1024 + cb * 512: ec * 1024 + (cb + 1) * 512],
                                              bank, AF.Silu),
                                 reads=[(P.psum, (ec * 2 + cb) % 4)], writes=[E["e1"]])
                        else:
                            P.op("scalar", lambda e, ec=ec, cb=cb, bank=bank:
                                 e.copy(E["e2"][0:128, ec * 1024 + cb * 512: ec * 1024 + (cb + 1) * 512], bank),
                                 reads=[(P.psum, (ec * 2 + cb) % 4)], writes=[E["e2"]])
            P.op("vector", lambda e: e.tensor_mul(E["e1"][:, :], E["e1"][:, :], E["e2"][:, :]),
                 reads=[E["e1"], E["e2"]], writes=[E["e1"]])
            for dc in range(2):
                for cb in range(2):
                    bank = P.bank((dc * 2 + cb) % 4)

                    def mmoq(e, dc=dc, cb=cb, bank=bank):
                        last = None
                        for kc in range(2):
                            last = e.matmul(bank,
                                            wn[0:128, 2560 + kc * 256 + dc * 128: 2560 + kc * 256 + dc * 128 + 128],
                                            E["e1"][0:128, kc * 1024 + cb * 512: kc * 1024 + (cb + 1) * 512],
                                            start=(kc == 0), stop=(kc == 1))
                        return last
                    P.op("tensor", mmoq, reads=[wn, E["e1"]],
                         writes=[(P.psum, (dc * 2 + cb) % 4)])
                    ch = 2 * lh + dc
                    xsl = x[0:128, ch * T + col0 + cb * 512: ch * T + col0 + (cb + 1) * 512]
                    P.op("vector", lambda e, xsl=xsl, bank=bank:
                         e.tensor_add(xsl, xsl, bank),
                         reads=[(P.psum, (dc * 2 + cb) % 4),
                                x.r((ch, (col0 + cb * 512) // 512))],
                         writes=[x.r((ch, (col0 + cb * 512) // 512))])



def emit_l1_attn_a(C, x, xn, ak, av, pt, outfm, aqc, dd, wbuf, cscr, mrowb,
                   zbsb, mconst, T):
    mb0, mb1 = mconst

    def band_ap(bi):
        if bi < 4:
            return mb0[0:128, bi * 512:(bi + 1) * 512]
        return mb1[0:128, (bi - 4) * 512:(bi - 4 + 1) * 512]

    def band_ref(bi):
        return mb0 if bi < 4 else mb1

    """dd: qw/kw/vw/ow hosttile(natural, 2) [128, 8192]; ob ([128, 8] cols);
    us ([1,1] unity_scale). mconst: Buf [128, 6*512] bf16: col 0: m4col
    ([128,1] j%4 pattern at col 0); band tiles for delta -384..128 at
    cols 512*(1+i). All masks host-fed constants."""
    P = C.P
    ntile = T // 128
    NQB = T // 512

    P.op("sync", lambda e: e.dma_start(out=mb0[0:128, 0:2048], in_=dd["mc"][0:128, 0:2048]),
         writes=[mb0], dma=True, slot="mb0", fence=[C.f32scr])
    P.op("sync", lambda e: e.dma_start(out=mb1[0:128, 0:1024], in_=dd["mc"][0:128, 2048:3072]),
         writes=[mb1], dma=True, slot="mb1", fence=[C.nscr])
    P.op("sync", lambda e: e.dma_start(out=cscr[0:128, 14:15], in_=dd["m4"]),
         writes=[cscr.r("m4")], dma=True, slot="m4")
    P.op("sync", lambda e: e.dma_start(out=cscr[0:1, 13:14], in_=dd["us"]),
         writes=[cscr.r("usraw")], dma=True, slot="us")
    P.op("scalar", lambda e: e.activation(cscr[0:1, 11:12], cscr[0:1, 13:14],
                                          AF.Sigmoid),
         reads=[cscr.r("usraw")], writes=[cscr.r("us")])
    P.op("tensor", lambda e: e.matmul(P.bank(4, 128, 1), C.ones_row_f32[:, :],
                                      cscr[0:1, 11:12], start=True, stop=True),
         reads=[cscr.r("us"), C.ones_row_f32], writes=[(P.psum, 4)])
    P.op("scalar", lambda e: e.mul(cscr[0:128, 12:13], P.bank(4, 128, 1), 0.125),
         reads=[(P.psum, 4)], writes=[cscr.r("qs")])

    for g in (0, 1):
        base = g * 4096
        P.op("sync", lambda e, g=g, base=base:
             e.dma_start(out=wbuf[0:128, base:base + 4096],
                         in_=dd["kw"][0:128, g * 4096:(g + 1) * 4096]),
             writes=[wbuf.r(("w", g))], dma=True, slot="wA" if g == 0 else "wB")
        for oc in range(4):
            goc = g * 4 + oc
            for tb in range(NQB):
                bk = (goc * NQB + tb) % 4
                bank = P.bank(bk)

                def mmk(e, base=base, oc=oc, tb=tb, bank=bank):
                    last = None
                    for kc in range(8):
                        last = e.matmul(bank,
                                        wbuf[0:128, base + kc * 512 + oc * 128: base + kc * 512 + oc * 128 + 128],
                                        xn[0:128, kc * T + tb * 512: kc * T + (tb + 1) * 512],
                                        start=(kc == 0), stop=(kc == 7))
                    return last
                P.op("tensor", mmk, reads=[wbuf.r(("w", g)), xn], writes=[(P.psum, bk)])
                P.op("scalar", lambda e, goc=goc, tb=tb, bank=bank:
                     e.copy(ak[0:128, goc * T + tb * 512: goc * T + (tb + 1) * 512], bank),
                     reads=[(P.psum, bk)], writes=[ak])
    for g in (0, 1):
        base = g * 4096
        P.op("sync", lambda e, g=g, base=base:
             e.dma_start(out=wbuf[0:128, base:base + 4096],
                         in_=dd["vw"][0:128, g * 4096:(g + 1) * 4096]),
             writes=[wbuf.r(("w", g))], dma=True, slot="wA" if g == 0 else "wB")
        for tt in range(ntile):
            bk = (tt + g) % 4
            bank = P.bank(bk)

            def mmv(e, base=base, tt=tt, bank=bank):
                last = None
                for kc in range(8):
                    last = e.matmul(bank,
                                    xn[0:128, kc * T + tt * 128: kc * T + (tt + 1) * 128],
                                    wbuf[0:128, base + kc * 512: base + (kc + 1) * 512],
                                    start=(kc == 0), stop=(kc == 7))
                return last
            P.op("tensor", mmv, reads=[wbuf.r(("w", g)), xn], writes=[(P.psum, bk)])
            P.op("scalar", lambda e, tt=tt, g=g, bank=bank:
                 e.copy(av[0:128, tt * 1024 + g * 512: tt * 1024 + (g + 1) * 512], bank),
                 reads=[(P.psum, bk)], writes=[av])
    for qb in range(NQB):
        for g in (0, 1):
            base = g * 4096
            P.op("sync", lambda e, g=g, base=base:
                 e.dma_start(out=wbuf[0:128, base:base + 4096],
                             in_=dd["qw"][0:128, g * 4096:(g + 1) * 4096]),
                 writes=[wbuf.r(("w", g))], dma=True, slot="wA" if g == 0 else "wB")
        for c in range(8):
            bank = P.bank(4)
            g = c // 4
            oc = c % 4

            def mmq(e, g=g, oc=oc, qb=qb, bank=bank):
                last = None
                for kc in range(8):
                    last = e.matmul(bank,
                                    wbuf[0:128, g * 4096 + kc * 512 + oc * 128: g * 4096 + kc * 512 + oc * 128 + 128],
                                    xn[0:128, kc * T + qb * 512: kc * T + (qb + 1) * 512],
                                    start=(kc == 0), stop=(kc == 7))
                return last
            P.op("tensor", mmq, reads=[wbuf.r(("w", g)), xn], writes=[(P.psum, 4)])
            P.op("scalar", lambda e, bank=bank:
                 e.activation(aqc[0:128, 0:512], bank, AF.Identity,
                              scale=cscr[0:128, 12:13]),
                 reads=[(P.psum, 4), cscr.r("qs")], writes=[aqc])
            for hp in range(2):
                h = 2 * c + hp
                hb = hp * 64
                for qt in range(4):
                    def mmS(e, c=c, hb=hb, qb=qb, qt=qt):
                        last = None
                        for kb in range(T // 512):
                            last = e.matmul(
                                P.bank(kb % 4, 128, 512),
                                aqc[hb:hb + 64, qt * 128:(qt + 1) * 128],
                                ak[hb:hb + 64, c * T + kb * 512: c * T + (kb + 1) * 512],
                                start=True, stop=True)
                        return last
                    P.op("tensor", mmS, reads=[aqc, ak],
                         writes=[(P.psum, kb) for kb in range(min(4, T // 512))])
                    P.op("vector", lambda e, qt=qt: e.reduce_max(
                        cscr[0:128, qt:qt + 1], P.bank(0, 128, min(2048, T)), axis=AX.X,
                        negate=True),
                         reads=[(P.psum, kb) for kb in range(min(4, T // 512))],
                         writes=[cscr.r(("nm", qt))])
                    P.op("tensor", lambda e, qt=qt: e.transpose(
                        P.bank(4, 1, 128), cscr[0:128, qt:qt + 1], C.id_f32[:, :]),
                         reads=[cscr.r(("nm", qt)), C.id_f32], writes=[(P.psum, 4)])
                    P.op("scalar", lambda e, qt=qt: e.copy(
                        mrowb[0:1, qt * 128:(qt + 1) * 128], P.bank(4, 1, 128)),
                         reads=[(P.psum, 4)], writes=[mrowb])
                for half in range(2):
                    for ki in range(ntile // 2):
                        kt = half * (ntile // 2) + ki
                        bst = 4 + (ki % 2)
                        bank = P.bank(bst)

                        def mmST(e, c=c, hb=hb, kt=kt, qb=qb, bank=bank):
                            e.matmul(bank, C.ones_row_bf[:, :], mrowb[0:1, 0:512],
                                     start=True, stop=False)
                            return e.matmul(bank,
                                            ak[hb:hb + 64, c * T + kt * 128: c * T + (kt + 1) * 128],
                                            aqc[hb:hb + 64, 0:512],
                                            start=False, stop=True)
                        P.op("tensor", mmST, reads=[ak, aqc, mrowb, C.ones_row_bf],
                             writes=[(P.psum, bst)])
                        delta = kt * 128 - qb * 512
                        P.op("scalar", lambda e, ki=ki, bank=bank:
                             e.activation(pt[0:128, ki * 512:(ki + 1) * 512], bank, AF.Exp),
                             reads=[(P.psum, bst)], writes=[pt.r(("p", ki))])
                        if -128 <= delta <= 512:
                            bi_m = (delta + 128) // 128
                            P.op("vector", lambda e, ki=ki, bi_m=bi_m:
                                 e.tensor_mul(pt[0:128, ki * 512:(ki + 1) * 512],
                                              pt[0:128, ki * 512:(ki + 1) * 512],
                                              band_ap(bi_m)),
                                 reads=[pt.r(("p", ki)), band_ref(bi_m)],
                                 writes=[pt.r(("p", ki))])
                        else:
                            P.op("vector", lambda e, ki=ki:
                                 e.tensor_scalar_mul(pt[0:128, ki * 512:(ki + 1) * 512],
                                                     pt[0:128, ki * 512:(ki + 1) * 512],
                                                     cscr[0:128, 14:15]),
                                 reads=[pt.r(("p", ki)), cscr.r("m4")],
                                 writes=[pt.r(("p", ki))])
                    def mmZ(e, half=half):
                        last = None
                        for ki in range(ntile // 2):
                            last = e.matmul(P.bank(3, 1, 512), C.ones_col_bf[:, :],
                                            pt[0:128, ki * 512:(ki + 1) * 512],
                                            start=(half == 0 and ki == 0),
                                            stop=(half == 1 and ki == ntile // 2 - 1))
                        return last
                    P.op("tensor", mmZ,
                         reads=[pt.r(("p", ki)) for ki in range(ntile // 2)] + [C.ones_col_bf],
                         writes=[(P.psum, 3)])

                    def mmO(e, half=half, h=h):
                        last = None
                        for ki in range(ntile // 2):
                            kt = half * (ntile // 2) + ki
                            last = e.matmul(P.bank(2, 64, 512),
                                            av[0:128, kt * 1024 + h * 64: kt * 1024 + h * 64 + 64],
                                            pt[0:128, ki * 512:(ki + 1) * 512],
                                            start=(half == 0 and ki == 0),
                                            stop=(half == 1 and ki == ntile // 2 - 1))
                        return last
                    P.op("tensor", mmO,
                         reads=[pt.r(("p", ki)) for ki in range(ntile // 2)] + [av],
                         writes=[(P.psum, 2)])
                zf = C.rowscr[0:1, 2 * SB:3 * SB]
                P.op("vector", lambda e, zf=zf: e.reciprocal(zf, P.bank(3, 1, 512)),
                     reads=[(P.psum, 3)], writes=[C.rowscr.r("zf")])
                P.op("tensor", lambda e, zf=zf: e.matmul(P.bank(5), C.ones_row_f32[:, :],
                                                         zf, start=True, stop=True),
                     reads=[C.rowscr.r("zf"), C.ones_row_f32], writes=[(P.psum, 5)])
                P.op("scalar", lambda e: e.copy(zbsb[0:128, 0:512], P.bank(5)),
                     reads=[(P.psum, 5)], writes=[zbsb])
                P.op("vector", lambda e, hb=hb, c=c:
                     e.tensor_mul(outfm[hb:hb + 64, c * 512:(c + 1) * 512],
                                  P.bank(2, 64, 512), zbsb[0:64, 0:512]),
                     reads=[(P.psum, 2), zbsb], writes=[outfm.r(("o", c, hb))])
        for hh2 in range(2):
            base2 = hh2 * 4096
            P.op("sync", lambda e, hh2=hh2, base2=base2:
                 e.dma_start(out=wbuf[0:128, base2:base2 + 4096],
                             in_=dd["ow"][0:128, hh2 * 4096:(hh2 + 1) * 4096]),
                 writes=[wbuf.r(("w", hh2))], dma=True, slot="wA" if hh2 == 0 else "wB")
            for oc in range(4):
                goc = hh2 * 4 + oc
                bk = goc % 2
                bank = P.bank(bk)

                def mmop(e, base2=base2, oc=oc, bank=bank):
                    last = None
                    for kc in range(8):
                        last = e.matmul(bank,
                                        wbuf[0:128, base2 + kc * 512 + oc * 128: base2 + kc * 512 + oc * 128 + 128],
                                        outfm[0:128, kc * 512:(kc + 1) * 512],
                                        start=(kc == 0), stop=(kc == 7))
                    return last
                P.op("tensor", mmop,
                     reads=[wbuf.r(("w", hh2))] + [outfm.r(("o", kc, b2)) for kc in range(8) for b2 in (0, 64)],
                     writes=[(P.psum, bk)])
                xsl = x[0:128, goc * T + qb * 512: goc * T + (qb + 1) * 512]
                P.op("vector", lambda e, xsl=xsl, bank=bank, goc=goc:
                     e.scalar_tensor_tensor(xsl, bank, dd["obc"][0:128, goc:goc + 1],
                                            xsl, ALU.add, ALU.add),
                     reads=[(P.psum, bk), x.r((goc, qb)), dd["obc"]],
                     writes=[x.r((goc, qb))])



def emit_l1_ffn(C, x, xn, u1, u2, dd, wbuf, scl, T):
    P = C.P
    NTB = T // 512
    for tb in range(NTB):
        for g in range(8):
            for wi, wname in ((0, "w1"), (1, "w2")):
                wslot = wi
                base = wslot * 4096
                P.op("sync", lambda e, wname=wname, g=g, base=base:
                     e.dma_start(out=wbuf[0:128, base:base + 4096],
                                 in_=dd[wname][0:128, g * 4096:(g + 1) * 4096]),
                     writes=[wbuf.r(("w", wslot))], dma=True,
                     slot="wA" if wslot == 0 else "wB")
                for oc in range(4):
                    goc = g * 4 + oc
                    bk = (goc * 2 + wi) % 4
                    bank = P.bank(bk)

                    def mmw(e, base=base, oc=oc, tb=tb, bank=bank):
                        last = None
                        for kc in range(8):
                            last = e.matmul(bank,
                                            wbuf[0:128, base + kc * 512 + oc * 128: base + kc * 512 + oc * 128 + 128],
                                            xn[0:128, kc * T + tb * 512: kc * T + (tb + 1) * 512],
                                            start=(kc == 0), stop=(kc == 7))
                        return last
                    P.op("tensor", mmw, reads=[wbuf.r(("w", wslot)), xn],
                         writes=[(P.psum, bk)])
                    if wi == 0:
                        P.op("scalar", lambda e, goc=goc, bank=bank:
                             e.activation(u2[0:128, goc * 512:(goc + 1) * 512], bank,
                                          AF.Silu, scale=scl[0:128, goc:goc + 1]),
                             reads=[(P.psum, bk), scl], writes=[u2.r(("u", goc))])
                    else:
                        P.op("vector", lambda e, goc=goc, bank=bank:
                             e.scalar_tensor_tensor(u1[0:128, goc * 512:(goc + 1) * 512],
                                                    bank, scl[0:128, 32 + goc:33 + goc],
                                                    u2[0:128, goc * 512:(goc + 1) * 512],
                                                    ALU.mult, ALU.mult),
                             reads=[(P.psum, bk), scl, u2.r(("u", goc))],
                             writes=[u1.r(("u", goc))])
        for (wname, srcb, dstb, scol) in (("h1", u1, u2, 64), ("h2", u2, u1, 96)):
            for g in range(32):
                wslot = g % 2
                base = wslot * 4096
                P.op("sync", lambda e, wname=wname, g=g, base=base:
                     e.dma_start(out=wbuf[0:128, base:base + 4096],
                                 in_=dd[wname][0:128, g * 4096:(g + 1) * 4096]),
                     writes=[wbuf.r(("w", wslot))], dma=True,
                     slot="wA" if wslot == 0 else "wB")
                bk = g % 4
                bank = P.bank(bk)

                def mmh(e, base=base, bank=bank, srcb=srcb):
                    last = None
                    for kc in range(32):
                        last = e.matmul(bank,
                                        wbuf[0:128, base + kc * 128: base + (kc + 1) * 128],
                                        srcb[0:128, kc * 512:(kc + 1) * 512],
                                        start=(kc == 0), stop=(kc == 31))
                    return last
                P.op("tensor", mmh,
                     reads=[wbuf.r(("w", wslot))] + [srcb.r(("u", kc)) for kc in range(32)],
                     writes=[(P.psum, bk)])
                P.op("scalar", lambda e, g=g, bank=bank, dstb=dstb, scol=scol:
                     e.activation(dstb[0:128, g * 512:(g + 1) * 512], bank, AF.Gelu,
                                  scale=scl[0:128, scol + g:scol + g + 1]),
                     reads=[(P.psum, bk), scl], writes=[dstb.r(("u", g))])
        for g in range(8):
            wslot = g % 2
            base = wslot * 4096
            P.op("sync", lambda e, g=g, base=base:
                 e.dma_start(out=wbuf[0:128, base:base + 4096],
                             in_=dd["w3"][0:128, g * 4096:(g + 1) * 4096]),
                 writes=[wbuf.r(("w", wslot))], dma=True,
                 slot="wA" if wslot == 0 else "wB")
            bk = g % 4
            bank = P.bank(bk)

            def mm3(e, base=base, bank=bank):
                last = None
                for kc in range(32):
                    last = e.matmul(bank,
                                    wbuf[0:128, base + kc * 128: base + (kc + 1) * 128],
                                    u1[0:128, kc * 512:(kc + 1) * 512],
                                    start=(kc == 0), stop=(kc == 31))
                return last
            P.op("tensor", mm3,
                 reads=[wbuf.r(("w", wslot))] + [u1.r(("u", kc)) for kc in range(32)],
                 writes=[(P.psum, bk)])
            xsl = x[0:128, g * T + tb * 512: g * T + (tb + 1) * 512]
            P.op("vector", lambda e, xsl=xsl, bank=bank, g=g:
                 e.scalar_tensor_tensor(xsl, bank, scl[0:128, 128 + g:129 + g],
                                        xsl, ALU.mult, ALU.add),
                 reads=[(P.psum, bk), scl, x.r((g, tb))],
                 writes=[x.r((g, tb))])


def emit_wvg(C, vgw, bvg, dd, wbuf, T):
    P = C.P
    if dd.get("_half", 0) == 0:
        P.op("sync", lambda e: e.dma_start(out=C.f32scr[0:8, 0:1024], in_=dd["bv_row"]),
             writes=[C.f32scr.r("bv")], dma=True, slot="bvrow")
        for half in range(2):
            P.op("tensor", lambda e, half=half: e.matmul(
                P.bank(4, 1, 512), C.ones_col_f32[0:8, 0:1],
                C.f32scr[0:8, half * 512:(half + 1) * 512], start=True, stop=True),
                 reads=[C.f32scr.r("bv"), C.ones_col_f32], writes=[(P.psum, 4)])
            P.op("scalar", lambda e, half=half: e.copy(bvg[0:1, half * 512:(half + 1) * 512],
                                                       P.bank(4, 1, 512)),
                 reads=[(P.psum, 4)], writes=[bvg])
    vh = dd.get("_half", 0)
    first = True
    for kc in range(8):
        for og in range(4 * vh, 4 * vh + 4):
            sl = wbuf[0:128, (og % 2) * 1024:(og % 2) * 1024 + 1024]
            P.op("sync", lambda e, kc=kc, og=og, sl=sl:
                 e.dma_start(out=sl, in_=dd["wv_nat"][kc * 128:(kc + 1) * 128,
                                                      og * 1024:(og + 1) * 1024]),
                 writes=[wbuf.r(("s", og % 2))], dma=True,
                 slot="vga" if og % 2 == 0 else "vgb")
            acc = C.f32scr[0:128, 0:128]
            P.op("vector", lambda e, sl=sl, acc=acc: e.tensor_copy(acc, sl[0:128, 0::8]),
                 reads=[wbuf.r(("s", og % 2))], writes=[C.f32scr.r("vg")],
                 fence=[C.f32scr] if first else [])
            first = False
            for r in range(1, 8):
                P.op("vector", lambda e, sl=sl, acc=acc, r=r:
                     e.tensor_add(acc, acc, sl[0:128, r::8]),
                     reads=[wbuf.r(("s", og % 2)), C.f32scr.r("vg")],
                     writes=[C.f32scr.r("vg")])
            P.op("vector", lambda e, kc=kc, og=og, acc=acc, vh=vh:
                 e.tensor_copy(vgw[0:128, kc * 512 + (og - 4 * vh) * 128:
                                   kc * 512 + (og - 4 * vh + 1) * 128], acc),
                 reads=[C.f32scr.r("vg")], writes=[vgw])


def emit_l1_attn_b(C, x, xn, ak, av, pt, aqc, vgw, bvg, dd, wbuf, bias, cscr,
                   mrowb, zbsb, T):
    P = C.P
    ntile = T // 128
    NQB = T // 512
    QS = 0.08838834764831845

    P.op("scalar", lambda e: e.mul(bias[0:128, 0:8], bias[0:128, 0:8], QS),
         reads=[bias], writes=[bias])
    for vh in range(2):
        dd["_half"] = vh
        emit_wvg(C, vgw, bvg, dd, dd["_vgstage"], T)
        for tt in range(ntile):
            bk = (tt + vh) % 4
            bank = P.bank(bk)

            def mmv(e, tt=tt, vh=vh, bank=bank):
                last = e.matmul(bank, C.ones_row_bf[:, :],
                                bvg[0:1, vh * 512:(vh + 1) * 512],
                                start=True, stop=False)
                for kc in range(8):
                    last = e.matmul(bank,
                                    xn[0:128, kc * T + tt * 128: kc * T + (tt + 1) * 128],
                                    vgw[0:128, kc * 512:(kc + 1) * 512],
                                    start=False, stop=(kc == 7))
                return last
            P.op("tensor", mmv, reads=[xn, vgw, bvg, C.ones_row_bf],
                 writes=[(P.psum, bk)])
            P.op("scalar", lambda e, tt=tt, vh=vh, bank=bank:
                 e.copy(av[0:128, tt * 1024 + vh * 512: tt * 1024 + (vh + 1) * 512], bank),
                 reads=[(P.psum, bk)], writes=[av])
    for g in (0, 1):
        base = g * 4096
        P.op("sync", lambda e, g=g, base=base:
             e.dma_start(out=wbuf[0:128, base:base + 4096],
                         in_=dd["kw"][0:128, g * 4096:(g + 1) * 4096]),
             writes=[wbuf.r(("w", g))], dma=True, slot="wA" if g == 0 else "wB")
        for oc in range(4):
            goc = g * 4 + oc
            for tb in range(NQB):
                bk = (goc * NQB + tb) % 4
                bank = P.bank(bk)

                def mmk(e, base=base, oc=oc, tb=tb, bank=bank):
                    last = None
                    for kc in range(8):
                        last = e.matmul(bank,
                                        wbuf[0:128, base + kc * 512 + oc * 128: base + kc * 512 + oc * 128 + 128],
                                        xn[0:128, kc * T + tb * 512: kc * T + (tb + 1) * 512],
                                        start=(kc == 0), stop=(kc == 7))
                    return last
                P.op("tensor", mmk, reads=[wbuf.r(("w", g)), xn], writes=[(P.psum, bk)])
                P.op("scalar", lambda e, goc=goc, tb=tb, bank=bank:
                     e.activation(ak[0:128, goc * T + tb * 512: goc * T + (tb + 1) * 512],
                                  bank, AF.Identity, bias=bias[0:128, 8 + goc:9 + goc]),
                     reads=[(P.psum, bk), bias], writes=[ak])
    for g in (0, 1):
        base = g * 4096
        P.op("sync", lambda e, g=g, base=base:
             e.dma_start(out=wbuf[0:128, base:base + 4096],
                         in_=dd["qw"][0:128, g * 4096:(g + 1) * 4096]),
             writes=[wbuf.r(("w", g))], dma=True, slot="wA" if g == 0 else "wB")
    for qb in range(NQB):
        for c in range(8):
            bank = P.bank(4)
            g = c // 4
            oc = c % 4

            def mmq(e, g=g, oc=oc, qb=qb, bank=bank):
                last = None
                for kc in range(8):
                    last = e.matmul(bank,
                                    wbuf[0:128, g * 4096 + kc * 512 + oc * 128: g * 4096 + kc * 512 + oc * 128 + 128],
                                    xn[0:128, kc * T + qb * 512: kc * T + (qb + 1) * 512],
                                    start=(kc == 0), stop=(kc == 7))
                return last
            P.op("tensor", mmq, reads=[wbuf.r(("w", g)), xn], writes=[(P.psum, 4)])
            P.op("scalar", lambda e, c=c, bank=bank:
                 e.activation(aqc[0:128, 0:512], bank, AF.Identity,
                              bias=bias[0:128, c:c + 1], scale=QS),
                 reads=[(P.psum, 4), bias], writes=[aqc])
            for qt in range(4):
                def mmS(e, c=c, qb=qb, qt=qt):
                    last = None
                    for kb in range(T // 512):
                        last = e.matmul(
                            P.bank(kb % 4, 128, 512),
                            aqc[0:128, qt * 128:(qt + 1) * 128],
                            ak[0:128, c * T + kb * 512: c * T + (kb + 1) * 512],
                            start=True, stop=True)
                    return last
                P.op("tensor", mmS, reads=[aqc, ak],
                     writes=[(P.psum, kb) for kb in range(min(4, T // 512))])
                P.op("vector", lambda e, qt=qt: e.reduce_max(
                    cscr[0:128, qt:qt + 1], P.bank(0, 128, min(2048, T)), axis=AX.X,
                    negate=True),
                     reads=[(P.psum, kb) for kb in range(min(4, T // 512))],
                     writes=[cscr.r(("nm", qt))])
                P.op("tensor", lambda e, qt=qt: e.transpose(
                    P.bank(4, 1, 128), cscr[0:128, qt:qt + 1], C.id_f32[:, :]),
                     reads=[cscr.r(("nm", qt)), C.id_f32], writes=[(P.psum, 4)])
                P.op("scalar", lambda e, qt=qt: e.copy(
                    mrowb[0:1, qt * 128:(qt + 1) * 128], P.bank(4, 1, 128)),
                     reads=[(P.psum, 4)], writes=[mrowb])
            for half in range(2):
                for ki in range(ntile // 2):
                    kt = half * (ntile // 2) + ki
                    bst = 4 + (ki % 2)
                    bank = P.bank(bst)

                    def mmST(e, c=c, kt=kt, qb=qb, bank=bank):
                        e.matmul(bank, C.ones_row_bf[:, :], mrowb[0:1, 0:512],
                                 start=True, stop=False)
                        return e.matmul(bank,
                                        ak[0:128, c * T + kt * 128: c * T + (kt + 1) * 128],
                                        aqc[0:128, 0:512],
                                        start=False, stop=True)
                    P.op("tensor", mmST, reads=[ak, aqc, mrowb, C.ones_row_bf],
                         writes=[(P.psum, bst)])
                    P.op("scalar", lambda e, ki=ki, bank=bank:
                         e.activation(pt[0:128, ki * 512:(ki + 1) * 512], bank, AF.Exp),
                         reads=[(P.psum, bst)], writes=[pt.r(("p", ki))])
                def mmZ(e, half=half):
                    last = None
                    for ki in range(ntile // 2):
                        last = e.matmul(P.bank(3, 1, 512), C.ones_col_bf[:, :],
                                        pt[0:128, ki * 512:(ki + 1) * 512],
                                        start=(half == 0 and ki == 0),
                                        stop=(half == 1 and ki == ntile // 2 - 1))
                    return last
                P.op("tensor", mmZ,
                     reads=[pt.r(("p", ki)) for ki in range(ntile // 2)] + [C.ones_col_bf],
                     writes=[(P.psum, 3)])

                def mmO(e, half=half, c=c):
                    last = None
                    for ki in range(ntile // 2):
                        kt = half * (ntile // 2) + ki
                        last = e.matmul(P.bank(2, 128, 512),
                                        av[0:128, kt * 1024 + c * 128: kt * 1024 + (c + 1) * 128],
                                        pt[0:128, ki * 512:(ki + 1) * 512],
                                        start=(half == 0 and ki == 0),
                                        stop=(half == 1 and ki == ntile // 2 - 1))
                    return last
                P.op("tensor", mmO,
                     reads=[pt.r(("p", ki)) for ki in range(ntile // 2)] + [av],
                     writes=[(P.psum, 2)])
            zf = C.rowscr[0:1, 2 * SB:3 * SB]
            P.op("vector", lambda e, zf=zf: e.reciprocal(zf, P.bank(3, 1, 512)),
                 reads=[(P.psum, 3)], writes=[C.rowscr.r("zf")])
            P.op("tensor", lambda e, zf=zf: e.matmul(P.bank(5), C.ones_row_f32[:, :],
                                                     zf, start=True, stop=True),
                 reads=[C.rowscr.r("zf"), C.ones_row_f32], writes=[(P.psum, 5)])
            P.op("scalar", lambda e: e.copy(zbsb[0:128, 0:512], P.bank(5)),
                 reads=[(P.psum, 5)], writes=[zbsb])
            P.op("vector", lambda e: e.tensor_mul(aqc[0:128, 512:1024],
                                                  P.bank(2, 128, 512), zbsb[0:128, 0:512]),
                 reads=[(P.psum, 2), zbsb], writes=[aqc.r("og")])
            for j in range(8):
                xsl = x[0:128, j * T + (T // 8) * c + qb * 64: j * T + (T // 8) * c + (qb + 1) * 64]
                P.op("vector", lambda e, xsl=xsl, j=j:
                     e.tensor_add(xsl, xsl, aqc[0:128, 512 + j:1024:8]),
                     reads=[aqc.r("og"), x.r(("sc", j, c, qb))],
                     writes=[x.r(("sc", j, c, qb))])



TS = 2048


def _io_and_base(nc, rowslabs=4):
    xin = nc.dram_tensor("xin", [D, TS], F32, kind="ExternalInput").ap()
    xout = nc.dram_tensor("xout", [D, TS], F32, kind="ExternalOutput").ap()
    P = Prog(nc)
    P.begin()
    C = Ctx(P, rowslabs=rowslabs)
    x = P.sbuf("x", [128, NC * TS], F32)
    xn = P.sbuf("xn", [128, NC * TS], BF16)
    for c in range(NC):
        P.op("sync", lambda e, c=c: e.dma_start(out=x[:, c * TS:(c + 1) * TS],
                                                in_=xin[c * 128:(c + 1) * 128, :]),
             writes=[x], dma=True, slot="xio")
    return P, C, x, xn, xout


def _store_x(P, x, xout):
    for c in range(NC):
        P.op("sync", lambda e, c=c: e.dma_start(out=xout[c * 128:(c + 1) * 128, :],
                                                in_=x[:, c * TS:(c + 1) * TS]),
             reads=[x], dma=True, slot="xout")
    P.barrier()
    P.finish()


def _decl_a0(nc, wc, p=""):
    dd = {k: WView(wc, p + k + "_d") for k in ("wq", "wk", "wv")}
    dd["b"] = nc.dram_tensor(p + "b_d", [128, 24], F32, kind="ExternalInput").ap()
    return dd


def emit_sub_a0(P, C, x, xn, dd, u=""):
    s1 = P.sbuf("s1" + u, [128, NC * TS], BF16)
    s2 = P.sbuf("s2" + u, [128, NC * TS], BF16)
    m4 = P.sbuf("m4" + u, [128, 8192], BF16)
    wbuf = P.sbuf("wbuf" + u, [128, 8192], BF16)
    bias = P.sbuf("bias" + u, [128, 24], F32)
    cscr = P.sbuf("cscr" + u, [128, 32], F32)
    P.op("sync", lambda e: e.dma_start(out=bias[:, :], in_=dd["b"]),
         writes=[bias], dma=True, slot="b0" + u)
    emit_norm(C, x, xn, TS, "plain")
    emit_l0_attn_a(C, x, xn, s1, s2, m4, dd, wbuf, bias, cscr, TS)


def _decl_moe(nc, wc, p=""):
    dd = {
        "gw": WView(wc, p + "gw_d"),
        "gb": nc.dram_tensor(p + "gb_d", [1, 32], F32, kind="ExternalInput").ap(),
        "b2": nc.dram_tensor(p + "b2_d", [32, 1024], F32, kind="ExternalInput").ap(),
        "onehot": nc.dram_tensor(p + "oneh_d", [32, 4096], BF16, kind="ExternalInput").ap(),
        "w1": [WView(wc, p + f"w1_{e}") for e in range(32)],
        "w2": [WView(wc, p + f"w2_{e}") for e in range(32)],
        "b1": [nc.dram_tensor(p + f"b1_{e}", [128, 16], F32, kind="ExternalInput").ap()
               for e in range(32)],
        "gl": nc.dram_tensor(p + "gl_in", [1, D], F32, kind="ExternalInput").ap(),
    }
    return dd


def emit_sub_moe(P, C, x, xn, dd, u=""):
    s1 = P.sbuf("s1" + u, [128, NC * TS], BF16)
    s2 = P.sbuf("s2" + u, [128, NC * TS], BF16)
    wbuf = P.sbuf("wbuf" + u, [128, 8192], BF16)
    gl = P.sbuf("gl" + u, [1, D], F32)
    moescr = {
        "gw": P.sbuf("gwb" + u, [128, 256], BF16),
        "gbrow": P.sbuf("gbrow" + u, [1, 32], F32),
        "gbbc": P.sbuf("gbbc" + u, [128, 32], F32),
        "gbuf": P.sbuf("gbuf" + u, [128, 64], F32),
        "top8": P.sbuf("top8" + u, [128, 16], F32),
        "mwork": P.sbuf("mwork" + u, [128, 64], F32),
        "ctm": P.sbuf("ctm" + u, [128, 64], F32),
        "cfmb": P.sbuf("cfmb" + u, [32, TS], BF16),
        "oneh": P.sbuf("oneh" + u, [32, 128], BF16),
        "cebc": P.sbuf("cebc" + u, [128, 512], BF16),
        "b1b": P.sbuf("b1b" + u, [128, 16], F32),
        "b2b": P.sbuf("b2b" + u, [32, 1024], BF16),
        "vs1": P.sbuf("vs1" + u, [128, 1024], F32),
        "vs2": P.sbuf("vs2" + u, [128, 1024], F32),
    }
    P.op("sync", lambda e: e.dma_start(out=gl[:, :], in_=dd["gl"]),
         writes=[gl], dma=True, slot="glin" + u)
    emit_norm(C, x, xn, TS, "lnrms", gl=gl, u="fold")
    emit_l0_ffn(C, x, xn, s1, s2, dd, wbuf, moescr, TS)


def _decl_ab(nc, wc, p=""):
    dd = {
        "ip": WView(wc, p + "ip_d"),
        "op": WView(wc, p + "op_d"),
        "ibrow_v": nc.dram_tensor(p + "ibv_d", [1, 1024], F32, kind="ExternalInput").ap(),
        "opb": nc.dram_tensor(p + "opb_d", [128, 8], F32, kind="ExternalInput").ap(),
        "bias_qk": nc.dram_tensor(p + "bqk_d", [128, 16], F32, kind="ExternalInput").ap(),
        "wl": [WView(wc, p + f"wl_{h}") for h in range(4)],
        "wln": [WView(wc, p + f"wln_{h}") for h in range(4)],
        "gb": nc.dram_tensor(p + "gb_in", [2, D], F32, kind="ExternalInput").ap(),
    }
    return dd


def emit_sub_ab(P, C, x, xn, dd, u=""):
    ak = P.sbuf("ak" + u, [128, NC * TS], BF16)
    av = P.sbuf("av" + u, [128, (TS // 128) * 1024], BF16)
    pt = P.sbuf("pt" + u, [128, 8 * 512], BF16)
    outfm = P.sbuf("outfm" + u, [128, 8 * 512], BF16)
    aqc = P.sbuf("aqc" + u, [128, 1024], BF16)
    wbuf = P.sbuf("wbuf" + u, [128, 8192], BF16)
    bias = P.sbuf("biasqk" + u, [128, 16], F32)
    cscr = P.sbuf("cscr" + u, [128, 32], F32)
    mrowb = P.sbuf("mrowb" + u, [1, 512], BF16)
    zbsb = P.sbuf("zbsb" + u, [128, 512], BF16)
    gb2 = C.f32scr
    lws = {
        "ibrow": SubBuf(C.nscr, 0, 1024),
        "opb": P.sbuf("opbc" + u, [128, 8], F32),
        "wl": SubBuf(pt, 0, 1536),
        "kvtm": SubBuf(av, 0, 2048),
        "ptm": SubBuf(av, 2048, 2048),
        "htm": SubBuf(av, 4096, 2048),
        "h2tm": SubBuf(av, 6144, 2048),
        "wn": SubBuf(av, 8192, 3072),
    }
    for i, k in enumerate(("e1", "e2", "e3", "e4", "e5", "e6", "e7")):
        lws[k] = SubBuf(ak, i * 2048, 2048)
    P.op("sync", lambda e: e.dma_start(out=gb2[0:2, 0:D], in_=dd["gb"]),
         writes=[gb2], dma=True, slot="gbin" + u)
    P.op("sync", lambda e: e.dma_start(out=bias[:, :], in_=dd["bias_qk"]),
         writes=[bias], dma=True, slot="bqk" + u)
    P.op("sync", lambda e: e.dma_start(out=lws["opb"][:, :], in_=dd["opb"]),
         writes=[lws["opb"]], dma=True, slot="opb" + u)
    emit_norm(C, x, xn, TS, "lnln", gb2=gb2)
    P.op("sync", lambda e: e.dma_start(out=C.f32scr[0:1, 0:1024], in_=dd["ibrow_v"]),
         writes=[C.f32scr], dma=True, slot="ibrow" + u)
    P.op("vector", lambda e: e.tensor_copy(lws["ibrow"][0:1, 0:1024], C.f32scr[0:1, 0:1024]),
         reads=[C.f32scr], writes=[lws["ibrow"]], fence=[C.nscr])
    emit_l0_attn_b(C, x, xn, ak, av, pt, outfm, aqc, dd, wbuf, bias, cscr,
                   mrowb, zbsb, lws, TS)


def _decl_a1(nc, wc, p=""):
    dd = {k: WView(wc, p + k + "_d") for k in ("qw", "kw", "vw", "ow")}
    dd["us"] = nc.dram_tensor(p + "us_d", [1, 1], F32, kind="ExternalInput").ap()
    dd["ob"] = nc.dram_tensor(p + "ob_d", [128, 8], F32, kind="ExternalInput").ap()
    dd["mc"] = nc.dram_tensor(p + "mc_d", [128, 3072], BF16, kind="ExternalInput").ap()
    dd["m4"] = nc.dram_tensor(p + "m4_d", [128, 1], F32, kind="ExternalInput").ap()
    return dd


def emit_sub_a1(P, C, x, xn, dd, u=""):
    ak = P.sbuf("ak" + u, [128, NC * TS], BF16)
    av = P.sbuf("av" + u, [128, (TS // 128) * 1024], BF16)
    pt = P.sbuf("pt" + u, [128, 8 * 512], BF16)
    outfm = P.sbuf("outfm" + u, [128, 8 * 512], BF16)
    aqc = P.sbuf("aqc" + u, [128, 1024], BF16)
    wbuf = P.sbuf("wbuf" + u, [128, 8192], BF16)
    cscr = P.sbuf("cscr" + u, [128, 32], F32)
    mrowb = P.sbuf("mrowb" + u, [1, 512], BF16)
    zbsb = P.sbuf("zbsb" + u, [128, 512], BF16)
    mconst = (SubBuf(C.f32scr, 0, 2048, dtype=BF16), SubBuf(C.nscr, 0, 1024))
    obc = P.sbuf("obc" + u, [128, 8], F32)
    dd = dict(dd)
    dd["obc"] = obc
    P.op("sync", lambda e: e.dma_start(out=obc[:, :], in_=dd["ob"]),
         writes=[obc], dma=True, slot="obc" + u)
    emit_norm(C, x, xn, TS, "plain")
    emit_l1_attn_a(C, x, xn, ak, av, pt, outfm, aqc, dd, wbuf, cscr, mrowb,
                   zbsb, mconst, TS)


def _decl_f1(nc, wc, p=""):
    dd = {k: WView(wc, p + k + "_d") for k in ("w1", "w2", "h1", "h2", "w3")}
    dd["scl"] = nc.dram_tensor(p + "scl_d", [128, 136], F32, kind="ExternalInput").ap()
    return dd


def emit_sub_f1(P, C, x, xn, dd, u=""):
    u1 = P.sbuf("u1" + u, [128, 32 * 512], BF16)
    u2 = P.sbuf("u2" + u, [128, 32 * 512], BF16)
    wbuf = P.sbuf("wbuf" + u, [128, 8192], BF16)
    scl = P.sbuf("scl" + u, [128, 136], F32)
    P.op("sync", lambda e: e.dma_start(out=scl[:, :], in_=dd["scl"]),
         writes=[scl], dma=True, slot="scl" + u)
    emit_norm(C, x, xn, TS, "plain")
    emit_l1_ffn(C, x, xn, u1, u2, dd, wbuf, scl, TS)


def _decl_b1(nc, wc, p=""):
    return {"qw": WView(wc, p + "qw_d"),
            "kw": WView(wc, p + "kw_d"),
            "wv_nat": nc.dram_tensor(p + "wv_d", [1024, 8192], F32, kind="ExternalInput").ap(),
            "bv_row": nc.dram_tensor(p + "bv_d", [8, 1024], F32, kind="ExternalInput").ap(),
            "bias": nc.dram_tensor(p + "bias_d", [128, 16], F32, kind="ExternalInput").ap(),
            "gb": nc.dram_tensor(p + "gb_in", [2, D], F32, kind="ExternalInput").ap()}


def emit_sub_b1(P, C, x, xn, dd, u=""):
    ak = P.sbuf("ak" + u, [128, NC * TS], BF16)
    av = P.sbuf("av" + u, [128, (TS // 128) * 1024], BF16)
    pt = P.sbuf("pt" + u, [128, 8 * 512], BF16)
    aqc = P.sbuf("aqc" + u, [128, 1024], BF16)
    vgw = P.sbuf("vgw" + u, [128, 4096], BF16)
    bvg = SubBuf(aqc, 0, 1024)
    wbuf = P.sbuf("wbuf" + u, [128, 8192], BF16)
    vgstage = SubBuf(ak, 0, 2048, dtype=F32)
    bias = P.sbuf("biasqk" + u, [128, 16], F32)
    cscr = P.sbuf("cscr" + u, [128, 32], F32)
    mrowb = P.sbuf("mrowb" + u, [1, 512], BF16)
    zbsb = P.sbuf("zbsb" + u, [128, 512], BF16)
    gb2 = C.f32scr
    dd = dict(dd)
    dd["_vgstage"] = vgstage
    P.op("sync", lambda e: e.dma_start(out=bias[:, :], in_=dd["bias"]),
         writes=[bias], dma=True, slot="bias" + u)
    P.op("sync", lambda e: e.dma_start(out=gb2[0:2, 0:D], in_=dd["gb"]),
         writes=[gb2], dma=True, slot="gbin" + u)
    emit_norm(C, x, xn, TS, "lnln", gb2=gb2)
    emit_l1_attn_b(C, x, xn, ak, av, pt, aqc, vgw, bvg, dd, wbuf, bias,
                   cscr, mrowb, zbsb, TS)


_DECLS = {"a0": _decl_a0, "moe": _decl_moe, "ab": _decl_ab, "a1": _decl_a1,
          "f1": _decl_f1, "b1": _decl_b1}
_EMITS = {"a0": emit_sub_a0, "moe": emit_sub_moe, "ab": emit_sub_ab,
          "a1": emit_sub_a1, "f1": emit_sub_f1, "b1": emit_sub_b1}


def build_prog_fused():
    nc = bass.Bass("TRN2", target_bir_lowering=False, dynamic_dma_scratch_size=SCRATCH)
    xin = nc.dram_tensor("xin", [2048, TS], F32, kind="ExternalInput").ap()
    xout = nc.dram_tensor("xout", [2048, TS], F32, kind="ExternalOutput").ap()
    wsrc = nc.dram_tensor("wpack", [128, PACK_TOT], F32, kind="ExternalInput").ap()
    wc = WCache(
        nc.dram_tensor("wcacheA", [128, WC_SPLIT], BF16, kind="Internal").ap(),
        nc.dram_tensor("wcacheB", [128, PACK_TOT - WC_SPLIT], BF16, kind="Internal").ap())
    P = Prog(nc)
    P.begin()
    C = Ctx(P, rowslabs=ROWSLABS)
    x = P.sbuf("x", [128, NC * TS], F32)
    xn = P.sbuf("xn", [128, NC * TS], BF16)
    emit_prepass(P, C, wsrc, wc)
    dds = {sub: _DECLS[sub](nc, wc, sub + "_") for sub in set(_SEQUENCE)}
    for b in range(2):
        for c in range(NC):
            P.op("sync", lambda e, c=c, b=b: e.dma_start(
                out=x[:, c * TS:(c + 1) * TS],
                in_=xin[b * 1024 + c * 128: b * 1024 + (c + 1) * 128, :]),
                 writes=[x], dma=True, slot="xio")
        for si, sub in enumerate(_SEQUENCE):
            P.push_scope()
            _EMITS[sub](P, C, x, xn, dds[sub], u=f"_{si}_{b}")
            P.barrier()
            P.pop_scope()
        for c in range(NC):
            P.op("sync", lambda e, c=c, b=b: e.dma_start(
                out=xout[b * 1024 + c * 128: b * 1024 + (c + 1) * 128, :],
                in_=x[:, c * TS:(c + 1) * TS]),
                 reads=[x], dma=True, slot="xout")
    P.barrier()
    P.finish()
    return nc


def _prep_weights(inp):
    w = {}
    w["a0"] = {"wq_d": hosttile(inp["L0_attn_a_Wq"].T, 2),
               "wk_d": hosttile(inp["L0_attn_a_Wk"].T, 2),
               "wv_d": hosttile(inp["L0_attn_a_Wv"].T, 2),
               "b_d": np.concatenate([hostbias(inp["L0_attn_a_bq"]),
                                      hostbias(inp["L0_attn_a_bk"]),
                                      hostbias(inp["L0_attn_a_bv"])], 1)}
    perm = np.concatenate([np.arange(0, 2048, 2), np.arange(1, 2048, 2)])
    oneh = np.zeros((32, 4096), np.float32)
    for e in range(32):
        oneh[e, e * 128:(e + 1) * 128] = 1.0
    moe = {"gw_d": hosttile(inp["L0_ffn_gate_w"].T, 1),
           "gb_d": inp["L0_ffn_gate_b"][None, :],
           "b2_d": inp["L0_ffn_b2"],
           "oneh_d": oneh.astype(BF16NP),
           "gl_in": inp["L0_ffn_norm_scale"][None, :]}
    for e in range(32):
        moe[f"w1_{e}"] = hosttile(np.ascontiguousarray(inp["L0_ffn_w1"][e][perm]).T, 4)
        moe[f"w2_{e}"] = hosttile(inp["L0_ffn_w2"][e].T, 2)
        moe[f"b1_{e}"] = hostbias(inp["L0_ffn_b1"][e][perm])
    w["moe"] = moe
    ipb = inp["L0_attn_b_in_proj_b"]
    ab = {"ip_d": hosttile(inp["L0_attn_b_in_proj_w"].T, 6),
          "op_d": hosttile(inp["L0_attn_b_out_proj_w"].T, 2),
          "ibv_d": ipb[None, 2048:3072],
          "opb_d": hostbias(inp["L0_attn_b_out_proj_b"]),
          "bqk_d": np.concatenate([hostbias(ipb[0:1024]), hostbias(ipb[1024:2048])], 1),
          "gb_in": np.stack([inp["L0_attn_b_ln_w"], inp["L0_attn_b_ln_b"]])}
    for h in range(4):
        ab[f"wl_{h}"] = np.concatenate([hosttile(inp["L0_attn_b_w0"][h].T, 1),
                                        hosttile(inp["L0_attn_b_w2"][h].T, 1),
                                        hosttile(inp["L0_attn_b_w1"][h].T, 1)], 1)
        ab[f"wln_{h}"] = hosttile(inp["L0_attn_b_w1"][h], 1)
    w["ab"] = ab
    mc = np.zeros((128, 3072), np.float32)
    for bi_m, delta in enumerate(range(-128, 513, 128)):
        row = np.arange(128)[:, None]
        col = np.arange(512)[None, :]
        mc[:, 512 * bi_m:512 * (bi_m + 1)] = (
            ((row % 4 == 0) | (np.abs(col - row - delta) <= 8)).astype(np.float32))
    m4col = (np.arange(128)[:, None] % 4 == 0).astype(np.float32)
    w["a1"] = {"qw_d": hosttile(inp["L1_attn_a_q_w"], 2),
               "kw_d": hosttile(inp["L1_attn_a_k_w"], 2),
               "vw_d": hosttile(inp["L1_attn_a_v_w"], 2),
               "ow_d": hosttile(inp["L1_attn_a_o_w"], 2),
               "ob_d": hostbias(inp["L1_attn_a_o_b"]),
               "us_d": np.asarray(inp["L1_attn_a_unity_scale"], np.float32).reshape(1, 1),
               "mc_d": mc.astype(BF16NP), "m4_d": m4col}
    scl = np.zeros((128, 136), np.float32)
    scl[:, 0:32] = hostbias(inp["L1_ffn_w1_s"])
    scl[:, 32:64] = hostbias(inp["L1_ffn_w2_s"])
    scl[:, 64:96] = hostbias(inp["L1_ffn_h1_s"])
    scl[:, 96:128] = hostbias(inp["L1_ffn_h2_s"])
    scl[:, 128:136] = hostbias(inp["L1_ffn_w3_s"])
    w["f1"] = {"w1_d": hosttile(inp["L1_ffn_w1_w"].T, 8),
               "w2_d": hosttile(inp["L1_ffn_w2_w"].T, 8),
               "h1_d": hosttile(inp["L1_ffn_h1_w"].T, 32),
               "h2_d": hosttile(inp["L1_ffn_h2_w"].T, 32),
               "w3_d": hosttile(inp["L1_ffn_w3_w"].T, 8),
               "scl_d": scl}
    bv = inp["L1_attn_b_bv"]
    w["b1"] = {"qw_d": hosttile(inp["L1_attn_b_Wq"], 2),
               "kw_d": hosttile(inp["L1_attn_b_Wk"], 2),
               "wv_d": inp["L1_attn_b_Wv"],
               "bv_d": np.ascontiguousarray(bv.reshape(8, 128, 8).transpose(2, 0, 1).reshape(8, 1024)),
               "bias_d": np.concatenate([hostbias(inp["L1_attn_b_bq"]),
                                         hostbias(inp["L1_attn_b_bk"])], 1),
               "gb_in": np.stack([inp["L1_attn_b_gamma"], inp["L1_attn_b_beta"]])}
    return w


_SEQUENCE = ["a0", "moe", "ab", "moe", "a1", "f1", "b1", "f1"]


def kernel(**inp):
    inp = {k: np.asarray(v) for k, v in inp.items()}
    consts = Ctx.host_consts()
    w = _prep_weights(inp)
    wpack = build_wpack(w)
    x0 = inp["L0_attn_a_hidden_states"].astype(np.float32)
    state = np.ascontiguousarray(x0.transpose(0, 2, 1))
    nc = build_prog_fused()
    wflat = {}
    for sub in set(_SEQUENCE):
        for k, v in w[sub].items():
            wflat[sub + "_" + k] = v
    in_maps = []
    for core in range(8):
        m = dict(wflat)
        m["wpack"] = wpack
        m.update(consts)
        m["xin"] = state[2 * core: 2 * core + 2].reshape(2048, TS)
        in_maps.append(m)
    res = run_bass_kernel_spmd(nc, in_maps, core_ids=list(range(8)))
    for core in range(8):
        state[2 * core: 2 * core + 2] = res.results[core]["xout"].reshape(2, 1024, TS)
    return np.ascontiguousarray(state.transpose(0, 2, 1))


# version 10 (final)
# speedup vs baseline: 1.0445x; 1.0445x over previous
import numpy as np
import ml_dtypes
from contextlib import ExitStack

import concourse.bass as bass
import concourse.mybir as mybir
from concourse.bass_utils import run_bass_kernel_spmd

F32 = mybir.dt.float32
BF16 = mybir.dt.bfloat16
AF = mybir.ActivationFunctionType
ALU = mybir.AluOpType
AX = mybir.AxisListType

BF16NP = ml_dtypes.bfloat16


ENGINES = ("tensor", "vector", "scalar", "gpsimd", "sync")


class Buf:

    __slots__ = ("name", "t", "prog")

    def __init__(self, prog, name, handle):
        self.prog = prog
        self.name = name
        self.t = handle

    def __getitem__(self, idx):
        return self.t[idx]

    def r(self, region):
        return (self, region)


class SubBuf:

    __slots__ = ("buf", "col0", "ncol", "dtype")

    def __init__(self, buf, col0, ncol, dtype=None):
        self.buf = buf
        self.col0 = col0
        self.ncol = ncol
        self.dtype = dtype

    def r(self, tag=None):
        return self

    def __getitem__(self, idx):
        rows, cols = idx
        start = cols.start or 0
        stop = cols.stop if cols.stop is not None else self.ncol
        if self.dtype is None:
            return self.buf.t[rows, self.col0 + start:self.col0 + stop]
        if self.dtype == F32:
            ap = self.buf.t[rows, self.col0 + 2 * start:self.col0 + 2 * stop]
        else:
            ap = self.buf.t[rows, self.col0 + start // 2:self.col0 + stop // 2]
        return ap.bitcast(self.dtype)


class Prog:
    def __init__(self, nc: bass.Bass):
        self.nc = nc
        self.es = ExitStack()
        self.ops = {e: [] for e in ENGINES}
        self.sem_count = {}
        self.vclock = {e: {} for e in ENGINES}
        self.sems = {}
        self.res = {}
        self._sb_bytes = 0
        self._psum = None
        self._started = False
        self._noinc_pending = {e: False for e in ENGINES}

    def begin(self):
        for e in ENGINES:
            self.sems[e] = self.es.enter_context(self.nc.semaphore("sem_" + e))
            self.sem_count[e] = 0
        self._psum = self.es.enter_context(self.nc.psum_tensor("psum_f32", [128, 3072], F32))
        self.psum = Buf(self, "psum_f32", self._psum)
        self._psumb = self.es.enter_context(self.nc.psum_tensor("psum_bf16", [128, 2048], BF16))
        self.psumb = Buf(self, "psum_bf16", self._psumb)
        self._started = True

    def sbuf(self, name, shape, dtype=F32):
        tgt = self._scopes[-1] if getattr(self, "_scopes", None) else self.es
        h = tgt.enter_context(self.nc.sbuf_tensor(name, list(shape), dtype))
        return Buf(self, name, h)

    def push_scope(self):
        if not hasattr(self, "_scopes"):
            self._scopes = []
        self._scopes.append(ExitStack())

    def pop_scope(self):
        self._scopes.pop().close()

    def bank(self, i, rows=128, cols=512):
        assert 0 <= i < 6
        return self._psum[0:rows, 512 * i: 512 * i + cols]

    def bankb(self, i, rows=128, cols=128, off=0):
        assert 0 <= i < 2
        return self._psumb[0:rows, 1024 * i + off: 1024 * i + off + cols]

    def _key(self, ref):
        if isinstance(ref, Buf):
            return (ref.name, None)
        if isinstance(ref, SubBuf):
            return (ref.buf.name, ("sub", ref.col0))
        buf, region = ref
        return (buf.name, region)

    def _conflicting_keys(self, key):
        name, region = key
        out = []
        for k in self.res:
            if k[0] != name:
                continue
            if region is None or k[1] is None or k[1] == region:
                out.append(k)
        return out

    def _need(self, needs, tok):
        if tok is None:
            return
        sem, val = tok
        if needs.get(sem, 0) < val:
            needs[sem] = val

    def op(self, eng, fn, reads=(), writes=(), dma=False, slot=None, inc=True,
           fence=()):
        assert self._started
        reads = list(reads) + list(fence)
        writes = list(writes) + list(fence)
        needs = {}
        rkeys = [self._key(r) for r in reads]
        wkeys = [self._key(w) for w in writes]
        for key in rkeys:
            for k in self._conflicting_keys(key):
                self._need(needs, self.res[k]["w"])
        for key in wkeys:
            for k in self._conflicting_keys(key):
                st = self.res[k]
                self._need(needs, st["w"])
                for sem, val in st["r"].items():
                    self._need(needs, (sem, val))
        vc = self.vclock[eng]
        waits = []
        for sem, val in needs.items():
            if sem.startswith("d:"):
                val = self.sem_count[sem]
            if vc.get(sem, 0) < val:
                waits.append((sem, val))
                vc[sem] = val
        if dma:
            assert slot is not None, "dma ops need a slot (one per physical buffer)"
            q = "d:" + slot
            if q not in self.sems:
                self.sems[q] = self.es.enter_context(self.nc.semaphore("sem_" + q.replace(":", "_")))
                self.sem_count[q] = 0
            self.sem_count[q] += 16
            tok = (q, self.sem_count[q])
            incspec = (q, 16)
        else:
            self.sem_count[eng] += 1
            tok = (eng, self.sem_count[eng])
            incspec = (eng, 1)
        for key in rkeys:
            st = self.res.setdefault(key, {"w": None, "r": {}})
            sem, val = tok
            if st["r"].get(sem, 0) < val:
                st["r"][sem] = val
        for key in wkeys:
            st = self.res.setdefault(key, {"w": None, "r": {}})
            st["w"] = tok
            st["r"] = {}
        self.ops[eng].append((waits, fn, incspec))
        return tok

    def wait_manual(self, eng, tok):
        sem, val = tok
        vc = self.vclock[eng]
        if vc.get(sem, 0) >= val:
            return
        vc[sem] = val
        self.ops[eng].append(([(sem, val)], None, None))

    def barrier(self, engines=ENGINES, dma=True):
        toks = []
        for e in engines:
            toks.append((e, self.sem_count[e]))
        if dma:
            for q in self.sems:
                if q.startswith("d:"):
                    toks.append((q, self.sem_count[q]))
        for e in engines:
            vc = self.vclock[e]
            waits = []
            for sem, val in toks:
                if val > 0 and vc.get(sem, 0) < val:
                    waits.append((sem, val))
                    vc[sem] = val
            if waits:
                self.ops[e].append((waits, None, None))
        if set(engines) == set(ENGINES):
            self.res.clear()

    def fresh(self, buf):
        for k in list(self.res):
            if k[0] == buf.name:
                del self.res[k]

    def drain(self):
        vc = self.vclock["sync"]
        waits = []
        for q, cnt in self.sem_count.items():
            if q.startswith("d:") and cnt > 0 and vc.get(q, 0) < cnt:
                waits.append((q, cnt))
                vc[q] = cnt
        if waits:
            self.ops["sync"].append((waits, None, None))

    def finish(self):
        self.drain()
        nc = self.nc
        block = self.es.enter_context(nc.Block())
        streams = self.ops
        sems = self.sems

        def make_body(eng_name):
            def body(e):
                for waits, fn, incspec in streams[eng_name]:
                    for semname, val in waits:
                        e.wait_ge(sems[semname], val)
                    if fn is not None:
                        inst = fn(e)
                        if incspec is not None:
                            assert inst is not None, f"op on {eng_name} returned no instruction"
                            inst.then_inc(sems[incspec[0]], incspec[1])
            return body

        block.tensor(make_body("tensor"))
        block.vector(make_body("vector"))
        block.scalar(make_body("scalar"))
        block.gpsimd(make_body("gpsimd"))
        block.sync(make_body("sync"))

    def close(self):
        self.es.close()


_DBGON = False
_DBGCNT = [0]
SCRATCH = 8192
ROWSLABS = 4


def dbg_dump(P, tag, buf, cols, dt=None, rows=128):
    if not _DBGON:
        return
    if dt is None:
        dt = mybir.dt.bfloat16
    nc = P.nc
    _DBGCNT[0] += 1
    name = f"dbg_{_DBGCNT[0]:02d}_{tag}"
    P.barrier()
    d = nc.dram_tensor(name, [rows, cols], dt, kind="ExternalOutput").ap()
    nslab = (cols + 2047) // 2048
    for s in range(nslab):
        c0 = s * 2048
        c1 = min(cols, c0 + 2048)
        P.op("sync", lambda e, c0=c0, c1=c1: e.dma_start(
            out=d[0:rows, c0:c1], in_=buf[0:rows, c0:c1]),
             reads=[buf], dma=True, slot="dbgd")
    P.barrier()



PACK_ITEMS = ([("a0_wq_d", 8192), ("a0_wk_d", 8192), ("a0_wv_d", 8192),
               ("moe_gw_d", 256)]
              + [(f"moe_w1_{e}", 16384) for e in range(32)]
              + [(f"moe_w2_{e}", 8192) for e in range(32)]
              + [("ab_ip_d", 24576), ("ab_op_d", 8192)]
              + [(f"ab_wl_{h}", 1536) for h in range(4)]
              + [(f"ab_wln_{h}", 512) for h in range(4)]
              + [("a1_qw_d", 8192), ("a1_kw_d", 8192), ("a1_vw_d", 8192),
                 ("a1_ow_d", 8192)]
              + [("f1_w1_d", 32768), ("f1_w2_d", 32768), ("f1_h1_d", 131072),
                 ("f1_h2_d", 131072), ("f1_w3_d", 32768)]
              + [("b1_qw_d", 8192), ("b1_kw_d", 8192)])
PACK_OFF = {}
_o = 0
for _n, _c in PACK_ITEMS:
    PACK_OFF[_n] = _o
    _o += _c
PACK_TOT = _o


WC_SPLIT = PACK_OFF["moe_w2_0"]


class WCache:

    __slots__ = ("apA", "apB")

    def __init__(self, apA, apB):
        self.apA = apA
        self.apB = apB

    def win(self, off, a, b):
        if off >= WC_SPLIT:
            return self.apB[0:128, off - WC_SPLIT + a: off - WC_SPLIT + b]
        assert off + b <= WC_SPLIT
        return self.apA[0:128, off + a: off + b]


class WView:

    __slots__ = ("wc", "off")

    def __init__(self, wc, name):
        self.wc = wc
        self.off = PACK_OFF[name]

    def __getitem__(self, idx):
        rows, cols = idx
        return self.wc.win(self.off, cols.start or 0, cols.stop)


def build_wpack(w):
    parts = []
    for name, ncols in PACK_ITEMS:
        sub, key = name.split("_", 1)
        arr = np.asarray(w[sub].pop(key), np.float32)
        assert arr.shape == (128, ncols), (name, arr.shape)
        parts.append(arr)
    return np.ascontiguousarray(np.concatenate(parts, axis=1))


def emit_prepass(P, C, wsrc, wc):
    P.push_scope()
    CH = 4096
    NPAR = 4
    stgf = P.sbuf("stgf", [128, NPAR * CH], F32)
    stgb = P.sbuf("stgb", [128, NPAR * CH], BF16)
    ci = 0
    for base, end, ap in ((0, WC_SPLIT, wc.apA), (WC_SPLIT, PACK_TOT, wc.apB)):
        for c0 in range(base, end, CH):
            c1 = min(end, c0 + CH)
            ww = c1 - c0
            par = ci % NPAR
            ci += 1
            fsl = stgf[0:128, par * CH: par * CH + ww]
            bsl = stgb[0:128, par * CH: par * CH + ww]
            P.op("sync", lambda e, fsl=fsl, c0=c0, c1=c1:
                 e.dma_start(out=fsl, in_=wsrc[0:128, c0:c1]),
                 writes=[stgf.r(par)], dma=True, slot="wp" + str(par))
            P.op("vector" if par % 2 == 0 else "gpsimd",
                 lambda e, fsl=fsl, bsl=bsl: e.tensor_copy(bsl, fsl),
                 reads=[stgf.r(par)], writes=[stgb.r(par)])
            P.op("scalar", lambda e, bsl=bsl, ap=ap, a0=c0 - base, a1=c1 - base:
                 e.dma_start(out=ap[0:128, a0:a1], in_=bsl),
                 reads=[stgb.r(par)], dma=True, slot="ws" + str(par))
    P.barrier()
    P.pop_scope()


def clear_scratch(P, bufs):
    engs = ("vector", "gpsimd")
    for i, b in enumerate(bufs):
        P.op(engs[i % 2], lambda e, b=b: e.memset(b.t[:, :], 0.0), writes=[b])


D = 1024
NC = 8
SB = 512
EPS = 1e-5
TH = 1024.0
TH2EPS = TH * TH * EPS
TH2EPS2 = TH * TH * EPS * EPS


def ceil_div(a, b):
    return (a + b - 1) // b


class Ctx:

    def __init__(self, P: Prog, rowslabs=6):
        self.P = P
        nc = P.nc
        self.d_consts = {}
        for name, shape, dt in [
            ("c_id_bf", [128, 128], BF16),
            ("c_id_f32", [128, 128], F32),
            ("c_ones_col_bf", [128, 1], BF16),
            ("c_ones_row_f32", [1, 128], F32),
            ("c_ones_row_bf", [1, 128], BF16),
            ("c_ones_col_f32", [128, 1], F32),
            ("c1024_row_f32", [1, 128], F32),
            ("c_ones_sb_f32", [1, SB], F32),
            ("c_bias_tbl", [128, 8], F32),
        ]:
            self.d_consts[name] = nc.dram_tensor(name, shape, dt, kind="ExternalInput").ap()
        self.id_bf = P.sbuf("id_bf", [128, 128], BF16)
        self.id_f32 = P.sbuf("id_f32", [128, 128], F32)
        self.ones_col_bf = P.sbuf("ones_col_bf", [128, 1], BF16)
        self.ones_row_f32 = P.sbuf("ones_row_f32", [1, 128], F32)
        self.ones_row_bf = P.sbuf("ones_row_bf", [1, 128], BF16)
        self.ones_col_f32 = P.sbuf("ones_col_f32", [128, 1], F32)
        self.c1024_row = P.sbuf("c1024_row", [1, 128], F32)
        self.cbias = P.sbuf("cbias", [128, 8], F32)
        self.row2 = P.sbuf("row2", [2, SB], F32)
        self.rowscr = P.sbuf("rowscr", [1, rowslabs * SB], F32)
        self.nscr = P.sbuf("nscr", [128, 2 * SB], BF16)
        self.f32scr = P.sbuf("f32scr", [128, 2 * SB], F32)
        for sb_buf, dname in [(self.id_bf, "c_id_bf"), (self.id_f32, "c_id_f32"),
                              (self.ones_col_bf, "c_ones_col_bf"),
                              (self.ones_row_f32, "c_ones_row_f32"),
                              (self.ones_row_bf, "c_ones_row_bf"),
                              (self.ones_col_f32, "c_ones_col_f32"),
                              (self.c1024_row, "c1024_row_f32"),
                              (self.cbias, "c_bias_tbl")]:
            dap = self.d_consts[dname]
            P.op("sync", lambda e, o=sb_buf, i=dap: e.dma_start(out=o[:, :], in_=i),
                 writes=[sb_buf], dma=True, slot="const")
        P.op("vector", lambda e: e.memset(self.row2[:, :], 1.0), writes=[self.row2])

    def cb(self, k, p=1):
        return self.cbias[0:p, k:k + 1]

    @staticmethod
    def host_consts():
        return {
            "c_id_bf": np.eye(128, dtype=np.float32).astype(BF16NP),
            "c_id_f32": np.eye(128, dtype=np.float32),
            "c_ones_col_bf": np.ones((128, 1), np.float32).astype(BF16NP),
            "c_ones_row_f32": np.ones((1, 128), np.float32),
            "c_ones_row_bf": np.ones((1, 128), np.float32).astype(BF16NP),
            "c_ones_col_f32": np.ones((128, 1), np.float32),
            "c1024_row_f32": np.full((1, 128), 1024.0, np.float32),
            "c_ones_sb_f32": np.ones((1, SB), np.float32),
            "c_bias_tbl": np.tile(np.array([[TH2EPS, TH2EPS2, 1.0, 0.0,
                                             0.0, 0.0, 0.0, 0.0]], np.float32), (128, 1)),
        }


def emit_norm(C: Ctx, x, xn, T, mode, banks=(0, 1, 2, 3), gl=None, gb2=None,
              u=None, ubanks=(4, 5)):
    P = C.P
    bSS, bS, bA, bB = banks
    nslab = ceil_div(T, SB)
    need_r2 = mode in ("lnln", "lnrms")
    for j in range(nslab):
        t0 = j * SB
        tt = min(SB, T - t0)
        rS = P.bank(bS, 1, tt)
        rSS = P.bank(bSS, 1, tt)
        for c in range(NC):
            xs = x[0:128, c * T + t0: c * T + t0 + tt]
            xb = xn[0:128, c * T + t0: c * T + t0 + tt]
            sq = C.nscr[0:128, (c % 2) * SB:(c % 2) * SB + tt]
            P.op("scalar", lambda e, o=xb, i=xs: e.copy(o, i),
                 reads=[x.r((c, j))], writes=[xn.r((c, j))])
            P.op("vector", lambda e, o=sq, i=xs: e.tensor_mul(o, i, i),
                 reads=[x.r((c, j))], writes=[C.nscr.r(c % 2)])

            def statmm(e, c=c, xb=xb, sq=sq, rS=rS, rSS=rSS):
                e.matmul(rS, C.ones_col_bf[:, :], xb, start=(c == 0), stop=(c == NC - 1))
                return e.matmul(rSS, C.ones_col_bf[:, :], sq, start=(c == 0), stop=(c == NC - 1))
            P.op("tensor", statmm,
                 reads=[xn.r((c, j)), C.nscr.r(c % 2), C.ones_col_bf],
                 writes=[(P.psum, bS), (P.psum, bSS)])
        r_row = C.rowscr[0:1, 0:tt]
        t_row = C.rowscr[0:1, SB:SB + tt]
        r0_row = C.rowscr[0:1, 2 * SB:2 * SB + tt]
        s2_row = C.rowscr[0:1, 3 * SB:3 * SB + tt]
        P.op("scalar", lambda e, o=s2_row, i=rS: e.copy(o, i),
             reads=[(P.psum, bS)], writes=[C.rowscr.r("s2")])
        P.op("vector", lambda e, o=s2_row: e.tensor_mul(o, o, o),
             reads=[C.rowscr.r("s2")], writes=[C.rowscr.r("s2")])
        P.op("vector", lambda e, o=t_row, i=rSS, s2=s2_row:
             e.scalar_tensor_tensor(o, i, TH, s2, ALU.mult, ALU.subtract),
             reads=[(P.psum, bSS), C.rowscr.r("s2")], writes=[C.rowscr.r("t")])
        if need_r2:
            P.op("scalar", lambda e, o=r_row, i=t_row:
                 e.activation(o, i, AF.Sqrt, bias=C.cb(1), scale=(1.0 + EPS)),
                 reads=[C.rowscr.r("t"), C.cbias], writes=[C.rowscr.r("r")])
            P.op("vector", lambda e, o=r_row: e.reciprocal(o, o),
                 reads=[C.rowscr.r("r")], writes=[C.rowscr.r("r")])
        else:
            P.op("scalar", lambda e, o=r_row, i=t_row:
                 e.activation(o, i, AF.Sqrt, bias=C.cb(0), scale=1.0),
                 reads=[C.rowscr.r("t"), C.cbias], writes=[C.rowscr.r("r")])
            P.op("vector", lambda e, o=r_row: e.reciprocal(o, o),
                 reads=[C.rowscr.r("r")], writes=[C.rowscr.r("r")])
        if u is not None:
            P.op("scalar", lambda e, o=r0_row, i=t_row:
                 e.activation(o, i, AF.Sqrt, bias=C.cb(0), scale=1.0),
                 reads=[C.rowscr.r("t"), C.cbias], writes=[C.rowscr.r("r0")])
            P.op("vector", lambda e, o=r0_row: e.reciprocal(o, o),
                 reads=[C.rowscr.r("r0")], writes=[C.rowscr.r("r0")])
        P.op("vector", lambda e, o=C.row2[0:1, 0:tt], i=rS, r=r_row:
             e.scalar_tensor_tensor(o, i, -1.0, r, ALU.mult, ALU.mult),
             reads=[(P.psum, bS), C.rowscr.r("r")], writes=[C.row2.r("b")])
        if u is not None:
            brow0 = C.rowscr[0:1, SB:SB + tt]
            P.op("vector", lambda e, o=brow0, i=rS, r=r0_row:
                 e.scalar_tensor_tensor(o, i, -1.0, r, ALU.mult, ALU.mult),
                 reads=[(P.psum, bS), C.rowscr.r("r0")], writes=[C.rowscr.r("t")])
        for c in range(NC):
            pA = P.bank(bA, 128, tt)
            pB = P.bank(bB, 128, tt)
            if mode == "plain":
                lA = C.c1024_row[:, :]
                lA_reads = [C.c1024_row]
            elif mode == "lnrms":
                lA = gl[0:1, c * 128:(c + 1) * 128]
                lA_reads = [gl]
            else:
                lA = gb2[0:1, c * 128:(c + 1) * 128]
                lA_reads = [gb2]

            def bcmm(e, lA=lA, r=r_row, pA=pA, mode=mode, c=c, pB=pB, tt=tt):
                e.matmul(pA, lA, r, start=True, stop=True)
                if mode == "plain":
                    return e.matmul(pB, C.ones_row_f32[:, :], C.row2[0:1, 0:tt],
                                    start=True, stop=True)
                elif mode == "lnrms":
                    return e.matmul(pB, gl[0:1, c * 128:(c + 1) * 128], C.row2[0:1, 0:tt],
                                    start=True, stop=True)
                else:
                    return e.matmul(pB, gb2[0:2, c * 128:(c + 1) * 128], C.row2[0:2, 0:tt],
                                    start=True, stop=True)
            P.op("tensor", bcmm,
                 reads=lA_reads + [C.rowscr.r("r"), C.row2, C.ones_row_f32],
                 writes=[(P.psum, bA), (P.psum, bB)])
            xb = xn[0:128, c * T + t0: c * T + t0 + tt]
            tmp = C.nscr[0:128, (c % 2) * SB:(c % 2) * SB + tt]
            scaleA = 1.0 if mode == "plain" else TH

            def apply1(e, tmp=tmp, xb=xb, pA=pA, scaleA=scaleA):
                if scaleA == 1.0:
                    return e.tensor_mul(tmp, xb, pA)
                return e.scalar_tensor_tensor(tmp, xb, scaleA, pA, ALU.mult, ALU.mult)
            P.op("vector", apply1,
                 reads=[xn.r((c, j)), (P.psum, bA)], writes=[C.nscr.r(c % 2)])
            P.op("vector", lambda e, o=xb, t=tmp, pB=pB: e.tensor_add(o, t, pB),
                 reads=[C.nscr.r(c % 2), (P.psum, bB)], writes=[xn.r((c, j))])
            if u is not None:
                bA0, bB0 = ubanks
                pA0 = P.bank(bA0, 128, tt)
                pB0 = P.bank(bB0, 128, tt)
                if u == "fold":
                    if c == 0:
                        a0p = C.rowscr[0:1, 3 * SB:3 * SB + tt]
                        P.op("vector", lambda e, o=a0p, i=r0_row:
                             e.tensor_scalar(o, i, TH, 1.0, ALU.mult, ALU.add),
                             reads=[C.rowscr.r("r0")], writes=[C.rowscr.r("s2")])

                def bcmm0(e, r0=r0_row, pA0=pA0, pB0=pB0, tt=tt,
                          brow0=C.rowscr[0:1, SB:SB + tt],
                          a0p=C.rowscr[0:1, 3 * SB:3 * SB + tt], fold=(u == "fold")):
                    if fold:
                        e.matmul(pA0, C.ones_row_f32[:, :], a0p, start=True, stop=True)
                    else:
                        e.matmul(pA0, C.c1024_row[:, :], r0, start=True, stop=True)
                    return e.matmul(pB0, C.ones_row_f32[:, :], brow0, start=True, stop=True)
                P.op("tensor", bcmm0,
                     reads=[C.c1024_row, C.rowscr.r("r0"), C.rowscr.r("t"),
                            C.rowscr.r("s2"), C.ones_row_f32],
                     writes=[(P.psum, bA0), (P.psum, bB0)])
                xs = x[0:128, c * T + t0: c * T + t0 + tt]
                if u == "fold":
                    tmpf = C.f32scr[0:128, (c % 2) * SB:(c % 2) * SB + tt]
                    P.op("vector", lambda e, o=tmpf, i=xs, pA0=pA0: e.tensor_mul(o, i, pA0),
                         reads=[x.r((c, j)), (P.psum, bA0)], writes=[C.f32scr.r(c % 2)])
                    P.op("vector", lambda e, o=xs, t=tmpf, pB0=pB0: e.tensor_add(o, t, pB0),
                         reads=[C.f32scr.r(c % 2), (P.psum, bB0)], writes=[x.r((c, j))])
                else:
                    us = u[0:128, c * T + t0: c * T + t0 + tt]
                    P.op("vector", lambda e, o=us, i=xs, pA0=pA0: e.tensor_mul(o, i, pA0),
                         reads=[x.r((c, j)), (P.psum, bA0)], writes=[u.r((c, j))])
                    P.op("vector", lambda e, o=us, pB0=pB0: e.tensor_add(o, o, pB0),
                         reads=[u.r((c, j)), (P.psum, bB0)], writes=[u.r((c, j))])



def hosttile(M, nsplit=1):
    K, N = M.shape
    assert K % 128 == 0 and N % nsplit == 0
    kc = K // 128
    ng = N // nsplit
    groups = []
    for g in range(nsplit):
        sub = M[:, g * ng:(g + 1) * ng]
        t = sub.reshape(kc, 128, ng).transpose(1, 0, 2)
        groups.append(t.reshape(128, kc * ng))
    return np.ascontiguousarray(np.concatenate(groups, axis=1))


def hostbias(b):
    return np.ascontiguousarray(b.reshape(-1, 128).T)



def emit_transpose_inplace(C, buf, R, Ct):
    P = C.P
    n = R * Ct

    def perm(s):
        r, t = divmod(s, Ct)
        return t * R + r

    def slot_ap(s):
        return buf[0:128, s * 128:(s + 1) * 128]

    bankpick = [0]
    first = [True]

    def tp(s, extra_reads=(), extra_writes=()):
        b = bankpick[0]
        bankpick[0] ^= 1
        pb = P.bankb(b)
        rd = [buf.r(("t", s)), C.id_bf] + list(extra_reads)
        wr = [(P.psumb, b)] + list(extra_writes)
        if first[0]:
            rd.append(buf)
            wr.append(buf)
            first[0] = False
        P.op("tensor", lambda e, i=slot_ap(s), pb=pb: e.transpose(pb, i, C.id_bf[:, :]),
             reads=rd, writes=wr)
        return b

    def evac(bank, dst_slot):
        P.op("vector", lambda e, o=slot_ap(dst_slot), pb=P.bankb(bank): e.tensor_copy(o, pb),
             reads=[(P.psumb, bank)], writes=[buf.r(("t", dst_slot))])

    seen = [False] * n
    for s0 in range(n):
        if seen[s0]:
            continue
        if perm(s0) == s0:
            seen[s0] = True
            b = tp(s0)
            evac(b, s0)
            continue
        cyc = [s0]
        s = perm(s0)
        while s != s0:
            seen[s] = True
            cyc.append(s)
            s = perm(s)
        seen[s0] = True
        hold = tp(cyc[0])
        for i in range(1, len(cyc)):
            nb = tp(cyc[i])
            evac(hold, cyc[i])
            hold = nb
        evac(hold, cyc[0])



def emit_projB(C, src, wdram, wbuf, T, n_out, dst=None, bias=None, bias_col0=0,
               nhalf=2, banks=(0, 1, 2, 3), wslots=("wA", "wB"), evac=None,
               src_col0=0):
    P = C.P
    TBn = T // 512
    ncol = n_out // nhalf
    noc = ncol // 128
    half_cols = 8 * ncol
    bi = 0
    for h in range(nhalf):
        wslot = h % 2
        base = wslot * half_cols
        P.op("sync", lambda e, h=h, base=base:
             e.dma_start(out=wbuf[0:128, base:base + half_cols],
                         in_=wdram[0:128, h * half_cols:(h + 1) * half_cols]),
             writes=[wbuf.r(("w", wslot))], dma=True, slot=wslots[wslot])
        for oc in range(noc):
            goc = h * noc + oc
            for tb in range(TBn):
                bk = banks[bi % len(banks)]
                bi += 1
                bank = P.bank(bk)

                def mm(e, base=base, oc=oc, tb=tb, bank=bank, wslot=wslot):
                    last = None
                    for kc in range(8):
                        last = e.matmul(
                            bank,
                            wbuf[0:128, base + kc * ncol + oc * 128: base + kc * ncol + oc * 128 + 128],
                            src[0:128, src_col0 + kc * T + tb * 512: src_col0 + kc * T + (tb + 1) * 512],
                            start=(kc == 0), stop=(kc == 7))
                    return last
                P.op("tensor", mm, reads=[wbuf.r(("w", wslot)), src],
                     writes=[(P.psum, bk)])
                if evac is not None:
                    evac(goc, tb, bank, bk)
                else:
                    dslice = dst[0:128, goc * T + tb * 512: goc * T + (tb + 1) * 512]
                    if bias is not None:
                        P.op("scalar", lambda e, o=dslice, bank=bank, col=bias_col0 + goc:
                             e.activation(o, bank, AF.Identity, bias=bias[0:128, col:col + 1]),
                             reads=[(P.psum, bk), bias], writes=[dst])
                    else:
                        P.op("scalar", lambda e, o=dslice, bank=bank:
                             e.copy(o, bank),
                             reads=[(P.psum, bk)], writes=[dst])



def emit_softmax_tokens(C, buf, T, cscr, scale2=None):
    P = C.P
    for c in range(8):
        sl = buf[0:128, c * T:(c + 1) * T]
        nm = cscr[0:128, c:c + 1]
        sm = cscr[0:128, 8 + c:9 + c]
        rc = cscr[0:128, 16 + c:17 + c]
        fence = [buf] if c == 0 else []
        P.op("vector", lambda e, o=nm, i=sl: e.reduce_max(o, i, axis=AX.X, negate=True),
             reads=[buf.r(("sm", c))], writes=[cscr.r(("nm", c))], fence=fence)
        P.op("scalar", lambda e, o=sl, i=sl, b=nm, a=sm:
             e.activation(o, i, AF.Exp, bias=b, accum_out=a),
             reads=[buf.r(("sm", c)), cscr.r(("nm", c))],
             writes=[buf.r(("sm", c)), cscr.r(("s", c))])
        P.op("vector", lambda e, o=rc, i=sm: e.reciprocal(o, i),
             reads=[cscr.r(("s", c))], writes=[cscr.r(("rc", c))])
        if scale2 is None:
            P.op("vector", lambda e, o=sl, r=rc: e.tensor_scalar_mul(o, o, r),
                 reads=[buf.r(("sm", c)), cscr.r(("rc", c))], writes=[buf.r(("sm", c))])
        else:
            P.op("vector", lambda e, o=sl, r=rc, s2=scale2:
                 e.tensor_scalar(o, o, r, s2, ALU.mult, ALU.mult),
                 reads=[buf.r(("sm", c)), cscr.r(("rc", c))], writes=[buf.r(("sm", c))])



def emit_l0_attn_a(C, x, xn, s1, s2, m4, dd, wbuf, bias, cscr, T):
    rbc = C.f32scr
    """x f32 [128,8T]; xn bf16 [128,8T] = LN(x) (consumed); s1,s2 bf16 [128,8T]
    arenas; m4 bf16 [128,8192]; dd: dram APs dict (wq,wk,wv tiled, nhalf=2);
    bias: Buf [128,24] (bq|bk|bv); cscr f32 [128,32]; zrow f32 [1, T];
    rbc f32 [128, 512]."""
    P = C.P
    Ct = T // 128
    TBn = T // 512
    emit_projB(C, xn, dd["wk"], wbuf, T, 1024, dst=s1, bias=bias, bias_col0=8)
    emit_softmax_tokens(C, s1, T, cscr, scale2=1.0 / 32.0)
    emit_transpose_inplace(C, s1, 8, Ct)
    emit_projB(C, xn, dd["wv"], wbuf, T, 1024, dst=s2, bias=bias, bias_col0=16)
    emit_softmax_tokens(C, s2, T, cscr)
    emit_transpose_inplace(C, s2, 8, Ct)
    bi = 0
    for oc in range(8):
        for nb in range(2):
            bk = (bi % 4)
            bi += 1
            bank = P.bank(bk)

            def mmM(e, oc=oc, nb=nb, bank=bank):
                last = None
                for tt in range(Ct):
                    last = e.matmul(
                        bank,
                        s1[0:128, (tt * 8 + oc) * 128:(tt * 8 + oc) * 128 + 128],
                        s2[0:128, (tt * 8 + nb * 4) * 128:(tt * 8 + nb * 4) * 128 + 512],
                        start=(tt == 0), stop=(tt == Ct - 1))
                return last
            P.op("tensor", mmM, reads=[s1, s2], writes=[(P.psum, bk)])
            P.op("scalar", lambda e, oc=oc, nb=nb, bank=bank:
                 e.copy(m4[0:128, oc * 1024 + nb * 512: oc * 1024 + (nb + 1) * 512], bank),
                 reads=[(P.psum, bk)], writes=[m4])
    emit_projB(C, xn, dd["wq"], wbuf, T, 1024, dst=s2, bias=bias, bias_col0=0)
    for c in range(8):
        P.op("vector", lambda e, c=c: e.reduce_max(cscr[0:128, c:c + 1],
                                                   s2[0:128, c * T:(c + 1) * T], axis=AX.X),
             reads=[s2], writes=[cscr.r(("gm", c))],
             fence=[s2, cscr] if c == 0 else [])
    P.op("vector", lambda e: e.reduce_max(cscr[0:128, 8:9], cscr[0:128, 0:8], axis=AX.X),
         reads=[cscr.r(("gm", c)) for c in range(8)], writes=[cscr.r("m128")])
    P.op("tensor", lambda e: e.transpose(P.bank(4, 1, 128), cscr[0:128, 8:9], C.id_f32[:, :]),
         reads=[cscr.r("m128"), C.id_f32], writes=[(P.psum, 4)])
    P.op("vector", lambda e: e.reduce_max(cscr[0:1, 9:10], P.bank(4, 1, 128), axis=AX.X),
         reads=[(P.psum, 4)], writes=[cscr.r("gmax")])
    P.op("tensor", lambda e: e.matmul(P.bank(4, 128, 1), C.ones_row_f32[:, :],
                                      cscr[0:1, 9:10], start=True, stop=True),
         reads=[cscr.r("gmax"), C.ones_row_f32], writes=[(P.psum, 4)])
    P.op("scalar", lambda e: e.mul(cscr[0:128, 10:11], P.bank(4, 128, 1), -1.0),
         reads=[(P.psum, 4)], writes=[cscr.r("negmax")])
    for c in range(8):
        P.op("scalar", lambda e, c=c: e.activation(s2[0:128, c * T:(c + 1) * T],
                                                   s2[0:128, c * T:(c + 1) * T],
                                                   AF.Exp, bias=cscr[0:128, 10:11]),
             reads=[s2.r(("e", c)), cscr.r("negmax")], writes=[s2.r(("e", c))],
             fence=[s2] if c == 0 else [])
    bi = 0
    zf = C.rowscr[0:1, 2 * SB:3 * SB]
    for tb in range(TBn):
        def mmZ(e, tb=tb):
            last = None
            for kc in range(8):
                last = e.matmul(P.bank(4, 1, 512), C.ones_col_bf[:, :],
                                s2[0:128, kc * T + tb * 512: kc * T + (tb + 1) * 512],
                                start=(kc == 0), stop=(kc == 7))
            return last
        P.op("tensor", mmZ, reads=[s2, C.ones_col_bf], writes=[(P.psum, 4)])
        P.op("vector", lambda e: e.reciprocal(zf, P.bank(4, 1, 512)),
             reads=[(P.psum, 4)], writes=[C.rowscr.r("zf")])
        P.op("tensor", lambda e: e.matmul(P.bank(5), C.ones_row_f32[:, :],
                                          zf, start=True, stop=True),
             reads=[C.rowscr.r("zf"), C.ones_row_f32], writes=[(P.psum, 5)])
        P.op("scalar", lambda e: e.copy(rbc[0:128, 0:512], P.bank(5)),
             reads=[(P.psum, 5)], writes=[rbc])
        for oc in range(8):
            bk = (bi % 4)
            bi += 1
            bank = P.bank(bk)

            def mmO(e, oc=oc, tb=tb, bank=bank):
                last = None
                for kc in range(8):
                    last = e.matmul(bank,
                                    m4[0:128, kc * 1024 + oc * 128: kc * 1024 + oc * 128 + 128],
                                    s2[0:128, kc * T + tb * 512: kc * T + (tb + 1) * 512],
                                    start=(kc == 0), stop=(kc == 7))
                return last
            P.op("tensor", mmO, reads=[m4, s2], writes=[(P.psum, bk)])
            xsl = x[0:128, oc * T + tb * 512: oc * T + (tb + 1) * 512]
            P.op("vector", lambda e, bank=bank: e.tensor_mul(rbc[0:128, 512:1024], bank, rbc[0:128, 0:512]),
                 reads=[(P.psum, bk), rbc.r("z")], writes=[rbc.r("t")])
            P.op("vector", lambda e, o=xsl: e.tensor_add(o, o, rbc[0:128, 512:1024]),
                 reads=[rbc.r("t"), x.r((oc, tb))], writes=[x.r((oc, tb))])



def emit_l0_ffn(C, x, t, s1, s2, dd, wbuf, moescr, T):
    P = C.P
    TBn = T // 512
    ntile = T // 128
    gw = moescr["gw"]; gbbc = moescr["gbbc"]; gbuf = moescr["gbuf"]
    top8 = moescr["top8"]; mwork = moescr["mwork"]; ctm = moescr["ctm"]
    cfmb = moescr["cfmb"]; oneh = moescr["oneh"]
    cebc = moescr["cebc"]; b1b = moescr["b1b"]; b2b = moescr["b2b"]
    vs1 = moescr["vs1"]; vs2 = moescr["vs2"]

    P.op("sync", lambda e: e.dma_start(out=gw[:, :], in_=dd["gw"][0:128, 0:256]),
         writes=[gw], dma=True, slot="gw")
    P.op("sync", lambda e: e.dma_start(out=vs1[0:32, 0:1024], in_=dd["b2"]),
         writes=[vs1], dma=True, slot="b2")
    P.op("vector", lambda e: e.tensor_copy(b2b[0:32, 0:1024], vs1[0:32, 0:1024]),
         reads=[vs1], writes=[b2b])

    P.op("sync", lambda e: e.dma_start(out=moescr["gbrow"][:, :], in_=dd["gb"]),
         writes=[moescr["gbrow"]], dma=True, slot="gbrow")
    P.op("tensor", lambda e: e.matmul(P.bank(4, 128, 32), C.ones_row_f32[:, :],
                                      moescr["gbrow"][0:1, :], start=True, stop=True),
         reads=[moescr["gbrow"], C.ones_row_f32], writes=[(P.psum, 4)])
    P.op("scalar", lambda e: e.copy(gbbc[:, :], P.bank(4, 128, 32)),
         reads=[(P.psum, 4)], writes=[gbbc])

    for tt in range(ntile):
        par = tt % 2
        def mmg(e, tt=tt):
            last = None
            for kc in range(8):
                last = e.matmul(P.bank(4, 128, 32),
                                t[0:128, kc * T + tt * 128: kc * T + (tt + 1) * 128],
                                gw[0:128, kc * 32:(kc + 1) * 32],
                                start=(kc == 0), stop=(kc == 7))
            return last
        P.op("tensor", mmg, reads=[t, gw], writes=[(P.psum, 4)])
        P.op("vector", lambda e, par=par: e.tensor_add(gbuf[0:128, par * 32:(par + 1) * 32],
                                                       P.bank(4, 128, 32), gbbc[:, :]),
             reads=[(P.psum, 4), gbbc], writes=[gbuf.r(("g", par))])
        gsl = gbuf[0:128, par * 32:(par + 1) * 32]
        P.op("vector", lambda e, gsl=gsl: e.max(top8[:, 0:8], gsl),
             reads=[gbuf.r(("g", par))], writes=[top8])
        P.op("scalar", lambda e: e.mul(top8[:, 8:9], top8[:, 0:1], -1.0),
             reads=[top8], writes=[top8.r("nm")])
        P.op("vector", lambda e, gsl=gsl: e.tensor_single_scalar(mwork[:, 0:32], gsl,
                                                                 top8[:, 3:4], ALU.is_ge),
             reads=[gbuf.r(("g", par)), top8], writes=[mwork.r("m")])
        P.op("scalar", lambda e, gsl=gsl: e.activation(mwork[:, 32:64], gsl, AF.Exp,
                                                       bias=top8[:, 8:9]),
             reads=[gbuf.r(("g", par)), top8.r("nm")], writes=[mwork.r("e")])
        P.op("vector", lambda e: e.tensor_mul(mwork[:, 32:64], mwork[:, 32:64], mwork[:, 0:32]),
             reads=[mwork.r("m"), mwork.r("e")], writes=[mwork.r("e")])
        P.op("vector", lambda e: e.reduce_sum(top8[:, 9:10], mwork[:, 32:64], axis=AX.X),
             reads=[mwork.r("e")], writes=[top8.r("z")])
        P.op("vector", lambda e: e.reciprocal(top8[:, 10:11], top8[:, 9:10]),
             reads=[top8.r("z")], writes=[top8.r("rz")])
        P.op("vector", lambda e, par=par: e.tensor_scalar_mul(ctm[0:128, par * 32:(par + 1) * 32],
                                                              mwork[:, 32:64], top8[:, 10:11]),
             reads=[mwork.r("e"), top8.r("rz")], writes=[ctm.r(("c", par))])
        P.op("tensor", lambda e, par=par: e.transpose(P.bank(5, 32, 128),
                                                      ctm[0:128, par * 32:(par + 1) * 32],
                                                      C.id_f32[:, :]),
             reads=[ctm.r(("c", par)), C.id_f32], writes=[(P.psum, 5)])
        P.op("scalar", lambda e, tt=tt: e.copy(cfmb[0:32, tt * 128:(tt + 1) * 128],
                                               P.bank(5, 32, 128)),
             reads=[(P.psum, 5)], writes=[cfmb])

    bi = 0
    for ei in range(32):
        P.op("sync", lambda e, ei=ei: e.dma_start(out=b1b[:, :], in_=dd["b1"][ei]),
             writes=[b1b], dma=True, slot="b1b")
        P.op("sync", lambda e, ei=ei: e.dma_start(
            out=oneh[0:32, 0:128], in_=dd["onehot"][0:32, ei * 128:(ei + 1) * 128]),
             writes=[oneh], dma=True, slot="oneh")

        for q in range(4):
            wslot = q % 2
            base = wslot * 4096
            P.op("sync", lambda e, ei=ei, q=q, base=base:
                 e.dma_start(out=wbuf[0:128, base:base + 4096],
                             in_=dd["w1"][ei][0:128, q * 4096:(q + 1) * 4096]),
                 writes=[wbuf.r(("w", wslot))], dma=True,
                 slot="wA" if wslot == 0 else "wB")
            if q < 2:
                octb = [(oc, tb) for oc in range(4) for tb in range(TBn)]
            else:
                octb = [(oc, tb) for tb in range(TBn) for oc in range(4)]
            for oc, tb in octb:
                if True:
                    goc = q * 4 + oc
                    bk = bi % 4
                    bi += 1
                    bank = P.bank(bk)

                    def mm1(e, base=base, oc=oc, tb=tb, bank=bank):
                        last = None
                        for kc in range(8):
                            last = e.matmul(
                                bank,
                                wbuf[0:128, base + kc * 512 + oc * 128: base + kc * 512 + oc * 128 + 128],
                                t[0:128, kc * T + tb * 512: kc * T + (tb + 1) * 512],
                                start=(kc == 0), stop=(kc == 7))
                        return last
                    P.op("tensor", mm1, reads=[wbuf.r(("w", wslot)), t],
                         writes=[(P.psum, bk)])
                    if goc < 8:
                        P.op("vector", lambda e, goc=goc, tb=tb, bank=bank:
                             e.tensor_scalar(s1[0:128, goc * T + tb * 512: goc * T + (tb + 1) * 512],
                                             bank, b1b[0:128, goc:goc + 1], 7.0,
                                             ALU.add, ALU.min),
                             reads=[(P.psum, bk), b1b], writes=[s1.r(("g", goc, tb))])
                    else:
                        c = goc - 8
                        if oc == 0:
                            P.op("tensor", lambda e, tb=tb:
                                 e.matmul(P.bank(4), oneh[0:32, 0:128],
                                          cfmb[0:32, tb * 512:(tb + 1) * 512],
                                          start=True, stop=True),
                                 reads=[oneh, cfmb], writes=[(P.psum, 4)])
                            P.op("scalar", lambda e: e.mul(cebc[0:128, 0:512],
                                                           P.bank(4), 1.0 / 1.702),
                                 reads=[(P.psum, 4)], writes=[cebc.r(("ce",))])
                        gsl = s1[0:128, c * T + tb * 512: c * T + (tb + 1) * 512]
                        par = (c * TBn + tb) % 2
                        v1 = vs1[0:128, par * 512:(par + 1) * 512]
                        v2 = vs2[0:128, par * 512:(par + 1) * 512]
                        P.op("scalar", lambda e, gsl=gsl, v1=v1:
                             e.activation(v1, gsl, AF.Silu, scale=1.702),
                             reads=[s1.r(("g", c, tb))], writes=[vs1.r(("a", par))])
                        P.op("vector", lambda e, c=c, bank=bank, v2=v2:
                             e.tensor_scalar(v2, bank, b1b[0:128, 8 + c:9 + c], 7.0,
                                             ALU.add, ALU.min),
                             reads=[(P.psum, bk), b1b], writes=[vs2.r(("b", par))])
                        P.op("vector", lambda e, v2=v2:
                             e.tensor_scalar(v2, v2, -7.0, 1.0, ALU.max, ALU.add),
                             reads=[vs2.r(("b", par))], writes=[vs2.r(("b", par))])
                        P.op("vector", lambda e, v1=v1, v2=v2:
                             e.tensor_mul(v1, v1, v2),
                             reads=[vs1.r(("a", par)), vs2.r(("b", par))], writes=[vs1.r(("a", par))])
                        P.op("vector", lambda e, c=c, tb=tb, v1=v1:
                             e.tensor_mul(s2[0:128, c * T + tb * 512: c * T + (tb + 1) * 512],
                                          v1, cebc[0:128, 0:512]),
                             reads=[vs1.r(("a", par)), cebc.r(("ce",))],
                             writes=[s2.r(("h", c, tb))])
        for hh in range(2):
            wslot = hh % 2
            base = wslot * 4096
            P.op("sync", lambda e, ei=ei, hh=hh, base=base:
                 e.dma_start(out=wbuf[0:128, base:base + 4096],
                             in_=dd["w2"][ei][0:128, hh * 4096:(hh + 1) * 4096]),
                 writes=[wbuf.r(("w", wslot))], dma=True,
                 slot="wA" if wslot == 0 else "wB")
            for oc in range(4):
                goc = hh * 4 + oc
                for tb in range(TBn):
                    bk = bi % 4
                    bi += 1
                    bank = P.bank(bk)

                    def mm2(e, base=base, oc=oc, tb=tb, bank=bank, ei=ei, goc=goc):
                        first = True
                        last = None
                        if ei == 0:
                            last = e.matmul(bank, b2b[0:32, goc * 128:(goc + 1) * 128],
                                            cfmb[0:32, tb * 512:(tb + 1) * 512],
                                            start=True, stop=False)
                            first = False
                        for kc in range(8):
                            last = e.matmul(
                                bank,
                                wbuf[0:128, base + kc * 512 + oc * 128: base + kc * 512 + oc * 128 + 128],
                                s2[0:128, kc * T + tb * 512: kc * T + (tb + 1) * 512],
                                start=first, stop=(kc == 7))
                            first = False
                        return last
                    rds = [wbuf.r(("w", wslot))] + [s2.r(("h", kc, tb)) for kc in range(8)]
                    if ei == 0:
                        rds += [b2b, cfmb]
                    P.op("tensor", mm2, reads=rds, writes=[(P.psum, bk)])
                    xsl = x[0:128, goc * T + tb * 512: goc * T + (tb + 1) * 512]
                    P.op("vector", lambda e, xsl=xsl, bank=bank:
                         e.tensor_add(xsl, xsl, bank),
                         reads=[(P.psum, bk), x.r((goc, tb))],
                         writes=[x.r((goc, tb))])



def emit_l0_attn_b(C, x, xn, ak, av, pt, outfm, aqc, dd, wbuf, bias, cscr,
                   mrowb, zbsb, lws, T, parts=("mha", "lact")):
    P = C.P
    ntile = T // 128
    NQB = T // 512
    nch = T // 1024

    def ptref(ki):
        return pt.r(("p", ki)) if isinstance(pt, Buf) else pt

    P.op("scalar", lambda e: e.mul(bias[0:128, 0:8], bias[0:128, 0:8], 0.125),
         reads=[bias], writes=[bias])

    def ipgroup(g):
        return dd["ip"][0:128, g * 4096:(g + 1) * 4096]

    for g in (2, 3):
        wslot = g % 2
        base = wslot * 4096
        P.op("sync", lambda e, g=g, base=base:
             e.dma_start(out=wbuf[0:128, base:base + 4096], in_=ipgroup(g)),
             writes=[wbuf.r(("w", wslot))], dma=True, slot="wA" if wslot == 0 else "wB")
        for oc in range(4):
            goc = (g - 2) * 4 + oc
            for tb in range(T // 512):
                bk = (goc * (T // 512) + tb) % 4
                bank = P.bank(bk)

                def mmk(e, base=base, oc=oc, tb=tb, bank=bank):
                    last = None
                    for kc in range(8):
                        last = e.matmul(bank,
                                        wbuf[0:128, base + kc * 512 + oc * 128: base + kc * 512 + oc * 128 + 128],
                                        xn[0:128, kc * T + tb * 512: kc * T + (tb + 1) * 512],
                                        start=(kc == 0), stop=(kc == 7))
                    return last
                P.op("tensor", mmk, reads=[wbuf.r(("w", wslot)), xn], writes=[(P.psum, bk)])
                P.op("scalar", lambda e, goc=goc, tb=tb, bank=bank:
                     e.activation(ak[0:128, goc * T + tb * 512: goc * T + (tb + 1) * 512],
                                  bank, AF.Identity, bias=bias[0:128, 8 + goc:9 + goc]),
                     reads=[(P.psum, bk), bias], writes=[ak])
    for g in (4, 5):
        wslot = g % 2
        base = wslot * 4096
        P.op("sync", lambda e, g=g, base=base:
             e.dma_start(out=wbuf[0:128, base:base + 4096], in_=ipgroup(g)),
             writes=[wbuf.r(("w", wslot))], dma=True, slot="wA" if wslot == 0 else "wB")
        vh = g - 4
        for tt in range(ntile):
            bk = (tt + vh) % 4
            bank = P.bank(bk)

            def mmv(e, base=base, tt=tt, bank=bank, vh=vh):
                last = e.matmul(bank, C.ones_row_bf[:, :],
                                lws["ibrow"][0:1, vh * 512:(vh + 1) * 512],
                                start=True, stop=False)
                for kc in range(8):
                    last = e.matmul(bank,
                                    xn[0:128, kc * T + tt * 128: kc * T + (tt + 1) * 128],
                                    wbuf[0:128, base + kc * 512: base + (kc + 1) * 512],
                                    start=False, stop=(kc == 7))
                return last
            P.op("tensor", mmv, reads=[wbuf.r(("w", wslot)), xn, lws["ibrow"],
                                       C.ones_row_bf], writes=[(P.psum, bk)])
            P.op("scalar", lambda e, tt=tt, vh=vh, bank=bank:
                 e.copy(av[0:128, tt * 1024 + vh * 512: tt * 1024 + (vh + 1) * 512], bank),
                 reads=[(P.psum, bk)], writes=[av])
    for qb in range(NQB if "mha" in parts else 0):
        for g in (0, 1):
            base = g * 4096
            P.op("sync", lambda e, g=g, base=base:
                 e.dma_start(out=wbuf[0:128, base:base + 4096], in_=ipgroup(g)),
                 writes=[wbuf.r(("w", g))], dma=True, slot="wA" if g == 0 else "wB")
        for c in range(8):
            bank = P.bank(4)
            g = c // 4
            oc = c % 4

            def mmq(e, g=g, oc=oc, qb=qb, bank=bank):
                last = None
                for kc in range(8):
                    last = e.matmul(bank,
                                    wbuf[0:128, g * 4096 + kc * 512 + oc * 128: g * 4096 + kc * 512 + oc * 128 + 128],
                                    xn[0:128, kc * T + qb * 512: kc * T + (qb + 1) * 512],
                                    start=(kc == 0), stop=(kc == 7))
                return last
            P.op("tensor", mmq, reads=[wbuf.r(("w", g)), xn], writes=[(P.psum, 4)])
            P.op("scalar", lambda e, c=c, bank=bank:
                 e.activation(aqc[0:128, 0:512], bank, AF.Identity,
                              bias=bias[0:128, c:c + 1], scale=0.125),
                 reads=[(P.psum, 4), bias], writes=[aqc])
            for hp in range(2):
                h = 2 * c + hp
                hb = hp * 64
                for qt in range(4):
                    def mmS(e, c=c, hb=hb, qb=qb, qt=qt):
                        last = None
                        for kb in range(T // 512):
                            last = e.matmul(
                                P.bank(kb % 4, 128, 512),
                                aqc[hb:hb + 64, qt * 128:(qt + 1) * 128],
                                ak[hb:hb + 64, c * T + kb * 512: c * T + (kb + 1) * 512],
                                start=True, stop=True)
                        return last
                    P.op("tensor", mmS, reads=[aqc, ak],
                         writes=[(P.psum, kb) for kb in range(min(4, T // 512))])
                    P.op("vector", lambda e, qt=qt: e.reduce_max(
                        cscr[0:128, qt:qt + 1], P.bank(0, 128, min(2048, T)), axis=AX.X,
                        negate=True),
                         reads=[(P.psum, kb) for kb in range(min(4, T // 512))],
                         writes=[cscr.r(("nm", qt))])
                    P.op("tensor", lambda e, qt=qt: e.transpose(
                        P.bank(4, 1, 128), cscr[0:128, qt:qt + 1], C.id_f32[:, :]),
                         reads=[cscr.r(("nm", qt)), C.id_f32], writes=[(P.psum, 4)])
                    P.op("scalar", lambda e, qt=qt: e.copy(
                        mrowb[0:1, qt * 128:(qt + 1) * 128], P.bank(4, 1, 128)),
                         reads=[(P.psum, 4)], writes=[mrowb])
                for half in range(2):
                    for ki in range(ntile // 2):
                        kt = half * (ntile // 2) + ki
                        bst = 4 + (ki % 2)
                        bank = P.bank(bst)

                        def mmST(e, c=c, hb=hb, kt=kt, qb=qb, bank=bank):
                            e.matmul(bank, C.ones_row_bf[:, :], mrowb[0:1, 0:512],
                                     start=True, stop=False)
                            return e.matmul(bank,
                                            ak[hb:hb + 64, c * T + kt * 128: c * T + (kt + 1) * 128],
                                            aqc[hb:hb + 64, 0:512],
                                            start=False, stop=True)
                        P.op("tensor", mmST, reads=[ak, aqc, mrowb, C.ones_row_bf],
                             writes=[(P.psum, bst)])
                        P.op("scalar", lambda e, ki=ki, bank=bank:
                             e.activation(pt[0:128, ki * 512:(ki + 1) * 512], bank, AF.Exp),
                             reads=[(P.psum, bst)], writes=[ptref(ki)])
                    def mmZ(e, half=half):
                        last = None
                        for ki in range(ntile // 2):
                            last = e.matmul(P.bank(3, 1, 512), C.ones_col_bf[:, :],
                                            pt[0:128, ki * 512:(ki + 1) * 512],
                                            start=(half == 0 and ki == 0),
                                            stop=(half == 1 and ki == ntile // 2 - 1))
                        return last
                    P.op("tensor", mmZ,
                         reads=[ptref(ki) for ki in range(ntile // 2)] + [C.ones_col_bf],
                         writes=[(P.psum, 3)])

                    def mmO(e, half=half, h=h):
                        last = None
                        for ki in range(ntile // 2):
                            kt = half * (ntile // 2) + ki
                            last = e.matmul(P.bank(2, 64, 512),
                                            av[0:128, kt * 1024 + h * 64: kt * 1024 + h * 64 + 64],
                                            pt[0:128, ki * 512:(ki + 1) * 512],
                                            start=(half == 0 and ki == 0),
                                            stop=(half == 1 and ki == ntile // 2 - 1))
                        return last
                    P.op("tensor", mmO,
                         reads=[ptref(ki) for ki in range(ntile // 2)] + [av],
                         writes=[(P.psum, 2)])
                zf = C.rowscr[0:1, 2 * SB:3 * SB]
                P.op("vector", lambda e, zf=zf: e.reciprocal(zf, P.bank(3, 1, 512)),
                     reads=[(P.psum, 3)], writes=[C.rowscr.r("zf")])
                P.op("tensor", lambda e, zf=zf: e.matmul(P.bank(5), C.ones_row_f32[:, :],
                                                         zf, start=True, stop=True),
                     reads=[C.rowscr.r("zf"), C.ones_row_f32], writes=[(P.psum, 5)])
                P.op("scalar", lambda e: e.copy(zbsb[0:128, 0:512], P.bank(5)),
                     reads=[(P.psum, 5)], writes=[zbsb])
                P.op("vector", lambda e, hb=hb, c=c:
                     e.tensor_mul(outfm[hb:hb + 64, c * 512:(c + 1) * 512],
                                  P.bank(2, 64, 512), zbsb[0:64, 0:512]),
                     reads=[(P.psum, 2), zbsb], writes=[outfm.r(("o", c, hb))])
        for hh2 in range(2):
            base2 = hh2 * 4096
            P.op("sync", lambda e, hh2=hh2, base2=base2:
                 e.dma_start(out=wbuf[0:128, base2:base2 + 4096],
                             in_=dd["op"][0:128, hh2 * 4096:(hh2 + 1) * 4096]),
                 writes=[wbuf.r(("w", hh2))], dma=True, slot="wA" if hh2 == 0 else "wB")
            for oc in range(4):
                goc = hh2 * 4 + oc
                bk = goc % 2
                bank = P.bank(bk)

                def mmop(e, base2=base2, oc=oc, bank=bank):
                    last = None
                    for kc in range(8):
                        last = e.matmul(bank,
                                        wbuf[0:128, base2 + kc * 512 + oc * 128: base2 + kc * 512 + oc * 128 + 128],
                                        outfm[0:128, kc * 512:(kc + 1) * 512],
                                        start=(kc == 0), stop=(kc == 7))
                    return last
                P.op("tensor", mmop,
                     reads=[wbuf.r(("w", hh2))] + [outfm.r(("o", kc, b2)) for kc in range(8) for b2 in (0, 64)],
                     writes=[(P.psum, bk)])
                xsl = x[0:128, goc * T + qb * 512: goc * T + (qb + 1) * 512]
                P.op("vector", lambda e, xsl=xsl, bank=bank, goc=goc:
                     e.scalar_tensor_tensor(xsl, bank, lws["opb"][0:128, goc:goc + 1],
                                            xsl, ALU.add, ALU.add),
                     reads=[(P.psum, bk), x.r((goc, qb)), lws["opb"]],
                     writes=[x.r((goc, qb))])

    wl = lws["wl"]; wn = lws["wn"]
    kvtm = lws["kvtm"]; ptm = lws["ptm"]; htm = lws["htm"]; h2tm = lws["h2tm"]
    for lh in (range(4) if "lact" in parts else range(0)):
        P.op("sync", lambda e, lh=lh: e.dma_start(out=wl[0:128, 0:1536],
                                                    in_=dd["wl"][lh][0:128, 0:1536]),
             writes=[wl], dma=True, slot="wl",
             fence=[ak, av, pt] if lh == 0 else [])
        P.op("sync", lambda e, lh=lh: e.dma_start(out=wn[0:128, 1024:1536],
                                                    in_=dd["wln"][lh][0:128, 0:512]),
             writes=[wn], dma=True, slot="wln")
        P.op("vector", lambda e: e.tensor_copy(wn[0:128, 0:1024], wl[0:128, 0:1024]),
             reads=[wl], writes=[wn])
        for n in range(nch):
            col0 = n * 1024
            ch0, ch1 = 2 * lh, 2 * lh + 1

            def kvap(kc, cb=None, lh=lh, col0=col0):
                ch = 2 * lh + kc
                if cb is None:
                    return xn[0:128, ch * T + col0: ch * T + col0 + 1024]
                return xn[0:128, ch * T + col0 + cb * 512: ch * T + col0 + (cb + 1) * 512]

            for dst, wbase in ((lws["e1"], 0), (lws["e2"], 512)):
                for ec in range(2):
                    for cb in range(2):
                        bank = P.bank((ec * 2 + cb) % 4)

                        def mmg(e, wbase=wbase, ec=ec, cb=cb, bank=bank, kvap=kvap):
                            last = None
                            for kc in range(2):
                                last = e.matmul(bank,
                                                wn[0:128, wbase + kc * 256 + ec * 128: wbase + kc * 256 + ec * 128 + 128],
                                                kvap(kc, cb), start=(kc == 0), stop=(kc == 1))
                            return last
                        P.op("tensor", mmg, reads=[wn, xn],
                             writes=[(P.psum, (ec * 2 + cb) % 4)])
                        P.op("scalar", lambda e, dst=dst, ec=ec, cb=cb, bank=bank:
                             e.copy(dst[0:128, ec * 1024 + cb * 512: ec * 1024 + (cb + 1) * 512], bank),
                             reads=[(P.psum, (ec * 2 + cb) % 4)], writes=[dst])
            for ec in range(2):
                for cb in range(2):
                    bank = P.bank((ec * 2 + cb) % 4)

                    def mmd(e, ec=ec, cb=cb, bank=bank, kvap=kvap):
                        last = None
                        for kc in range(2):
                            last = e.matmul(bank,
                                            wn[0:128, 1024 + kc * 256 + ec * 128: 1024 + kc * 256 + ec * 128 + 128],
                                            kvap(kc, cb), start=(kc == 0), stop=(kc == 1))
                        return last
                    P.op("tensor", mmd, reads=[wn, xn],
                         writes=[(P.psum, (ec * 2 + cb) % 4)])
                    P.op("scalar", lambda e, ec=ec, cb=cb, bank=bank:
                         e.copy(lws["e3"][0:128, ec * 1024 + cb * 512: ec * 1024 + (cb + 1) * 512], bank),
                         reads=[(P.psum, (ec * 2 + cb) % 4)], writes=[lws["e3"]])
            E = lws
            P.op("scalar", lambda e: e.activation(E["e4"][:, :], E["e1"][:, :], AF.Sigmoid),
                 reads=[E["e1"]], writes=[E["e4"]])
            P.op("vector", lambda e: e.tensor_mul(E["e5"][:, :], E["e1"][:, :], E["e4"][:, :]),
                 reads=[E["e1"], E["e4"]], writes=[E["e5"]])
            P.op("vector", lambda e: e.tensor_mul(E["e6"][:, :], E["e5"][:, :], E["e2"][:, :]),
                 reads=[E["e5"], E["e2"]], writes=[E["e6"]])
            P.op("vector", lambda e: e.scalar_tensor_tensor(E["e7"][:, :], E["e4"][:, :],
                                                            -1.0, E["e1"][:, :], ALU.mult,
                                                            ALU.bypass) if False else
                 e.tensor_scalar(E["e7"][:, :], E["e4"][:, :], -1.0, 1.0, ALU.mult, ALU.add),
                 reads=[E["e4"]], writes=[E["e7"]])
            P.op("vector", lambda e: e.tensor_mul(E["e7"][:, :], E["e7"][:, :], E["e1"][:, :]),
                 reads=[E["e7"], E["e1"]], writes=[E["e7"]])
            P.op("vector", lambda e: e.tensor_scalar_add(E["e7"][:, :], E["e7"][:, :], 1.0),
                 reads=[E["e7"]], writes=[E["e7"]])
            P.op("vector", lambda e: e.tensor_mul(E["e7"][:, :], E["e7"][:, :], E["e4"][:, :]),
                 reads=[E["e7"], E["e4"]], writes=[E["e7"]])
            P.op("vector", lambda e: e.tensor_mul(E["e1"][:, :], E["e3"][:, :], E["e2"][:, :]),
                 reads=[E["e3"], E["e2"]], writes=[E["e1"]])
            P.op("vector", lambda e: e.tensor_mul(E["e1"][:, :], E["e1"][:, :], E["e7"][:, :]),
                 reads=[E["e1"], E["e7"]], writes=[E["e1"]])
            P.op("vector", lambda e: e.tensor_mul(E["e3"][:, :], E["e3"][:, :], E["e5"][:, :]),
                 reads=[E["e3"], E["e5"]], writes=[E["e3"]])
            for (srcf, dstb) in ((None, kvtm), (E["e1"], ptm), (E["e6"], htm), (E["e3"], h2tm)):
                for ec in range(2):
                    for ttl in range(8):
                        b = (ec * 8 + ttl) % 2
                        pb = P.bankb(b)
                        if srcf is None:
                            chx = 2 * lh + ec
                            sap = xn[0:128, chx * T + col0 + ttl * 128: chx * T + col0 + (ttl + 1) * 128]
                            rd = [xn]
                        else:
                            sap = srcf[0:128, ec * 1024 + ttl * 128: ec * 1024 + (ttl + 1) * 128]
                            rd = [srcf]
                        P.op("tensor", lambda e, sap=sap, pb=pb: e.transpose(pb, sap, C.id_bf[:, :]),
                             reads=rd + [C.id_bf], writes=[(P.psumb, b)])
                        P.op("vector", lambda e, dstb=dstb, ttl=ttl, ec=ec, pb=pb:
                             e.tensor_copy(dstb[0:128, ttl * 256 + ec * 128: ttl * 256 + (ec + 1) * 128], pb),
                             reads=[(P.psumb, b)], writes=[dstb])
            for (wncol, lbuf, rbuf, wlcol) in ((1536, kvtm, ptm, 0),
                                               (2048, kvtm, h2tm, 512),
                                               (2560, htm, kvtm, 1024)):
                for dc in range(2):
                    bank = P.bank(4 + dc % 2, 128, 256)

                    def mmw(e, lbuf=lbuf, rbuf=rbuf, dc=dc, bank=bank):
                        last = None
                        for ttl in range(8):
                            last = e.matmul(bank,
                                            lbuf[0:128, ttl * 256 + dc * 128: ttl * 256 + dc * 128 + 128],
                                            rbuf[0:128, ttl * 256: ttl * 256 + 256],
                                            start=(ttl == 0), stop=(ttl == 7))
                        return last
                    P.op("tensor", mmw, reads=[lbuf, rbuf], writes=[(P.psum, 4 + dc % 2)])
                    P.op("vector", lambda e, wncol=wncol, dc=dc, wlcol=wlcol, bank=bank:
                         e.tensor_add(wn[0:128, wncol + dc * 256: wncol + dc * 256 + 256],
                                      P.bank(4 + dc % 2, 128, 256),
                                      wl[0:128, wlcol + dc * 256: wlcol + dc * 256 + 256]),
                         reads=[(P.psum, 4 + dc % 2), wl], writes=[wn])
            for (dst, wncol) in ((E["e1"], 1536), (E["e2"], 2048)):
                for ec in range(2):
                    for cb in range(2):
                        bank = P.bank((ec * 2 + cb) % 4)

                        def mmq2(e, wncol=wncol, ec=ec, cb=cb, bank=bank, kvap=kvap):
                            last = None
                            for kc in range(2):
                                last = e.matmul(bank,
                                                wn[0:128, wncol + kc * 256 + ec * 128: wncol + kc * 256 + ec * 128 + 128],
                                                kvap(kc, cb), start=(kc == 0), stop=(kc == 1))
                            return last
                        P.op("tensor", mmq2, reads=[wn, xn],
                             writes=[(P.psum, (ec * 2 + cb) % 4)])
                        if dst is E["e1"]:
                            P.op("scalar", lambda e, ec=ec, cb=cb, bank=bank:
                                 e.activation(E["e1"][0:128, ec * 1024 + cb * 512: ec * 1024 + (cb + 1) * 512],
                                              bank, AF.Silu),
                                 reads=[(P.psum, (ec * 2 + cb) % 4)], writes=[E["e1"]])
                        else:
                            P.op("scalar", lambda e, ec=ec, cb=cb, bank=bank:
                                 e.copy(E["e2"][0:128, ec * 1024 + cb * 512: ec * 1024 + (cb + 1) * 512], bank),
                                 reads=[(P.psum, (ec * 2 + cb) % 4)], writes=[E["e2"]])
            P.op("vector", lambda e: e.tensor_mul(E["e1"][:, :], E["e1"][:, :], E["e2"][:, :]),
                 reads=[E["e1"], E["e2"]], writes=[E["e1"]])
            for dc in range(2):
                for cb in range(2):
                    bank = P.bank((dc * 2 + cb) % 4)

                    def mmoq(e, dc=dc, cb=cb, bank=bank):
                        last = None
                        for kc in range(2):
                            last = e.matmul(bank,
                                            wn[0:128, 2560 + kc * 256 + dc * 128: 2560 + kc * 256 + dc * 128 + 128],
                                            E["e1"][0:128, kc * 1024 + cb * 512: kc * 1024 + (cb + 1) * 512],
                                            start=(kc == 0), stop=(kc == 1))
                        return last
                    P.op("tensor", mmoq, reads=[wn, E["e1"]],
                         writes=[(P.psum, (dc * 2 + cb) % 4)])
                    ch = 2 * lh + dc
                    xsl = x[0:128, ch * T + col0 + cb * 512: ch * T + col0 + (cb + 1) * 512]
                    P.op("vector", lambda e, xsl=xsl, bank=bank:
                         e.tensor_add(xsl, xsl, bank),
                         reads=[(P.psum, (dc * 2 + cb) % 4),
                                x.r((ch, (col0 + cb * 512) // 512))],
                         writes=[x.r((ch, (col0 + cb * 512) // 512))])



def emit_l1_attn_a(C, x, xn, ak, av, pt, outfm, aqc, dd, wbuf, cscr, mrowb,
                   zbsb, mconst, T):
    mb0, mb1 = mconst

    def band_ap(bi):
        if bi < 4:
            return mb0[0:128, bi * 512:(bi + 1) * 512]
        return mb1[0:128, (bi - 4) * 512:(bi - 4 + 1) * 512]

    def band_ref(bi):
        return mb0 if bi < 4 else mb1

    """dd: qw/kw/vw/ow hosttile(natural, 2) [128, 8192]; ob ([128, 8] cols);
    us ([1,1] unity_scale). mconst: Buf [128, 6*512] bf16: col 0: m4col
    ([128,1] j%4 pattern at col 0); band tiles for delta -384..128 at
    cols 512*(1+i). All masks host-fed constants."""
    P = C.P
    ntile = T // 128
    NQB = T // 512

    P.op("sync", lambda e: e.dma_start(out=mb0[0:128, 0:2048], in_=dd["mc"][0:128, 0:2048]),
         writes=[mb0], dma=True, slot="mb0", fence=[C.f32scr])
    P.op("sync", lambda e: e.dma_start(out=mb1[0:128, 0:1024], in_=dd["mc"][0:128, 2048:3072]),
         writes=[mb1], dma=True, slot="mb1", fence=[C.nscr])
    P.op("sync", lambda e: e.dma_start(out=cscr[0:128, 14:15], in_=dd["m4"]),
         writes=[cscr.r("m4")], dma=True, slot="m4")
    P.op("sync", lambda e: e.dma_start(out=cscr[0:1, 13:14], in_=dd["us"]),
         writes=[cscr.r("usraw")], dma=True, slot="us")
    P.op("scalar", lambda e: e.activation(cscr[0:1, 11:12], cscr[0:1, 13:14],
                                          AF.Sigmoid),
         reads=[cscr.r("usraw")], writes=[cscr.r("us")])
    P.op("tensor", lambda e: e.matmul(P.bank(4, 128, 1), C.ones_row_f32[:, :],
                                      cscr[0:1, 11:12], start=True, stop=True),
         reads=[cscr.r("us"), C.ones_row_f32], writes=[(P.psum, 4)])
    P.op("scalar", lambda e: e.mul(cscr[0:128, 12:13], P.bank(4, 128, 1), 0.125),
         reads=[(P.psum, 4)], writes=[cscr.r("qs")])

    for g in (0, 1):
        base = g * 4096
        P.op("sync", lambda e, g=g, base=base:
             e.dma_start(out=wbuf[0:128, base:base + 4096],
                         in_=dd["kw"][0:128, g * 4096:(g + 1) * 4096]),
             writes=[wbuf.r(("w", g))], dma=True, slot="wA" if g == 0 else "wB")
        for oc in range(4):
            goc = g * 4 + oc
            for tb in range(NQB):
                bk = (goc * NQB + tb) % 4
                bank = P.bank(bk)

                def mmk(e, base=base, oc=oc, tb=tb, bank=bank):
                    last = None
                    for kc in range(8):
                        last = e.matmul(bank,
                                        wbuf[0:128, base + kc * 512 + oc * 128: base + kc * 512 + oc * 128 + 128],
                                        xn[0:128, kc * T + tb * 512: kc * T + (tb + 1) * 512],
                                        start=(kc == 0), stop=(kc == 7))
                    return last
                P.op("tensor", mmk, reads=[wbuf.r(("w", g)), xn], writes=[(P.psum, bk)])
                P.op("scalar", lambda e, goc=goc, tb=tb, bank=bank:
                     e.copy(ak[0:128, goc * T + tb * 512: goc * T + (tb + 1) * 512], bank),
                     reads=[(P.psum, bk)], writes=[ak])
    for g in (0, 1):
        base = g * 4096
        P.op("sync", lambda e, g=g, base=base:
             e.dma_start(out=wbuf[0:128, base:base + 4096],
                         in_=dd["vw"][0:128, g * 4096:(g + 1) * 4096]),
             writes=[wbuf.r(("w", g))], dma=True, slot="wA" if g == 0 else "wB")
        for tt in range(ntile):
            bk = (tt + g) % 4
            bank = P.bank(bk)

            def mmv(e, base=base, tt=tt, bank=bank):
                last = None
                for kc in range(8):
                    last = e.matmul(bank,
                                    xn[0:128, kc * T + tt * 128: kc * T + (tt + 1) * 128],
                                    wbuf[0:128, base + kc * 512: base + (kc + 1) * 512],
                                    start=(kc == 0), stop=(kc == 7))
                return last
            P.op("tensor", mmv, reads=[wbuf.r(("w", g)), xn], writes=[(P.psum, bk)])
            P.op("scalar", lambda e, tt=tt, g=g, bank=bank:
                 e.copy(av[0:128, tt * 1024 + g * 512: tt * 1024 + (g + 1) * 512], bank),
                 reads=[(P.psum, bk)], writes=[av])
    for qb in range(NQB):
        for g in (0, 1):
            base = g * 4096
            P.op("sync", lambda e, g=g, base=base:
                 e.dma_start(out=wbuf[0:128, base:base + 4096],
                             in_=dd["qw"][0:128, g * 4096:(g + 1) * 4096]),
                 writes=[wbuf.r(("w", g))], dma=True, slot="wA" if g == 0 else "wB")
        for c in range(8):
            bank = P.bank(4)
            g = c // 4
            oc = c % 4

            def mmq(e, g=g, oc=oc, qb=qb, bank=bank):
                last = None
                for kc in range(8):
                    last = e.matmul(bank,
                                    wbuf[0:128, g * 4096 + kc * 512 + oc * 128: g * 4096 + kc * 512 + oc * 128 + 128],
                                    xn[0:128, kc * T + qb * 512: kc * T + (qb + 1) * 512],
                                    start=(kc == 0), stop=(kc == 7))
                return last
            P.op("tensor", mmq, reads=[wbuf.r(("w", g)), xn], writes=[(P.psum, 4)])
            P.op("scalar", lambda e, bank=bank:
                 e.activation(aqc[0:128, 0:512], bank, AF.Identity,
                              scale=cscr[0:128, 12:13]),
                 reads=[(P.psum, 4), cscr.r("qs")], writes=[aqc])
            for hp in range(2):
                h = 2 * c + hp
                hb = hp * 64
                for qt in range(4):
                    def mmS(e, c=c, hb=hb, qb=qb, qt=qt):
                        last = None
                        for kb in range(T // 512):
                            last = e.matmul(
                                P.bank(kb % 4, 128, 512),
                                aqc[hb:hb + 64, qt * 128:(qt + 1) * 128],
                                ak[hb:hb + 64, c * T + kb * 512: c * T + (kb + 1) * 512],
                                start=True, stop=True)
                        return last
                    P.op("tensor", mmS, reads=[aqc, ak],
                         writes=[(P.psum, kb) for kb in range(min(4, T // 512))])
                    P.op("vector", lambda e, qt=qt: e.reduce_max(
                        cscr[0:128, qt:qt + 1], P.bank(0, 128, min(2048, T)), axis=AX.X,
                        negate=True),
                         reads=[(P.psum, kb) for kb in range(min(4, T // 512))],
                         writes=[cscr.r(("nm", qt))])
                    P.op("tensor", lambda e, qt=qt: e.transpose(
                        P.bank(4, 1, 128), cscr[0:128, qt:qt + 1], C.id_f32[:, :]),
                         reads=[cscr.r(("nm", qt)), C.id_f32], writes=[(P.psum, 4)])
                    P.op("scalar", lambda e, qt=qt: e.copy(
                        mrowb[0:1, qt * 128:(qt + 1) * 128], P.bank(4, 1, 128)),
                         reads=[(P.psum, 4)], writes=[mrowb])
                for half in range(2):
                    for ki in range(ntile // 2):
                        kt = half * (ntile // 2) + ki
                        bst = 4 + (ki % 2)
                        bank = P.bank(bst)

                        def mmST(e, c=c, hb=hb, kt=kt, qb=qb, bank=bank):
                            e.matmul(bank, C.ones_row_bf[:, :], mrowb[0:1, 0:512],
                                     start=True, stop=False)
                            return e.matmul(bank,
                                            ak[hb:hb + 64, c * T + kt * 128: c * T + (kt + 1) * 128],
                                            aqc[hb:hb + 64, 0:512],
                                            start=False, stop=True)
                        P.op("tensor", mmST, reads=[ak, aqc, mrowb, C.ones_row_bf],
                             writes=[(P.psum, bst)])
                        delta = kt * 128 - qb * 512
                        P.op("scalar", lambda e, ki=ki, bank=bank:
                             e.activation(pt[0:128, ki * 512:(ki + 1) * 512], bank, AF.Exp),
                             reads=[(P.psum, bst)], writes=[pt.r(("p", ki))])
                        if -128 <= delta <= 512:
                            bi_m = (delta + 128) // 128
                            P.op("vector", lambda e, ki=ki, bi_m=bi_m:
                                 e.tensor_mul(pt[0:128, ki * 512:(ki + 1) * 512],
                                              pt[0:128, ki * 512:(ki + 1) * 512],
                                              band_ap(bi_m)),
                                 reads=[pt.r(("p", ki)), band_ref(bi_m)],
                                 writes=[pt.r(("p", ki))])
                        else:
                            P.op("vector", lambda e, ki=ki:
                                 e.tensor_scalar_mul(pt[0:128, ki * 512:(ki + 1) * 512],
                                                     pt[0:128, ki * 512:(ki + 1) * 512],
                                                     cscr[0:128, 14:15]),
                                 reads=[pt.r(("p", ki)), cscr.r("m4")],
                                 writes=[pt.r(("p", ki))])
                    def mmZ(e, half=half):
                        last = None
                        for ki in range(ntile // 2):
                            last = e.matmul(P.bank(3, 1, 512), C.ones_col_bf[:, :],
                                            pt[0:128, ki * 512:(ki + 1) * 512],
                                            start=(half == 0 and ki == 0),
                                            stop=(half == 1 and ki == ntile // 2 - 1))
                        return last
                    P.op("tensor", mmZ,
                         reads=[pt.r(("p", ki)) for ki in range(ntile // 2)] + [C.ones_col_bf],
                         writes=[(P.psum, 3)])

                    def mmO(e, half=half, h=h):
                        last = None
                        for ki in range(ntile // 2):
                            kt = half * (ntile // 2) + ki
                            last = e.matmul(P.bank(2, 64, 512),
                                            av[0:128, kt * 1024 + h * 64: kt * 1024 + h * 64 + 64],
                                            pt[0:128, ki * 512:(ki + 1) * 512],
                                            start=(half == 0 and ki == 0),
                                            stop=(half == 1 and ki == ntile // 2 - 1))
                        return last
                    P.op("tensor", mmO,
                         reads=[pt.r(("p", ki)) for ki in range(ntile // 2)] + [av],
                         writes=[(P.psum, 2)])
                zf = C.rowscr[0:1, 2 * SB:3 * SB]
                P.op("vector", lambda e, zf=zf: e.reciprocal(zf, P.bank(3, 1, 512)),
                     reads=[(P.psum, 3)], writes=[C.rowscr.r("zf")])
                P.op("tensor", lambda e, zf=zf: e.matmul(P.bank(5), C.ones_row_f32[:, :],
                                                         zf, start=True, stop=True),
                     reads=[C.rowscr.r("zf"), C.ones_row_f32], writes=[(P.psum, 5)])
                P.op("scalar", lambda e: e.copy(zbsb[0:128, 0:512], P.bank(5)),
                     reads=[(P.psum, 5)], writes=[zbsb])
                P.op("vector", lambda e, hb=hb, c=c:
                     e.tensor_mul(outfm[hb:hb + 64, c * 512:(c + 1) * 512],
                                  P.bank(2, 64, 512), zbsb[0:64, 0:512]),
                     reads=[(P.psum, 2), zbsb], writes=[outfm.r(("o", c, hb))])
        for hh2 in range(2):
            base2 = hh2 * 4096
            P.op("sync", lambda e, hh2=hh2, base2=base2:
                 e.dma_start(out=wbuf[0:128, base2:base2 + 4096],
                             in_=dd["ow"][0:128, hh2 * 4096:(hh2 + 1) * 4096]),
                 writes=[wbuf.r(("w", hh2))], dma=True, slot="wA" if hh2 == 0 else "wB")
            for oc in range(4):
                goc = hh2 * 4 + oc
                bk = goc % 2
                bank = P.bank(bk)

                def mmop(e, base2=base2, oc=oc, bank=bank):
                    last = None
                    for kc in range(8):
                        last = e.matmul(bank,
                                        wbuf[0:128, base2 + kc * 512 + oc * 128: base2 + kc * 512 + oc * 128 + 128],
                                        outfm[0:128, kc * 512:(kc + 1) * 512],
                                        start=(kc == 0), stop=(kc == 7))
                    return last
                P.op("tensor", mmop,
                     reads=[wbuf.r(("w", hh2))] + [outfm.r(("o", kc, b2)) for kc in range(8) for b2 in (0, 64)],
                     writes=[(P.psum, bk)])
                xsl = x[0:128, goc * T + qb * 512: goc * T + (qb + 1) * 512]
                P.op("vector", lambda e, xsl=xsl, bank=bank, goc=goc:
                     e.scalar_tensor_tensor(xsl, bank, dd["obc"][0:128, goc:goc + 1],
                                            xsl, ALU.add, ALU.add),
                     reads=[(P.psum, bk), x.r((goc, qb)), dd["obc"]],
                     writes=[x.r((goc, qb))])



def emit_l1_ffn(C, x, xn, u1, u2, dd, wbuf, scl, T):
    P = C.P
    NTB = T // 512
    for tb in range(NTB):
        for g in range(8):
            for wi, wname in ((0, "w1"), (1, "w2")):
                wslot = wi
                base = wslot * 4096
                P.op("sync", lambda e, wname=wname, g=g, base=base:
                     e.dma_start(out=wbuf[0:128, base:base + 4096],
                                 in_=dd[wname][0:128, g * 4096:(g + 1) * 4096]),
                     writes=[wbuf.r(("w", wslot))], dma=True,
                     slot="wA" if wslot == 0 else "wB")
                for oc in range(4):
                    goc = g * 4 + oc
                    bk = (goc * 2 + wi) % 4
                    bank = P.bank(bk)

                    def mmw(e, base=base, oc=oc, tb=tb, bank=bank):
                        last = None
                        for kc in range(8):
                            last = e.matmul(bank,
                                            wbuf[0:128, base + kc * 512 + oc * 128: base + kc * 512 + oc * 128 + 128],
                                            xn[0:128, kc * T + tb * 512: kc * T + (tb + 1) * 512],
                                            start=(kc == 0), stop=(kc == 7))
                        return last
                    P.op("tensor", mmw, reads=[wbuf.r(("w", wslot)), xn],
                         writes=[(P.psum, bk)])
                    if wi == 0:
                        P.op("scalar", lambda e, goc=goc, bank=bank:
                             e.activation(u2[0:128, goc * 512:(goc + 1) * 512], bank,
                                          AF.Silu, scale=scl[0:128, goc:goc + 1]),
                             reads=[(P.psum, bk), scl], writes=[u2.r(("u", goc))])
                    else:
                        P.op("vector", lambda e, goc=goc, bank=bank:
                             e.scalar_tensor_tensor(u1[0:128, goc * 512:(goc + 1) * 512],
                                                    bank, scl[0:128, 32 + goc:33 + goc],
                                                    u2[0:128, goc * 512:(goc + 1) * 512],
                                                    ALU.mult, ALU.mult),
                             reads=[(P.psum, bk), scl, u2.r(("u", goc))],
                             writes=[u1.r(("u", goc))])
        for (wname, srcb, dstb, scol) in (("h1", u1, u2, 64), ("h2", u2, u1, 96)):
            for g in range(32):
                wslot = g % 2
                base = wslot * 4096
                P.op("sync", lambda e, wname=wname, g=g, base=base:
                     e.dma_start(out=wbuf[0:128, base:base + 4096],
                                 in_=dd[wname][0:128, g * 4096:(g + 1) * 4096]),
                     writes=[wbuf.r(("w", wslot))], dma=True,
                     slot="wA" if wslot == 0 else "wB")
                bk = g % 4
                bank = P.bank(bk)

                def mmh(e, base=base, bank=bank, srcb=srcb):
                    last = None
                    for kc in range(32):
                        last = e.matmul(bank,
                                        wbuf[0:128, base + kc * 128: base + (kc + 1) * 128],
                                        srcb[0:128, kc * 512:(kc + 1) * 512],
                                        start=(kc == 0), stop=(kc == 31))
                    return last
                P.op("tensor", mmh,
                     reads=[wbuf.r(("w", wslot))] + [srcb.r(("u", kc)) for kc in range(32)],
                     writes=[(P.psum, bk)])
                P.op("scalar", lambda e, g=g, bank=bank, dstb=dstb, scol=scol:
                     e.activation(dstb[0:128, g * 512:(g + 1) * 512], bank, AF.Gelu,
                                  scale=scl[0:128, scol + g:scol + g + 1]),
                     reads=[(P.psum, bk), scl], writes=[dstb.r(("u", g))])
        for g in range(8):
            wslot = g % 2
            base = wslot * 4096
            P.op("sync", lambda e, g=g, base=base:
                 e.dma_start(out=wbuf[0:128, base:base + 4096],
                             in_=dd["w3"][0:128, g * 4096:(g + 1) * 4096]),
                 writes=[wbuf.r(("w", wslot))], dma=True,
                 slot="wA" if wslot == 0 else "wB")
            bk = g % 4
            bank = P.bank(bk)

            def mm3(e, base=base, bank=bank):
                last = None
                for kc in range(32):
                    last = e.matmul(bank,
                                    wbuf[0:128, base + kc * 128: base + (kc + 1) * 128],
                                    u1[0:128, kc * 512:(kc + 1) * 512],
                                    start=(kc == 0), stop=(kc == 31))
                return last
            P.op("tensor", mm3,
                 reads=[wbuf.r(("w", wslot))] + [u1.r(("u", kc)) for kc in range(32)],
                 writes=[(P.psum, bk)])
            xsl = x[0:128, g * T + tb * 512: g * T + (tb + 1) * 512]
            P.op("vector", lambda e, xsl=xsl, bank=bank, g=g:
                 e.scalar_tensor_tensor(xsl, bank, scl[0:128, 128 + g:129 + g],
                                        xsl, ALU.mult, ALU.add),
                 reads=[(P.psum, bk), scl, x.r((g, tb))],
                 writes=[x.r((g, tb))])


def emit_wvg(C, vgw, bvg, dd, wbuf, T):
    P = C.P
    if dd.get("_half", 0) == 0:
        P.op("sync", lambda e: e.dma_start(out=C.f32scr[0:8, 0:1024], in_=dd["bv_row"]),
             writes=[C.f32scr.r("bv")], dma=True, slot="bvrow")
        for half in range(2):
            P.op("tensor", lambda e, half=half: e.matmul(
                P.bank(4, 1, 512), C.ones_col_f32[0:8, 0:1],
                C.f32scr[0:8, half * 512:(half + 1) * 512], start=True, stop=True),
                 reads=[C.f32scr.r("bv"), C.ones_col_f32], writes=[(P.psum, 4)])
            P.op("scalar", lambda e, half=half: e.copy(bvg[0:1, half * 512:(half + 1) * 512],
                                                       P.bank(4, 1, 512)),
                 reads=[(P.psum, 4)], writes=[bvg])
    vh = dd.get("_half", 0)
    first = True
    for kc in range(8):
        for og in range(4 * vh, 4 * vh + 4):
            sl = wbuf[0:128, (og % 2) * 1024:(og % 2) * 1024 + 1024]
            P.op("sync", lambda e, kc=kc, og=og, sl=sl:
                 e.dma_start(out=sl, in_=dd["wv_nat"][kc * 128:(kc + 1) * 128,
                                                      og * 1024:(og + 1) * 1024]),
                 writes=[wbuf.r(("s", og % 2))], dma=True,
                 slot="vga" if og % 2 == 0 else "vgb")
            acc = C.f32scr[0:128, 0:128]
            P.op("vector", lambda e, sl=sl, acc=acc: e.tensor_copy(acc, sl[0:128, 0::8]),
                 reads=[wbuf.r(("s", og % 2))], writes=[C.f32scr.r("vg")],
                 fence=[C.f32scr] if first else [])
            first = False
            for r in range(1, 8):
                P.op("vector", lambda e, sl=sl, acc=acc, r=r:
                     e.tensor_add(acc, acc, sl[0:128, r::8]),
                     reads=[wbuf.r(("s", og % 2)), C.f32scr.r("vg")],
                     writes=[C.f32scr.r("vg")])
            P.op("vector", lambda e, kc=kc, og=og, acc=acc, vh=vh:
                 e.tensor_copy(vgw[0:128, kc * 512 + (og - 4 * vh) * 128:
                                   kc * 512 + (og - 4 * vh + 1) * 128], acc),
                 reads=[C.f32scr.r("vg")], writes=[vgw])


def emit_l1_attn_b(C, x, xn, ak, av, pt, aqc, vgw, bvg, dd, wbuf, bias, cscr,
                   mrowb, zbsb, T):
    P = C.P
    ntile = T // 128
    NQB = T // 512
    QS = 0.08838834764831845

    P.op("scalar", lambda e: e.mul(bias[0:128, 0:8], bias[0:128, 0:8], QS),
         reads=[bias], writes=[bias])
    for vh in range(2):
        dd["_half"] = vh
        emit_wvg(C, vgw, bvg, dd, dd["_vgstage"], T)
        for tt in range(ntile):
            bk = (tt + vh) % 4
            bank = P.bank(bk)

            def mmv(e, tt=tt, vh=vh, bank=bank):
                last = e.matmul(bank, C.ones_row_bf[:, :],
                                bvg[0:1, vh * 512:(vh + 1) * 512],
                                start=True, stop=False)
                for kc in range(8):
                    last = e.matmul(bank,
                                    xn[0:128, kc * T + tt * 128: kc * T + (tt + 1) * 128],
                                    vgw[0:128, kc * 512:(kc + 1) * 512],
                                    start=False, stop=(kc == 7))
                return last
            P.op("tensor", mmv, reads=[xn, vgw, bvg, C.ones_row_bf],
                 writes=[(P.psum, bk)])
            P.op("scalar", lambda e, tt=tt, vh=vh, bank=bank:
                 e.copy(av[0:128, tt * 1024 + vh * 512: tt * 1024 + (vh + 1) * 512], bank),
                 reads=[(P.psum, bk)], writes=[av])
    for g in (0, 1):
        base = g * 4096
        P.op("sync", lambda e, g=g, base=base:
             e.dma_start(out=wbuf[0:128, base:base + 4096],
                         in_=dd["kw"][0:128, g * 4096:(g + 1) * 4096]),
             writes=[wbuf.r(("w", g))], dma=True, slot="wA" if g == 0 else "wB")
        for oc in range(4):
            goc = g * 4 + oc
            for tb in range(NQB):
                bk = (goc * NQB + tb) % 4
                bank = P.bank(bk)

                def mmk(e, base=base, oc=oc, tb=tb, bank=bank):
                    last = None
                    for kc in range(8):
                        last = e.matmul(bank,
                                        wbuf[0:128, base + kc * 512 + oc * 128: base + kc * 512 + oc * 128 + 128],
                                        xn[0:128, kc * T + tb * 512: kc * T + (tb + 1) * 512],
                                        start=(kc == 0), stop=(kc == 7))
                    return last
                P.op("tensor", mmk, reads=[wbuf.r(("w", g)), xn], writes=[(P.psum, bk)])
                P.op("scalar", lambda e, goc=goc, tb=tb, bank=bank:
                     e.activation(ak[0:128, goc * T + tb * 512: goc * T + (tb + 1) * 512],
                                  bank, AF.Identity, bias=bias[0:128, 8 + goc:9 + goc]),
                     reads=[(P.psum, bk), bias], writes=[ak])
    for g in (0, 1):
        base = g * 4096
        P.op("sync", lambda e, g=g, base=base:
             e.dma_start(out=wbuf[0:128, base:base + 4096],
                         in_=dd["qw"][0:128, g * 4096:(g + 1) * 4096]),
             writes=[wbuf.r(("w", g))], dma=True, slot="wA" if g == 0 else "wB")
    for qb in range(NQB):
        for c in range(8):
            bank = P.bank(4)
            g = c // 4
            oc = c % 4

            def mmq(e, g=g, oc=oc, qb=qb, bank=bank):
                last = None
                for kc in range(8):
                    last = e.matmul(bank,
                                    wbuf[0:128, g * 4096 + kc * 512 + oc * 128: g * 4096 + kc * 512 + oc * 128 + 128],
                                    xn[0:128, kc * T + qb * 512: kc * T + (qb + 1) * 512],
                                    start=(kc == 0), stop=(kc == 7))
                return last
            P.op("tensor", mmq, reads=[wbuf.r(("w", g)), xn], writes=[(P.psum, 4)])
            P.op("scalar", lambda e, c=c, bank=bank:
                 e.activation(aqc[0:128, 0:512], bank, AF.Identity,
                              bias=bias[0:128, c:c + 1], scale=QS),
                 reads=[(P.psum, 4), bias], writes=[aqc])
            for qt in range(4):
                def mmS(e, c=c, qb=qb, qt=qt):
                    last = None
                    for kb in range(T // 512):
                        last = e.matmul(
                            P.bank(kb % 4, 128, 512),
                            aqc[0:128, qt * 128:(qt + 1) * 128],
                            ak[0:128, c * T + kb * 512: c * T + (kb + 1) * 512],
                            start=True, stop=True)
                    return last
                P.op("tensor", mmS, reads=[aqc, ak],
                     writes=[(P.psum, kb) for kb in range(min(4, T // 512))])
                P.op("vector", lambda e, qt=qt: e.reduce_max(
                    cscr[0:128, qt:qt + 1], P.bank(0, 128, min(2048, T)), axis=AX.X,
                    negate=True),
                     reads=[(P.psum, kb) for kb in range(min(4, T // 512))],
                     writes=[cscr.r(("nm", qt))])
                P.op("tensor", lambda e, qt=qt: e.transpose(
                    P.bank(4, 1, 128), cscr[0:128, qt:qt + 1], C.id_f32[:, :]),
                     reads=[cscr.r(("nm", qt)), C.id_f32], writes=[(P.psum, 4)])
                P.op("scalar", lambda e, qt=qt: e.copy(
                    mrowb[0:1, qt * 128:(qt + 1) * 128], P.bank(4, 1, 128)),
                     reads=[(P.psum, 4)], writes=[mrowb])
            for half in range(2):
                for ki in range(ntile // 2):
                    kt = half * (ntile // 2) + ki
                    bst = 4 + (ki % 2)
                    bank = P.bank(bst)

                    def mmST(e, c=c, kt=kt, qb=qb, bank=bank):
                        e.matmul(bank, C.ones_row_bf[:, :], mrowb[0:1, 0:512],
                                 start=True, stop=False)
                        return e.matmul(bank,
                                        ak[0:128, c * T + kt * 128: c * T + (kt + 1) * 128],
                                        aqc[0:128, 0:512],
                                        start=False, stop=True)
                    P.op("tensor", mmST, reads=[ak, aqc, mrowb, C.ones_row_bf],
                         writes=[(P.psum, bst)])
                    P.op("scalar", lambda e, ki=ki, bank=bank:
                         e.activation(pt[0:128, ki * 512:(ki + 1) * 512], bank, AF.Exp),
                         reads=[(P.psum, bst)], writes=[pt.r(("p", ki))])
                def mmZ(e, half=half):
                    last = None
                    for ki in range(ntile // 2):
                        last = e.matmul(P.bank(3, 1, 512), C.ones_col_bf[:, :],
                                        pt[0:128, ki * 512:(ki + 1) * 512],
                                        start=(half == 0 and ki == 0),
                                        stop=(half == 1 and ki == ntile // 2 - 1))
                    return last
                P.op("tensor", mmZ,
                     reads=[pt.r(("p", ki)) for ki in range(ntile // 2)] + [C.ones_col_bf],
                     writes=[(P.psum, 3)])

                def mmO(e, half=half, c=c):
                    last = None
                    for ki in range(ntile // 2):
                        kt = half * (ntile // 2) + ki
                        last = e.matmul(P.bank(2, 128, 512),
                                        av[0:128, kt * 1024 + c * 128: kt * 1024 + (c + 1) * 128],
                                        pt[0:128, ki * 512:(ki + 1) * 512],
                                        start=(half == 0 and ki == 0),
                                        stop=(half == 1 and ki == ntile // 2 - 1))
                    return last
                P.op("tensor", mmO,
                     reads=[pt.r(("p", ki)) for ki in range(ntile // 2)] + [av],
                     writes=[(P.psum, 2)])
            zf = C.rowscr[0:1, 2 * SB:3 * SB]
            P.op("vector", lambda e, zf=zf: e.reciprocal(zf, P.bank(3, 1, 512)),
                 reads=[(P.psum, 3)], writes=[C.rowscr.r("zf")])
            P.op("tensor", lambda e, zf=zf: e.matmul(P.bank(5), C.ones_row_f32[:, :],
                                                     zf, start=True, stop=True),
                 reads=[C.rowscr.r("zf"), C.ones_row_f32], writes=[(P.psum, 5)])
            P.op("scalar", lambda e: e.copy(zbsb[0:128, 0:512], P.bank(5)),
                 reads=[(P.psum, 5)], writes=[zbsb])
            P.op("vector", lambda e: e.tensor_mul(aqc[0:128, 512:1024],
                                                  P.bank(2, 128, 512), zbsb[0:128, 0:512]),
                 reads=[(P.psum, 2), zbsb], writes=[aqc.r("og")])
            for j in range(8):
                xsl = x[0:128, j * T + (T // 8) * c + qb * 64: j * T + (T // 8) * c + (qb + 1) * 64]
                P.op("vector", lambda e, xsl=xsl, j=j:
                     e.tensor_add(xsl, xsl, aqc[0:128, 512 + j:1024:8]),
                     reads=[aqc.r("og"), x.r(("sc", j, c, qb))],
                     writes=[x.r(("sc", j, c, qb))])



TS = 2048


def _io_and_base(nc, rowslabs=4):
    xin = nc.dram_tensor("xin", [D, TS], F32, kind="ExternalInput").ap()
    xout = nc.dram_tensor("xout", [D, TS], F32, kind="ExternalOutput").ap()
    P = Prog(nc)
    P.begin()
    C = Ctx(P, rowslabs=rowslabs)
    x = P.sbuf("x", [128, NC * TS], F32)
    xn = P.sbuf("xn", [128, NC * TS], BF16)
    for c in range(NC):
        P.op("sync", lambda e, c=c: e.dma_start(out=x[:, c * TS:(c + 1) * TS],
                                                in_=xin[c * 128:(c + 1) * 128, :]),
             writes=[x], dma=True, slot="xio")
    return P, C, x, xn, xout


def _store_x(P, x, xout):
    for c in range(NC):
        P.op("sync", lambda e, c=c: e.dma_start(out=xout[c * 128:(c + 1) * 128, :],
                                                in_=x[:, c * TS:(c + 1) * TS]),
             reads=[x], dma=True, slot="xout")
    P.barrier()
    P.finish()


def _decl_a0(nc, wc, p=""):
    dd = {k: WView(wc, p + k + "_d") for k in ("wq", "wk", "wv")}
    dd["b"] = nc.dram_tensor(p + "b_d", [128, 24], F32, kind="ExternalInput").ap()
    return dd


def emit_sub_a0(P, C, x, xn, dd, u=""):
    s1 = P.sbuf("s1" + u, [128, NC * TS], BF16)
    s2 = P.sbuf("s2" + u, [128, NC * TS], BF16)
    m4 = P.sbuf("m4" + u, [128, 8192], BF16)
    wbuf = P.sbuf("wbuf" + u, [128, 8192], BF16)
    bias = P.sbuf("bias" + u, [128, 24], F32)
    cscr = P.sbuf("cscr" + u, [128, 32], F32)
    P.op("sync", lambda e: e.dma_start(out=bias[:, :], in_=dd["b"]),
         writes=[bias], dma=True, slot="b0" + u)
    emit_norm(C, x, xn, TS, "plain")
    emit_l0_attn_a(C, x, xn, s1, s2, m4, dd, wbuf, bias, cscr, TS)


def _decl_moe(nc, wc, p=""):
    dd = {
        "gw": WView(wc, p + "gw_d"),
        "gb": nc.dram_tensor(p + "gb_d", [1, 32], F32, kind="ExternalInput").ap(),
        "b2": nc.dram_tensor(p + "b2_d", [32, 1024], F32, kind="ExternalInput").ap(),
        "onehot": nc.dram_tensor(p + "oneh_d", [32, 4096], BF16, kind="ExternalInput").ap(),
        "w1": [WView(wc, p + f"w1_{e}") for e in range(32)],
        "w2": [WView(wc, p + f"w2_{e}") for e in range(32)],
        "b1": [nc.dram_tensor(p + f"b1_{e}", [128, 16], F32, kind="ExternalInput").ap()
               for e in range(32)],
        "gl": nc.dram_tensor(p + "gl_in", [1, D], F32, kind="ExternalInput").ap(),
    }
    return dd


def emit_sub_moe(P, C, x, xn, dd, u=""):
    s1 = P.sbuf("s1" + u, [128, NC * TS], BF16)
    s2 = P.sbuf("s2" + u, [128, NC * TS], BF16)
    wbuf = P.sbuf("wbuf" + u, [128, 8192], BF16)
    gl = P.sbuf("gl" + u, [1, D], F32)
    moescr = {
        "gw": P.sbuf("gwb" + u, [128, 256], BF16),
        "gbrow": P.sbuf("gbrow" + u, [1, 32], F32),
        "gbbc": P.sbuf("gbbc" + u, [128, 32], F32),
        "gbuf": P.sbuf("gbuf" + u, [128, 64], F32),
        "top8": P.sbuf("top8" + u, [128, 16], F32),
        "mwork": P.sbuf("mwork" + u, [128, 64], F32),
        "ctm": P.sbuf("ctm" + u, [128, 64], F32),
        "cfmb": P.sbuf("cfmb" + u, [32, TS], BF16),
        "oneh": P.sbuf("oneh" + u, [32, 128], BF16),
        "cebc": P.sbuf("cebc" + u, [128, 512], BF16),
        "b1b": P.sbuf("b1b" + u, [128, 16], F32),
        "b2b": P.sbuf("b2b" + u, [32, 1024], BF16),
        "vs1": P.sbuf("vs1" + u, [128, 1024], F32),
        "vs2": P.sbuf("vs2" + u, [128, 1024], F32),
    }
    P.op("sync", lambda e: e.dma_start(out=gl[:, :], in_=dd["gl"]),
         writes=[gl], dma=True, slot="glin" + u)
    emit_norm(C, x, xn, TS, "lnrms", gl=gl, u="fold")
    emit_l0_ffn(C, x, xn, s1, s2, dd, wbuf, moescr, TS)


def _decl_ab(nc, wc, p=""):
    dd = {
        "ip": WView(wc, p + "ip_d"),
        "op": WView(wc, p + "op_d"),
        "ibrow_v": nc.dram_tensor(p + "ibv_d", [1, 1024], F32, kind="ExternalInput").ap(),
        "opb": nc.dram_tensor(p + "opb_d", [128, 8], F32, kind="ExternalInput").ap(),
        "bias_qk": nc.dram_tensor(p + "bqk_d", [128, 16], F32, kind="ExternalInput").ap(),
        "wl": [WView(wc, p + f"wl_{h}") for h in range(4)],
        "wln": [WView(wc, p + f"wln_{h}") for h in range(4)],
        "gb": nc.dram_tensor(p + "gb_in", [2, D], F32, kind="ExternalInput").ap(),
    }
    return dd


def emit_sub_ab(P, C, x, xn, dd, u=""):
    ak = P.sbuf("ak" + u, [128, NC * TS], BF16)
    av = P.sbuf("av" + u, [128, (TS // 128) * 1024], BF16)
    pt = P.sbuf("pt" + u, [128, 8 * 512], BF16)
    outfm = P.sbuf("outfm" + u, [128, 8 * 512], BF16)
    aqc = P.sbuf("aqc" + u, [128, 1024], BF16)
    wbuf = P.sbuf("wbuf" + u, [128, 8192], BF16)
    bias = P.sbuf("biasqk" + u, [128, 16], F32)
    cscr = P.sbuf("cscr" + u, [128, 32], F32)
    mrowb = P.sbuf("mrowb" + u, [1, 512], BF16)
    zbsb = P.sbuf("zbsb" + u, [128, 512], BF16)
    gb2 = C.f32scr
    lws = {
        "ibrow": SubBuf(C.nscr, 0, 1024),
        "opb": P.sbuf("opbc" + u, [128, 8], F32),
        "wl": SubBuf(pt, 0, 1536),
        "kvtm": SubBuf(av, 0, 2048),
        "ptm": SubBuf(av, 2048, 2048),
        "htm": SubBuf(av, 4096, 2048),
        "h2tm": SubBuf(av, 6144, 2048),
        "wn": SubBuf(av, 8192, 3072),
    }
    for i, k in enumerate(("e1", "e2", "e3", "e4", "e5", "e6", "e7")):
        lws[k] = SubBuf(ak, i * 2048, 2048)
    P.op("sync", lambda e: e.dma_start(out=gb2[0:2, 0:D], in_=dd["gb"]),
         writes=[gb2], dma=True, slot="gbin" + u)
    P.op("sync", lambda e: e.dma_start(out=bias[:, :], in_=dd["bias_qk"]),
         writes=[bias], dma=True, slot="bqk" + u)
    P.op("sync", lambda e: e.dma_start(out=lws["opb"][:, :], in_=dd["opb"]),
         writes=[lws["opb"]], dma=True, slot="opb" + u)
    emit_norm(C, x, xn, TS, "lnln", gb2=gb2)
    P.op("sync", lambda e: e.dma_start(out=C.f32scr[0:1, 0:1024], in_=dd["ibrow_v"]),
         writes=[C.f32scr], dma=True, slot="ibrow" + u)
    P.op("vector", lambda e: e.tensor_copy(lws["ibrow"][0:1, 0:1024], C.f32scr[0:1, 0:1024]),
         reads=[C.f32scr], writes=[lws["ibrow"]], fence=[C.nscr])
    emit_l0_attn_b(C, x, xn, ak, av, pt, outfm, aqc, dd, wbuf, bias, cscr,
                   mrowb, zbsb, lws, TS)


def _decl_a1(nc, wc, p=""):
    dd = {k: WView(wc, p + k + "_d") for k in ("qw", "kw", "vw", "ow")}
    dd["us"] = nc.dram_tensor(p + "us_d", [1, 1], F32, kind="ExternalInput").ap()
    dd["ob"] = nc.dram_tensor(p + "ob_d", [128, 8], F32, kind="ExternalInput").ap()
    dd["mc"] = nc.dram_tensor(p + "mc_d", [128, 3072], BF16, kind="ExternalInput").ap()
    dd["m4"] = nc.dram_tensor(p + "m4_d", [128, 1], F32, kind="ExternalInput").ap()
    return dd


def emit_sub_a1(P, C, x, xn, dd, u=""):
    ak = P.sbuf("ak" + u, [128, NC * TS], BF16)
    av = P.sbuf("av" + u, [128, (TS // 128) * 1024], BF16)
    pt = P.sbuf("pt" + u, [128, 8 * 512], BF16)
    outfm = P.sbuf("outfm" + u, [128, 8 * 512], BF16)
    aqc = P.sbuf("aqc" + u, [128, 1024], BF16)
    wbuf = P.sbuf("wbuf" + u, [128, 8192], BF16)
    cscr = P.sbuf("cscr" + u, [128, 32], F32)
    mrowb = P.sbuf("mrowb" + u, [1, 512], BF16)
    zbsb = P.sbuf("zbsb" + u, [128, 512], BF16)
    mconst = (SubBuf(C.f32scr, 0, 2048, dtype=BF16), SubBuf(C.nscr, 0, 1024))
    obc = P.sbuf("obc" + u, [128, 8], F32)
    dd = dict(dd)
    dd["obc"] = obc
    P.op("sync", lambda e: e.dma_start(out=obc[:, :], in_=dd["ob"]),
         writes=[obc], dma=True, slot="obc" + u)
    emit_norm(C, x, xn, TS, "plain")
    emit_l1_attn_a(C, x, xn, ak, av, pt, outfm, aqc, dd, wbuf, cscr, mrowb,
                   zbsb, mconst, TS)


def _decl_f1(nc, wc, p=""):
    dd = {k: WView(wc, p + k + "_d") for k in ("w1", "w2", "h1", "h2", "w3")}
    dd["scl"] = nc.dram_tensor(p + "scl_d", [128, 136], F32, kind="ExternalInput").ap()
    return dd


def emit_sub_f1(P, C, x, xn, dd, u=""):
    u1 = P.sbuf("u1" + u, [128, 32 * 512], BF16)
    u2 = P.sbuf("u2" + u, [128, 32 * 512], BF16)
    wbuf = P.sbuf("wbuf" + u, [128, 8192], BF16)
    scl = P.sbuf("scl" + u, [128, 136], F32)
    P.op("sync", lambda e: e.dma_start(out=scl[:, :], in_=dd["scl"]),
         writes=[scl], dma=True, slot="scl" + u)
    emit_norm(C, x, xn, TS, "plain")
    emit_l1_ffn(C, x, xn, u1, u2, dd, wbuf, scl, TS)


def _decl_b1(nc, wc, p=""):
    return {"qw": WView(wc, p + "qw_d"),
            "kw": WView(wc, p + "kw_d"),
            "wv_nat": nc.dram_tensor(p + "wv_d", [1024, 8192], F32, kind="ExternalInput").ap(),
            "bv_row": nc.dram_tensor(p + "bv_d", [8, 1024], F32, kind="ExternalInput").ap(),
            "bias": nc.dram_tensor(p + "bias_d", [128, 16], F32, kind="ExternalInput").ap(),
            "gb": nc.dram_tensor(p + "gb_in", [2, D], F32, kind="ExternalInput").ap()}


def emit_sub_b1(P, C, x, xn, dd, u=""):
    ak = P.sbuf("ak" + u, [128, NC * TS], BF16)
    av = P.sbuf("av" + u, [128, (TS // 128) * 1024], BF16)
    pt = P.sbuf("pt" + u, [128, 8 * 512], BF16)
    aqc = P.sbuf("aqc" + u, [128, 1024], BF16)
    vgw = P.sbuf("vgw" + u, [128, 4096], BF16)
    bvg = SubBuf(aqc, 0, 1024)
    wbuf = P.sbuf("wbuf" + u, [128, 8192], BF16)
    vgstage = SubBuf(ak, 0, 2048, dtype=F32)
    bias = P.sbuf("biasqk" + u, [128, 16], F32)
    cscr = P.sbuf("cscr" + u, [128, 32], F32)
    mrowb = P.sbuf("mrowb" + u, [1, 512], BF16)
    zbsb = P.sbuf("zbsb" + u, [128, 512], BF16)
    gb2 = C.f32scr
    dd = dict(dd)
    dd["_vgstage"] = vgstage
    P.op("sync", lambda e: e.dma_start(out=bias[:, :], in_=dd["bias"]),
         writes=[bias], dma=True, slot="bias" + u)
    P.op("sync", lambda e: e.dma_start(out=gb2[0:2, 0:D], in_=dd["gb"]),
         writes=[gb2], dma=True, slot="gbin" + u)
    emit_norm(C, x, xn, TS, "lnln", gb2=gb2)
    emit_l1_attn_b(C, x, xn, ak, av, pt, aqc, vgw, bvg, dd, wbuf, bias,
                   cscr, mrowb, zbsb, TS)


_DECLS = {"a0": _decl_a0, "moe": _decl_moe, "ab": _decl_ab, "a1": _decl_a1,
          "f1": _decl_f1, "b1": _decl_b1}
_EMITS = {"a0": emit_sub_a0, "moe": emit_sub_moe, "ab": emit_sub_ab,
          "a1": emit_sub_a1, "f1": emit_sub_f1, "b1": emit_sub_b1}


def build_prog_fused():
    nc = bass.Bass("TRN2", target_bir_lowering=False, dynamic_dma_scratch_size=SCRATCH)
    xin = nc.dram_tensor("xin", [2048, TS], F32, kind="ExternalInput").ap()
    xout = nc.dram_tensor("xout", [2048, TS], F32, kind="ExternalOutput").ap()
    wsrc = nc.dram_tensor("wpack", [128, PACK_TOT], F32, kind="ExternalInput").ap()
    wc = WCache(
        nc.dram_tensor("wcacheA", [128, WC_SPLIT], BF16, kind="Internal").ap(),
        nc.dram_tensor("wcacheB", [128, PACK_TOT - WC_SPLIT], BF16, kind="Internal").ap())
    P = Prog(nc)
    P.begin()
    C = Ctx(P, rowslabs=ROWSLABS)
    x = P.sbuf("x", [128, NC * TS], F32)
    xn = P.sbuf("xn", [128, NC * TS], BF16)
    emit_prepass(P, C, wsrc, wc)
    dds = {sub: _DECLS[sub](nc, wc, sub + "_") for sub in set(_SEQUENCE)}
    for b in range(2):
        for c in range(NC):
            P.op("sync", lambda e, c=c, b=b: e.dma_start(
                out=x[:, c * TS:(c + 1) * TS],
                in_=xin[b * 1024 + c * 128: b * 1024 + (c + 1) * 128, :]),
                 writes=[x], dma=True, slot="xio")
        for si, sub in enumerate(_SEQUENCE):
            P.push_scope()
            _EMITS[sub](P, C, x, xn, dds[sub], u=f"_{si}_{b}")
            P.barrier()
            P.pop_scope()
        for c in range(NC):
            P.op("sync", lambda e, c=c, b=b: e.dma_start(
                out=xout[b * 1024 + c * 128: b * 1024 + (c + 1) * 128, :],
                in_=x[:, c * TS:(c + 1) * TS]),
                 reads=[x], dma=True, slot="xout")
    P.barrier()
    P.finish()
    return nc


def _prep_weights(inp):
    w = {}
    w["a0"] = {"wq_d": hosttile(inp["L0_attn_a_Wq"].T, 2),
               "wk_d": hosttile(inp["L0_attn_a_Wk"].T, 2),
               "wv_d": hosttile(inp["L0_attn_a_Wv"].T, 2),
               "b_d": np.concatenate([hostbias(inp["L0_attn_a_bq"]),
                                      hostbias(inp["L0_attn_a_bk"]),
                                      hostbias(inp["L0_attn_a_bv"])], 1)}
    perm = np.concatenate([np.arange(0, 2048, 2), np.arange(1, 2048, 2)])
    oneh = np.zeros((32, 4096), np.float32)
    for e in range(32):
        oneh[e, e * 128:(e + 1) * 128] = 1.0
    moe = {"gw_d": hosttile(inp["L0_ffn_gate_w"].T, 1),
           "gb_d": inp["L0_ffn_gate_b"][None, :],
           "b2_d": inp["L0_ffn_b2"],
           "oneh_d": oneh.astype(BF16NP),
           "gl_in": inp["L0_ffn_norm_scale"][None, :]}
    for e in range(32):
        moe[f"w1_{e}"] = hosttile(np.ascontiguousarray(inp["L0_ffn_w1"][e][perm]).T, 4)
        moe[f"w2_{e}"] = hosttile(inp["L0_ffn_w2"][e].T, 2)
        moe[f"b1_{e}"] = hostbias(inp["L0_ffn_b1"][e][perm])
    w["moe"] = moe
    ipb = inp["L0_attn_b_in_proj_b"]
    ab = {"ip_d": hosttile(inp["L0_attn_b_in_proj_w"].T, 6),
          "op_d": hosttile(inp["L0_attn_b_out_proj_w"].T, 2),
          "ibv_d": ipb[None, 2048:3072],
          "opb_d": hostbias(inp["L0_attn_b_out_proj_b"]),
          "bqk_d": np.concatenate([hostbias(ipb[0:1024]), hostbias(ipb[1024:2048])], 1),
          "gb_in": np.stack([inp["L0_attn_b_ln_w"], inp["L0_attn_b_ln_b"]])}
    for h in range(4):
        ab[f"wl_{h}"] = np.concatenate([hosttile(inp["L0_attn_b_w0"][h].T, 1),
                                        hosttile(inp["L0_attn_b_w2"][h].T, 1),
                                        hosttile(inp["L0_attn_b_w1"][h].T, 1)], 1)
        ab[f"wln_{h}"] = hosttile(inp["L0_attn_b_w1"][h], 1)
    w["ab"] = ab
    mc = np.zeros((128, 3072), np.float32)
    for bi_m, delta in enumerate(range(-128, 513, 128)):
        row = np.arange(128)[:, None]
        col = np.arange(512)[None, :]
        mc[:, 512 * bi_m:512 * (bi_m + 1)] = (
            ((row % 4 == 0) | (np.abs(col - row - delta) <= 8)).astype(np.float32))
    m4col = (np.arange(128)[:, None] % 4 == 0).astype(np.float32)
    w["a1"] = {"qw_d": hosttile(inp["L1_attn_a_q_w"], 2),
               "kw_d": hosttile(inp["L1_attn_a_k_w"], 2),
               "vw_d": hosttile(inp["L1_attn_a_v_w"], 2),
               "ow_d": hosttile(inp["L1_attn_a_o_w"], 2),
               "ob_d": hostbias(inp["L1_attn_a_o_b"]),
               "us_d": np.asarray(inp["L1_attn_a_unity_scale"], np.float32).reshape(1, 1),
               "mc_d": mc.astype(BF16NP), "m4_d": m4col}
    scl = np.zeros((128, 136), np.float32)
    scl[:, 0:32] = hostbias(inp["L1_ffn_w1_s"])
    scl[:, 32:64] = hostbias(inp["L1_ffn_w2_s"])
    scl[:, 64:96] = hostbias(inp["L1_ffn_h1_s"])
    scl[:, 96:128] = hostbias(inp["L1_ffn_h2_s"])
    scl[:, 128:136] = hostbias(inp["L1_ffn_w3_s"])
    w["f1"] = {"w1_d": hosttile(inp["L1_ffn_w1_w"].T, 8),
               "w2_d": hosttile(inp["L1_ffn_w2_w"].T, 8),
               "h1_d": hosttile(inp["L1_ffn_h1_w"].T, 32),
               "h2_d": hosttile(inp["L1_ffn_h2_w"].T, 32),
               "w3_d": hosttile(inp["L1_ffn_w3_w"].T, 8),
               "scl_d": scl}
    bv = inp["L1_attn_b_bv"]
    w["b1"] = {"qw_d": hosttile(inp["L1_attn_b_Wq"], 2),
               "kw_d": hosttile(inp["L1_attn_b_Wk"], 2),
               "wv_d": inp["L1_attn_b_Wv"],
               "bv_d": np.ascontiguousarray(bv.reshape(8, 128, 8).transpose(2, 0, 1).reshape(8, 1024)),
               "bias_d": np.concatenate([hostbias(inp["L1_attn_b_bq"]),
                                         hostbias(inp["L1_attn_b_bk"])], 1),
               "gb_in": np.stack([inp["L1_attn_b_gamma"], inp["L1_attn_b_beta"]])}
    return w


_SEQUENCE = ["a0", "moe", "ab", "moe", "a1", "f1", "b1", "f1"]


def kernel(**inp):
    inp = {k: np.asarray(v) for k, v in inp.items()}
    consts = Ctx.host_consts()
    w = _prep_weights(inp)
    wpack = build_wpack(w)
    x0 = inp["L0_attn_a_hidden_states"].astype(np.float32)
    state = np.ascontiguousarray(x0.transpose(0, 2, 1))
    nc = build_prog_fused()
    wflat = {}
    for sub in set(_SEQUENCE):
        for k, v in w[sub].items():
            wflat[sub + "_" + k] = v
    in_maps = []
    for core in range(8):
        m = dict(wflat)
        m["wpack"] = wpack
        m.update(consts)
        m["xin"] = state[2 * core: 2 * core + 2].reshape(2048, TS)
        in_maps.append(m)
    res = run_bass_kernel_spmd(nc, in_maps, core_ids=list(range(8)))
    for core in range(8):
        state[2 * core: 2 * core + 2] = res.results[core]["xout"].reshape(2, 1024, TS)
    return np.ascontiguousarray(state.transpose(0, 2, 1))
